# version 13
# baseline (speedup 1.0000x reference)
import os
from contextlib import ExitStack

import numpy as np
import ml_dtypes

import concourse.bass as bass
import concourse.mybir as mybir
from concourse.bass_utils import run_bass_kernel_spmd

F32 = mybir.dt.float32
BF16 = mybir.dt.bfloat16
AF = mybir.ActivationFunctionType
ALU = mybir.AluOpType
AX = mybir.AxisListType

FP64_SIM = os.environ.get("BASS_SIM_FP64") == "1"

H = 16
HD = 64
D = 1024
FF = 4096
P = 128
DTC = 8
FT_ = 32
GC = 128
SC = 256
QCH = 256
LN_EPS = 1e-5
XSA_EPS = 1e-8
W_KEYS = [
    "L0_attn_Wq", "L0_attn_Wk", "L0_attn_Wv", "L0_attn_Wg", "L0_attn_bg",
    "L0_attn_Wo", "L0_ffn_W1", "L0_ffn_b1", "L0_ffn_W2", "L0_ffn_b2",
    "L1_attn_Wq", "L1_attn_Wk", "L1_attn_Wv", "L1_attn_Wo", "L1_attn_xsa_scale",
    "L1_ffn_W1", "L1_ffn_b1", "L1_ffn_W2", "L1_ffn_b2",
]


def _consts():
    bf = ml_dtypes.bfloat16
    ident_bf = np.eye(P, dtype=np.float32).astype(bf)
    ident_f32 = np.eye(P, dtype=np.float32)
    mask_u = np.triu(np.ones((GC, GC), dtype=np.float32)).astype(bf)
    mask_tiled = np.tile(mask_u, (1, H))
    m1 = (0.5 * np.ones((GC, GC)) - np.triu(np.ones((GC, GC)))).astype(np.float32)
    ones_row_bf = np.ones((1, P), dtype=np.float32).astype(bf)
    ones_col_f32 = np.ones((P, 1), dtype=np.float32)
    dup = np.zeros((64, P), dtype=np.float32)
    dup[np.arange(64), np.arange(64)] = 1.0
    dup[np.arange(64), 64 + np.arange(64)] = 1.0
    sel = np.zeros((P, P), dtype=np.float32)
    sel[127, :] = 1.0
    return {
        "c_ident_bf": ident_bf,
        "c_ident_f32": ident_f32,
        "c_mask": mask_tiled,
        "c_m1": m1,
        "c_ones_row_bf": ones_row_bf,
        "c_ones_col_f32": ones_col_f32,
        "c_dup": dup.astype(bf),
        "c_sel": sel.astype(bf),
    }


class Bld:

    def __init__(self, nc, stack):
        self.nc = nc
        self.stack = stack
        self.pe = nc.tensor
        self.ve = nc.vector
        self.se = nc.scalar
        self.sq = nc.sync
        self.gp = nc.gpsimd
        self.sem = {}
        self.cnt = {}
        self.kind = {}
        for name in ("pe", "ve", "se"):
            self.sem[name] = stack.enter_context(nc.semaphore("s_" + name))
            self.cnt[name] = 0

    def dsem(self, name, kind):
        if name not in self.sem:
            self.sem[name] = self.stack.enter_context(self.nc.semaphore("d_" + name))
            self.cnt[name] = 0
            self.kind[name] = kind
        assert self.kind[name] == kind
        return self.sem[name]

    def tick(self, instr, name, n=1):
        instr.then_inc(self.sem[name], n)
        self.cnt[name] += n
        return self.cnt[name]

    def wait(self, eng, name, val=None):
        eng.wait_ge(self.sem[name], self.cnt[name] if val is None else val)

    def wait_all(self, engs, names):
        for e in engs:
            for n in names:
                self.wait(e, n)

    def phase_barrier(self):
        for e in (self.pe, self.ve, self.se, self.sq, self.gp):
            for n in ("pe", "ve", "se"):
                self.wait(e, n)
        for nm, kd in list(self.kind.items()):
            self.wait(self.sq, nm)
            self.wait(self.gp, nm)

    def sb(self, name, shape, dt):
        return self.stack.enter_context(self.nc.sbuf_tensor(name, shape, dt))

    def load(self, name, dst_ap, src_ap):
        self.dsem(name, "h")
        return self.tick(self.sq.dma_start(dst_ap, src_ap), name, 16)

    def loadc(self, name, dst_ap, src_ap):
        self.dsem(name, "g")
        return self.tick(self.gp.dma_start(dst_ap, src_ap), name, 16)

    def store(self, name, dst_ap, src_ap):
        self.dsem(name, "h")
        return self.tick(self.sq.dma_start(dst_ap, src_ap), name, 16)

    def storeg(self, name, dst_ap, src_ap):
        self.dsem(name, "g")
        return self.tick(self.gp.dma_start(dst_ap, src_ap), name, 16)

    def transpose_block(self, dst_ps_ap, src_sb_ap, ident_ap):
        return self.pe.transpose(dst_ps_ap, src_sb_ap, ident_ap)

    def emit_round(self, dst_ap, src_ap, scr_sgn, scr_bf, scr_f32, n):
        b = self
        if not FP64_SIM:
            b.wait(b.se, "ve")
            b.tick(b.se.activation(scr_sgn[:, :n], src_ap, AF.Sign), "se")
            b.wait(b.ve, "se")
            b.ve.scalar_tensor_tensor(scr_bf[:, :n], scr_sgn[:, :n], 128.0,
                                      src_ap, ALU.mult, ALU.add)
            b.ve.drain()
            b.tick(b.ve.scalar_tensor_tensor(dst_ap, scr_sgn[:, :n], -128.0,
                                             scr_bf[:, :n], ALU.mult, ALU.add), "ve")
        else:
            v = scr_f32
            b.ve.tensor_scalar_add(v[:, :n], src_ap, 128.5)
            b.ve.drain()
            for c in (256.0, 128.0, 64.0, 32.0, 16.0, 8.0, 4.0, 2.0, 1.0):
                b.ve.tensor_single_scalar(scr_sgn[:, :n], v[:, :n], c, ALU.is_ge)
                b.ve.drain()
                b.ve.scalar_tensor_tensor(v[:, :n], scr_sgn[:, :n], -c, v[:, :n],
                                          ALU.mult, ALU.add)
                b.ve.drain()
            b.ve.tensor_scalar(scr_sgn[:, :n], src_ap, 128.5, -128.0, ALU.add, ALU.add)
            b.ve.drain()
            b.tick(b.ve.tensor_sub(dst_ap, scr_sgn[:, :n], v[:, :n]), "ve")


def kernel(**inputs):
    x = np.ascontiguousarray(np.asarray(inputs["L0_attn_x"], dtype=np.float32))
    B, S, Dm = x.shape
    n_cores = 8
    b_loc = B // n_cores
    consts = _consts()
    nc = build_program(b_loc, S)
    in_maps = []
    for c in range(n_cores):
        m = {"x_in": x[c * b_loc:(c + 1) * b_loc]}
        for k in W_KEYS:
            m[k] = np.ascontiguousarray(np.asarray(inputs[k], dtype=np.float32))
        m.update(consts)
        in_maps.append(m)
    res = run_bass_kernel_spmd(nc, in_maps, core_ids=list(range(n_cores)))
    return np.concatenate([r["x_out"] for r in res.results], axis=0)


def build_program(b_loc, S, skip=()):
    nc = bass.Bass("TRN2", target_bir_lowering=False)
    stack = ExitStack()
    b = Bld(nc, stack)
    emit_all(b, b_loc, S, set(skip))
    stack.close()
    return nc


def emit_all(b, b_loc, S, skip=()):
    nc = b.nc
    NTB = S // P
    NT = b_loc * NTB
    NSC = S // SC
    FT = FF // P
    DT = D // P

    x_in = nc.dram_tensor("x_in", [b_loc, S, D], F32, kind="ExternalInput")
    x_out = nc.dram_tensor("x_out", [b_loc, S, D], F32, kind="ExternalOutput")
    Wd = {}
    shp = {
        "L0_attn_Wq": [D, D], "L0_attn_Wk": [D, D], "L0_attn_Wv": [D, D],
        "L0_attn_Wg": [D, D], "L0_attn_bg": [D], "L0_attn_Wo": [D, D],
        "L0_ffn_W1": [FF, D], "L0_ffn_b1": [FF], "L0_ffn_W2": [D, FF],
        "L0_ffn_b2": [D],
        "L1_attn_Wq": [D, D], "L1_attn_Wk": [D, D], "L1_attn_Wv": [D, D],
        "L1_attn_Wo": [D, D], "L1_attn_xsa_scale": [1],
        "L1_ffn_W1": [FF, D], "L1_ffn_b1": [FF], "L1_ffn_W2": [D, FF],
        "L1_ffn_b2": [D],
    }
    for k, s_ in shp.items():
        Wd[k] = nc.dram_tensor(k, s_, F32, kind="ExternalInput")
    cd = {
        "c_ident_bf": nc.dram_tensor("c_ident_bf", [P, P], BF16, kind="ExternalInput"),
        "c_ident_f32": nc.dram_tensor("c_ident_f32", [P, P], F32, kind="ExternalInput"),
        "c_mask": nc.dram_tensor("c_mask", [P, H * GC], BF16, kind="ExternalInput"),
        "c_m1": nc.dram_tensor("c_m1", [GC, GC], F32, kind="ExternalInput"),
        "c_ones_row_bf": nc.dram_tensor("c_ones_row_bf", [1, P], BF16, kind="ExternalInput"),
        "c_ones_col_f32": nc.dram_tensor("c_ones_col_f32", [P, 1], F32, kind="ExternalInput"),
        "c_dup": nc.dram_tensor("c_dup", [64, P], BF16, kind="ExternalInput"),
        "c_sel": nc.dram_tensor("c_sel", [P, P], BF16, kind="ExternalInput"),
    }
    s_w1t0 = nc.dram_tensor("s_w1t0", [FT, P, D], BF16)
    s_w1t1 = nc.dram_tensor("s_w1t1", [FT, P, D], BF16)
    s_w2t0 = nc.dram_tensor("s_w2t0", [FT, P, D], BF16)
    s_w2t1 = nc.dram_tensor("s_w2t1", [FT, P, D], BF16)
    s_wT = {k: nc.dram_tensor("s_" + k + "T", [D, D], BF16)
            for k in ("L1_attn_Wq", "L1_attn_Wk", "L1_attn_Wv", "L1_attn_Wo")}
    s_wbf = {k: nc.dram_tensor("s_" + k + "B", [D, D], BF16)
             for k in ("L0_attn_Wq", "L0_attn_Wk", "L0_attn_Wv", "L0_attn_Wg",
                       "L0_attn_Wo")}
    s_atok = nc.dram_tensor("s_atok", [b_loc, S, D], BF16)

    xres = b.sb("xres", [P, NT * D], BF16)
    ident_bf = b.sb("ident_bf", [P, P], BF16)
    ident_f32 = b.sb("ident_f32", [P, P], F32)
    ones_row = b.sb("ones_row", [1, P], BF16)
    ones_col = b.sb("ones_col", [P, 1], F32)
    b1c1 = b.sb("b1c1", [P, FT], F32)
    b2r1 = b.sb("b2r1", [1, D], BF16)
    xsa_col = b.sb("xsa_col", [P, 1], F32)
    thcol = [b.sb(f"thcol{l}", [P, 1], F32) for l in (0, 1)]
    thcol2 = [b.sb(f"thcol2{l}", [P, 1], F32) for l in (0, 1)]
    thneg = b.sb("thneg", [P, 1], F32)
    wscol = [b.sb(f"wscol{l}", [P, 1], F32) for l in (0, 1)]
    rowtmp = b.sb("rowtmp", [1, 8], F32)
    shift_col = b.sb("shift_col", [P, 1], F32)

    b.tick(b.ve.memset(shift_col[:], -14.0), "ve")
    b.stack.enter_context(nc.allow_non_contiguous_dma(reason="one-time small bias loads"))
    b.load("c0", ident_bf[:], cd["c_ident_bf"].ap())
    b.load("c0", ident_f32[:], cd["c_ident_f32"].ap())
    b.load("c0", ones_row[:], cd["c_ones_row_bf"].ap())
    b.load("c0", ones_col[:], cd["c_ones_col_f32"].ap())
    b.load("c0", b1c1[:], Wd["L1_ffn_b1"].ap().rearrange("(i p) -> p i", p=P))
    b.loadc("cg", b2r1[:], Wd["L1_ffn_b2"].ap().rearrange("(o f) -> o f", o=1))


    ones_row_f = b.sb("ones_row_f", [1, P], F32)
    xsa_sb = b.sb("xsa_sb", [1, 1], F32)
    b.load("c0", ones_row_f[:], cd["c_ones_col_f32"].ap().rearrange("p o -> o p"))
    b.load("c0", xsa_sb[:], Wd["L1_attn_xsa_scale"].ap().rearrange("(o x) -> o x", o=1))
    b.wait_all([b.pe, b.ve, b.se, b.sq, b.gp], ["c0", "cg"])
    with nc.psum_tensor("pxsa", [P, 1], F32) as pxsa:
        b.tick(b.pe.matmul(pxsa[:], ones_row_f[:], xsa_sb[:], start=True, stop=True), "pe")
        b.wait(b.ve, "pe")
        b.tick(b.ve.tensor_scalar_mul(xsa_col[:], pxsa[:], -1.0), "ve")

    cast_sk = ExitStack()
    _prep_cast_all(b, [(Wd[k], s_wbf[k]) for k in s_wbf], cast_sk)

    def wide_a(scratch):
        return lambda i, stg: (scratch.ap()[i, :, :], stg[:])

    def wide_b(scratch):
        return lambda i, stg: (
            scratch.ap()[:, :, i * P:(i + 1) * P].rearrange("j p c -> p j c"),
            stg.ap().rearrange("p (j c) -> p j c", c=P))
    _prep_ternary(b, Wd["L0_ffn_W1"], wide_a(s_w1t0),
                  1e-8, thcol[0], None, "a0", ones_col, ones_row_f, ident_bf, thneg)
    _prep_ternary(b, Wd["L0_ffn_W2"], wide_b(s_w2t0),
                  1e-8, thcol2[0], None, "b0", ones_col, ones_row_f, ident_bf, thneg)
    for k in s_wT:
        _prep_transpose(b, Wd[k], s_wT[k], ident_bf, "t" + k[-2:])
    _prep_ternary(b, Wd["L1_ffn_W1"], wide_a(s_w1t1),
                  1e-5, thcol[1], wscol[0], "a1", ones_col, ones_row_f, ident_bf, thneg)
    _prep_ternary(b, Wd["L1_ffn_W2"], wide_b(s_w2t1),
                  1e-5, thcol2[1], wscol[1], "b1", ones_col, ones_row_f, ident_bf, thneg)
    for nm in list(b.kind):
        if nm.startswith("pst"):
            b.wait(b.sq, nm)
            b.wait(b.gp, nm)
    for e in (b.pe, b.ve, b.se, b.sq, b.gp):
        for nm in ("pe", "ve", "se"):
            b.wait(e, nm)
    cast_sk.close()

    st = _State(b, locals())
    st.uid = 0
    with ExitStack() as sk0:
        st.maskc = maskc = sk0.enter_context(nc.sbuf_tensor("maskc", [P, H * GC], BF16))
        st.m1c = m1c = sk0.enter_context(nc.sbuf_tensor("m1c", [GC, GC], F32))
        st.bg_bc = bg_bc = sk0.enter_context(nc.sbuf_tensor("bg_bc", [P, D], F32))
        st.b1c0 = b1c0 = sk0.enter_context(nc.sbuf_tensor("b1c0", [P, FT], F32))
        st.b2c0 = b2c0 = sk0.enter_context(nc.sbuf_tensor("b2c0", [P, DT], F32))
        st.dup_bf = dup_bf = sk0.enter_context(nc.sbuf_tensor("dup_bf", [64, P], BF16))
        st.sel_bf = sel_bf = sk0.enter_context(nc.sbuf_tensor("sel_bf", [P, P], BF16))
        bg_row = sk0.enter_context(nc.sbuf_tensor("bg_row", [1, D], F32))
        b.phase_barrier()
        with nc.allow_non_contiguous_dma(reason="one-time small bias loads"):
            b.load("c0", maskc[:], cd["c_mask"].ap())
            b.load("c0", m1c[:], cd["c_m1"].ap())
            b.load("c0", b1c0[:], Wd["L0_ffn_b1"].ap().rearrange("(i p) -> p i", p=P))
            b.load("c0", b2c0[:], Wd["L0_ffn_b2"].ap().rearrange("(i p) -> p i", p=P))
            b.load("c0", bg_row[:], Wd["L0_attn_bg"].ap().rearrange("(o f) -> o f", o=1))
            b.load("c0", dup_bf[:], cd["c_dup"].ap())
            b.load("c0", sel_bf[:], cd["c_sel"].ap())
        b.wait_all([b.pe, b.ve, b.se], ["c0"])
        with nc.psum_tensor("pbg2", [P, D], F32) as pbg2:
            b.wait(b.pe, "ve")
            for j in range(D // 512):
                mm = b.pe.matmul(pbg2[:, j * 512:(j + 1) * 512], ones_row_f[:],
                                 bg_row[:, j * 512:(j + 1) * 512], start=True, stop=True)
            b.tick(mm, "pe")
            b.wait(b.ve, "pe")
            b.tick(b.ve.tensor_copy(bg_bc[:], pbg2[:]), "ve")
        for bi in range(b_loc):
            _layer0(st, bi)
        b.phase_barrier()
    if "l1" not in st.skip:
        for bi in range(b_loc):
            _layer1(st, bi)

    for nm, kd in list(b.kind.items()):
        b.wait(b.sq if kd == "h" else b.gp, nm)


class _State:
    def __init__(self, b, env):
        self.b = b
        for k, v in env.items():
            setattr(self, k, v)


def _prep_ternary(b, Wdram, wide_fn, eps, th_col, ws_col, tag,
                  ones_col, ones_row_f, ident_bf, thneg):
    nc = b.nc
    R, C = Wdram.ap().shape
    ntr, ncc = R // P, C // P
    nbuf = 4 if C <= 2048 else 2
    lsk = ExitStack()

    def LS(name, shape, dt):
        return lsk.enter_context(nc.sbuf_tensor(name, shape, dt))
    wnat = [LS(f"wn{tag}{i}", [P, C], F32) for i in range(nbuf)]
    scr = LS(f"wsc{tag}", [P, C], BF16)
    sums = LS(f"wsm{tag}", [P, ntr], F32)
    stg = [LS(f"wst{tag}{i}", [P, C], BF16) for i in (0, 1)]
    row1 = LS(f"wr1{tag}", [1, 1], F32)
    row2 = LS(f"wr2{tag}", [1, 1], F32)
    ve_done = [0] * nbuf
    for i in range(ntr):
        k = i % nbuf
        if i >= nbuf:
            b.sq.wait_ge(b.sem["ve"], ve_done[k])
        v = b.load(f"wn{tag}{k}", wnat[k][:], Wdram.ap()[i * P:(i + 1) * P, :])
        b.wait(b.ve, f"wn{tag}{k}", v)
        ve_done[k] = b.tick(
            b.ve.reduce_sum(sums[:, i:i + 1], wnat[k][:], axis=AX.X,
                            apply_absolute_value=True), "ve")
    with nc.psum_tensor("pws" + tag, [1, ntr], F32) as pws:
        b.wait(b.pe, "ve")
        b.tick(b.pe.matmul(pws[:], ones_col[:], sums[:], start=True, stop=True), "pe")
        b.wait(b.ve, "pe")
        b.ve.reduce_sum(row1[:], pws[:], axis=AX.X)
    b.ve.drain()
    b.ve.tensor_scalar(row2[:], row1[:], 1.0 / (R * C), eps, ALU.mult, ALU.max)
    b.tick(b.ve.drain(), "ve")
    with nc.psum_tensor("pth" + tag, [P, 1], F32) as pth:
        b.wait(b.pe, "ve")
        b.tick(b.pe.matmul(pth[:], ones_row_f[:], row2[:], start=True, stop=True), "pe")
        b.wait(b.ve, "pe")
        if ws_col is not None:
            b.ve.tensor_copy(ws_col[:], pth[:])
        b.ve.tensor_scalar_mul(th_col[:], pth[:], 0.5)
        b.ve.drain()
        b.tick(b.ve.tensor_scalar_mul(thneg[:], th_col[:], -1.0), "ve")
        b.ve.drain()
    with (nc.psum_tensor("ptr" + tag + "0", [P, 512], BF16) as pt0,
          nc.psum_tensor("ptr" + tag + "1", [P, 512], BF16) as pt1):
        pts = [pt0, pt1]
        b.wait(b.pe, "ve")
        b.wait(b.pe, "se")
        dh_done = {}
        ve_evac = {}
        ve_th = [0] * nbuf
        pe_tr_last = 0
        for i in range(ntr):
            k = i % nbuf
            b.sq.wait_ge(b.sem["ve"], ve_th[k] if i >= nbuf else ve_done[k])
            v = b.load(f"wn{tag}{k}", wnat[k][:], Wdram.ap()[i * P:(i + 1) * P, :])
            b.wait(b.ve, f"wn{tag}{k}", v)
            if i >= 1:
                b.wait(b.ve, "pe", pe_tr_last)
            b.ve.tensor_single_scalar(scr[:], wnat[k][:], thneg[:], ALU.is_lt)
            b.ve.drain()
            ve_th[k] = b.tick(
                b.ve.scalar_tensor_tensor(scr[:], wnat[k][:], th_col[:], scr[:],
                                          ALU.is_gt, ALU.subtract), "ve")
            b.wait(b.pe, "ve", ve_th[k])
            ks = i % 2
            last_evac = 0
            nb4 = max(1, ncc // 4)
            for jb in range(nb4):
                kp = jb % 2
                if kp in ve_evac:
                    b.pe.wait_ge(b.sem["ve"], ve_evac[kp])
                for j4 in range(min(4, ncc)):
                    j = jb * 4 + j4
                    mm = b.pe.transpose(pts[kp][:, j4 * P:(j4 + 1) * P],
                                        scr[:, j * P:(j + 1) * P], ident_bf[:])
                pv = b.tick(mm, "pe")
                if jb == nb4 - 1:
                    pe_tr_last = pv
                b.wait(b.ve, "pe", pv)
                if jb == 0 and ks in dh_done:
                    b.ve.wait_ge(b.sem[f"pst{tag}{ks}"], dh_done[ks])
                w4 = min(4, ncc) * P
                ve_evac[kp] = last_evac = b.tick(
                    b.ve.tensor_copy(stg[ks][:, jb * 512:jb * 512 + w4],
                                     pts[kp][:, 0:w4]), "ve")
            b.sq.wait_ge(b.sem["ve"], last_evac)
            if ks in dh_done:
                b.sq.wait_ge(b.sem[f"pst{tag}{ks}"], dh_done[ks])
            dst_ap, src_ap = wide_fn(i, stg[ks])
            dh_done[ks] = b.store(f"pst{tag}{ks}", dst_ap, src_ap)
    for e in (b.sq, b.gp):
        b.wait(e, "ve")
        b.wait(e, "pe")
        b.wait(e, f"pst{tag}0")
        b.wait(e, f"pst{tag}1")
    lsk.close()


def _prep_cast_all(b, pairs, sk):
    nc = b.nc
    bufs = [sk.enter_context(nc.sbuf_tensor(f"cbw{i}", [P, D], BF16))
            for i in range(4)]
    n = 0
    for w, dsc in pairs:
        for i in range(D // P):
            k = n % 4
            if n >= 4:
                b.wait(b.gp, f"pstc{k}")
            v = b.loadc(f"cwl{k}", bufs[k][:], w.ap()[i * P:(i + 1) * P, :])
            b.wait(b.gp, f"cwl{k}", v)
            b.storeg(f"pstc{k}", dsc.ap()[i * P:(i + 1) * P, :], bufs[k][:])
            n += 1


def _prep_transpose(b, Wdram, scratch, ident_bf, tag):
    nc = b.nc
    ntr = D // P
    lsk = ExitStack()
    wnat = [lsk.enter_context(nc.sbuf_tensor(f"tn{tag}{i}", [P, D], BF16))
            for i in (0, 1)]
    stg = [lsk.enter_context(nc.sbuf_tensor(f"tg{tag}{i}", [P, D], BF16))
           for i in (0, 1)]
    pe_done = [0, 0]
    dh_done = {}
    ve_evac = {}
    with (nc.psum_tensor("ptt" + tag + "0", [P, 512], BF16) as pt0,
          nc.psum_tensor("ptt" + tag + "1", [P, 512], BF16) as pt1):
        pts = [pt0, pt1]
        b.wait(b.pe, "ve")
        b.wait(b.pe, "se")
        for i in range(ntr):
            buf = wnat[i % 2]
            if i >= 2:
                b.gp.wait_ge(b.sem["pe"], pe_done[i % 2])
            v = b.loadc(f"tn{tag}{i % 2}", buf[:], Wdram.ap()[i * P:(i + 1) * P, :])
            b.wait(b.pe, f"tn{tag}{i % 2}", v)
            ks = i % 2
            last_evac = 0
            for jb in range(ntr // 4):
                k = jb % 2
                if k in ve_evac:
                    b.pe.wait_ge(b.sem["ve"], ve_evac[k])
                for j4 in range(4):
                    j = jb * 4 + j4
                    mm = b.pe.transpose(pts[k][:, j4 * P:(j4 + 1) * P],
                                        buf[:, j * P:(j + 1) * P], ident_bf[:])
                pv = b.tick(mm, "pe")
                if jb == ntr // 4 - 1:
                    pe_done[i % 2] = pv
                b.wait(b.ve, "pe", pv)
                if jb == 0 and ks in dh_done:
                    b.ve.wait_ge(b.sem[f"pst{tag}{ks}"], dh_done[ks])
                ve_evac[k] = last_evac = b.tick(
                    b.ve.tensor_copy(stg[ks][:, jb * 512:(jb + 1) * 512],
                                     pts[k][:]), "ve")
            b.sq.wait_ge(b.sem["ve"], last_evac)
            if ks in dh_done:
                b.sq.wait_ge(b.sem[f"pst{tag}{ks}"], dh_done[ks])
            dh_done[ks] = b.store(
                f"pst{tag}{ks}",
                scratch.ap()[:, i * P:(i + 1) * P].rearrange(
                    "(j p) c -> p j c", p=P),
                stg[ks].ap().rearrange("p (j c) -> p j c", c=P))
    for e in (b.sq, b.gp):
        b.wait(e, "ve")
        b.wait(e, "pe")
        b.wait(e, f"pst{tag}0")
        b.wait(e, f"pst{tag}1")
    lsk.close()

def _ln_tiles(st, bi, layer):
    b = st.b
    nc = b.nc
    S = st.S
    NTB = st.NTB
    xn_fm = st.cur["xn_fm"]
    lsk = ExitStack()
    xt = [lsk.enter_context(nc.sbuf_tensor(f"xt{layer}{bi}{i}", [P, D], F32))
          for i in (0, 1)]
    sq = lsk.enter_context(nc.sbuf_tensor(f"sq{layer}{bi}", [P, D], F32))
    stat = [lsk.enter_context(nc.sbuf_tensor(f"st{layer}{bi}{i}", [P, 4], F32))
            for i in (0, 1)]
    xn_tok = [lsk.enter_context(nc.sbuf_tensor(f"xk{layer}{bi}{i}", [P, D], BF16))
              for i in (0, 1)]
    with (nc.psum_tensor(f"ptl{layer}{bi}0", [P, 512], BF16) as pt0,
          nc.psum_tensor(f"ptl{layer}{bi}1", [P, 512], BF16) as pt1):
        pts = [pt0, pt1]
        b.wait(b.pe, "ve")
        b.wait(b.pe, "se")
        evac = {}
        done = {}
        sqv = {}

        def emit_stats(t):
            k = t % 2
            if layer == 0:
                if t >= 2:
                    b.sq.wait_ge(b.sem["ve"], done[t - 2][0])
                    b.sq.wait_ge(b.sem["se"], done[t - 2][1])
                else:
                    b.wait(b.sq, "pe")
                    b.wait(b.sq, "ve")
                    b.wait(b.sq, "se")
                v = b.load(f"xld{k}", xt[k][:], st.x_in.ap()[bi, t * P:(t + 1) * P, :])
                b.wait(b.ve, f"xld{k}", v)
            else:
                if t >= 2:
                    b.wait(b.ve, "se", done[t - 2][1])
                b.ve.drain()
                b.ve.tensor_copy(xt[k][:], st.xres[:, (bi * NTB + t) * D:
                                                   (bi * NTB + t + 1) * D])
                b.ve.drain()
            stt = stat[k]
            b.ve.reduce_sum(stt[:, 0:1], xt[k][:], axis=AX.X)
            b.ve.drain()
            b.ve.tensor_scalar_mul(stt[:, 1:2], stt[:, 0:1], -1.0 / D)
            b.ve.drain()
            vs = b.tick(b.ve.scalar_tensor_tensor(sq[:], xt[k][:], stt[:, 1:2],
                                                  xt[k][:], ALU.add, ALU.mult,
                                                  accum_out=stt[:, 0:1]), "ve")
            b.ve.drain()
            b.ve.tensor_scalar(stt[:, 2:3], stt[:, 0:1], 1.0 / D, LN_EPS,
                               ALU.mult, ALU.add)
            b.ve.drain()
            vv = b.tick(b.ve.reciprocal(stt[:, 3:4], stt[:, 2:3]), "ve")
            b.wait(b.se, "ve", vv)
            b.se.drain()
            sv = b.tick(b.se.activation(stt[:, 2:3], stt[:, 3:4], AF.Sqrt), "se")
            b.wait(b.ve, "se", sv)
            b.ve.drain()
            bv = b.tick(b.ve.tensor_mul(stt[:, 0:1], stt[:, 1:2], stt[:, 2:3]), "ve")
            sqv[t] = (vs, bv)

        def emit_apply(t):
            k = t % 2
            stt = stat[k]
            b.wait(b.se, "ve", sqv[t][1])
            b.se.drain()
            sa = b.tick(b.se.activation(xn_tok[k][:], xt[k][:], AF.Identity,
                                        bias=stt[:, 0:1], scale=stt[:, 2:3]), "se")
            if layer == 0:
                b.se.drain()
                sa = b.tick(b.se.activation(
                    st.xres[:, (bi * NTB + t) * D:(bi * NTB + t + 1) * D],
                    xt[k][:], AF.Identity), "se")
            done[t] = (sqv[t][0], sa)
            b.wait(b.pe, "se", sa)
            xn3 = xn_fm.ap().rearrange("p (j s) -> p j s", j=DTC)
            for jb in range(2):
                kp = jb % 2
                if kp in evac:
                    en, val = evac[kp]
                    b.pe.wait_ge(b.sem[en], val)
                for j4 in range(4):
                    j = jb * 4 + j4
                    mm = b.pe.transpose(pts[kp][:, j4 * P:(j4 + 1) * P],
                                        xn_tok[k][:, j * P:(j + 1) * P],
                                        st.ident_bf[:])
                pv = b.tick(mm, "pe")
                dst = xn3[:, jb * 4:(jb + 1) * 4, t * P:(t + 1) * P]
                src = pts[kp].ap().rearrange("p (j c) -> p j c", j=4)
                if jb == 0:
                    b.wait(b.ve, "pe", pv)
                    evac[kp] = ("ve", b.tick(b.ve.tensor_copy(dst, src), "ve"))
                else:
                    b.wait(b.se, "pe", pv)
                    b.se.drain()
                    evac[kp] = ("se", b.tick(
                        b.se.activation(dst, src, AF.Identity), "se"))

        for t in range(NTB):
            emit_stats(t)
            if t >= 1:
                emit_apply(t - 1)
        emit_apply(NTB - 1)
    b.wait(b.pe, "ve")
    b.wait(b.pe, "se")
    lsk.close()


def _layer0(st, bi):
    b = st.b
    nc = b.nc
    S = st.S
    NTB = st.NTB
    NSC = st.NSC
    skip = st.skip
    with ExitStack() as sk:
        st.sk = sk
        st.cur = {}
        st.cur["xn_fm"] = sk.enter_context(
            nc.sbuf_tensor(f"xnfm0{bi}", [P, DTC * S], BF16))
        _ln_tiles(st, bi, 0)
        if "gla" not in skip:
            _gla(st, bi)
        if "ffn0" not in skip:
            _ffn0(st, bi)
    if "l1" in st.skip and bi == st.b_loc - 1:
        _emit_output_from_xres(st)


def _gla(st, bi):
    st.b.phase_barrier()
    b = st.b
    nc = b.nc
    S = st.S
    NSC = st.NSC
    sk = st.sk
    xn_fm = st.cur["xn_fm"]
    lsk = ExitStack()

    def A(name, shape, dt):
        return lsk.enter_context(nc.sbuf_tensor(f"g{name}{bi}", shape, dt))

    NCC = SC // GC
    wall = A("wall", [P, DTC * D], BF16)
    qfm = A("qfm", [P, DTC * SC], BF16)
    kfm = A("kfm", [P, DTC * SC], BF16)
    ktok = A("ktok", [P, NCC * D], BF16)
    vtok = A("vtok", [P, NCC * D], BF16)
    logg = A("logg", [P, NCC * D], F32)
    gs1 = A("gs1", [P, D], F32)
    gs2 = A("gs2", [P, D], F32)
    Fg4 = A("Fg4", [P, NCC * D], BF16)
    G24 = A("G24", [P, NCC * D], BF16)
    pdb4 = A("pdb4", [P, NCC * D], BF16)
    vt4 = A("vt4", [P, NCC * D], BF16)
    state = A("state", [64, D], F32)
    tmps = A("tmps", [64, D], BF16)
    tmp2 = A("tmp2", [P, D], BF16)
    At = A("At", [P, H * GC], BF16)
    otok = A("otok", [P, D], BF16)
    ofm = A("ofm", [P, DTC * SC], BF16)
    attn_s = A("attns", [P, DTC * SC], BF16)
    st.cur.update(dict(wall=wall, qfm=qfm, kfm=kfm, ktok=ktok, vtok=vtok,
                       logg=logg, gs1=gs1, gs2=gs2, Fg4=Fg4, G24=G24,
                       pdb4=pdb4, vt4=vt4, state=state, tmps=tmps, tmp2=tmp2,
                       At=At, otok=otok, ofm=ofm, attn_s=attn_s, wall_rd=0))

    b.ve.drain()
    b.tick(b.ve.memset(state[:], 0.0), "ve")

    for sc in range(NSC):
        t0 = sc * SC
        _proj_fm(st, wall, st.s_wbf["L0_attn_Wq"], xn_fm, qfm, t0, bi)
        _proj_fm(st, wall, st.s_wbf["L0_attn_Wk"], xn_fm, kfm, t0, bi, also_tok=ktok)
        _proj_tok(st, wall, st.s_wbf["L0_attn_Wv"], xn_fm, vtok, t0, bi)
        _proj_tok(st, wall, st.s_wbf["L0_attn_Wg"], xn_fm, None, t0, bi,
                  logg=(logg, gs1, gs2))
        _gla_gates(st, sc)
        for cc in range(NCC):
            _gla_chunk(st, bi, sc, cc)
        _gla_wo(st, bi, sc)
    b.phase_barrier()
    lsk.close()


def _gla_gates(st, sc):
    b = st.b
    nc = b.nc
    c = st.cur
    logg, vtok = c["logg"], c["vtok"]
    Fg4, G24, pdb4, vt4 = c["Fg4"], c["G24"], c["pdb4"], c["vt4"]
    NCC = SC // GC
    with (nc.psum_tensor(_psname(st, "psm"), [P, D], F32) as pm0,
          nc.psum_tensor(_psname(st, "psm"), [P, D], F32) as pm1,
          nc.psum_tensor(_psname(st, "pdb"), [P, D], F32) as pdbp):
        pmp = [pm0, pm1]
        b.wait(b.se, "ve")
        b.wait(b.se, "pe")
        b.wait(b.pe, "se")
        b.wait(b.pe, "ve")
        exps_done = {}
        pdb_ev = None
        for cc in range(NCC):
            k = cc % 2
            if k in exps_done:
                b.pe.wait_ge(b.sem["se"], exps_done[k])
            for half in range(2):
                sl = slice(half * 512, (half + 1) * 512)
                mm = b.pe.matmul(
                    pmp[k][:, sl], st.m1c[:],
                    logg[:, cc * D + half * 512:cc * D + (half + 1) * 512],
                    start=True, stop=True)
            pv = b.tick(mm, "pe")
            b.wait(b.se, "pe", pv)
            b.se.drain()
            b.se.activation(Fg4[:, cc * D:(cc + 1) * D], pmp[k][:], AF.Exp)
            sg = b.tick(b.se.activation(G24[:, cc * D:(cc + 1) * D], pmp[k][:],
                                        AF.Exp, scale=-1.0), "se")
            exps_done[k] = sg
            b.wait(b.pe, "se", sg)
            if pdb_ev is not None:
                b.pe.wait_ge(b.sem["ve"], pdb_ev)
            for half in range(2):
                sl = slice(half * 512, (half + 1) * 512)
                mm = b.pe.matmul(
                    pdbp[:, sl], st.sel_bf[:],
                    G24[:, cc * D + half * 512:cc * D + (half + 1) * 512],
                    start=True, stop=True)
            pv = b.tick(mm, "pe")
            b.wait(b.ve, "pe", pv)
            b.ve.drain()
            b.ve.tensor_copy(pdb4[:, cc * D:(cc + 1) * D], pdbp[:])
            b.ve.drain()
            pdb_ev = b.tick(b.ve.tensor_mul(vt4[:, cc * D:(cc + 1) * D],
                                            vtok[:, cc * D:(cc + 1) * D],
                                            Fg4[:, cc * D:(cc + 1) * D]), "ve")


def _psname(st, base):
    st.uid += 1
    return f"{base}{st.uid}"


def _proj_fm(st, wall, Wdram, xn_fm, out_fm, t0, bi, also_tok=None):
    b = st.b
    nc = b.nc
    S = st.S
    b.wait(b.gp, "pe", st.cur["wall_rd"])
    vlast = b.loadc("wall", wall.ap().rearrange("p (jd e) -> p jd e", jd=DTC),
                    Wdram.ap().rearrange("(jd p) e -> p jd e", p=P))
    b.wait(b.pe, "wall", vlast)
    with (nc.psum_tensor(_psname(st, "pq"), [P, 2 * SC], F32) as p0,
          nc.psum_tensor(_psname(st, "pq"), [P, 2 * SC], F32) as p1):
        pp = [p0, p1]
        b.wait(b.pe, "ve")
        b.wait(b.pe, "se")
        ve_evac = {}
        for jq in range(0, DTC, 2):
            k = (jq // 2) % 2
            if k in ve_evac:
                b.pe.wait_ge(b.sem["ve"], ve_evac[k])
            for j2 in range(2):
                for jd in range(DTC):
                    mm = b.pe.matmul(
                        pp[k][:, j2 * SC:(j2 + 1) * SC],
                        wall[:, jd * D + (jq + j2) * P:jd * D + (jq + j2 + 1) * P],
                        xn_fm[:, jd * S + t0:jd * S + t0 + SC],
                        start=(jd == 0), stop=(jd == DTC - 1))
            pv = b.tick(mm, "pe")
            b.wait(b.ve, "pe", pv)
            ve_evac[k] = b.tick(
                b.ve.tensor_copy(out_fm[:, jq * SC:(jq + 2) * SC], pp[k][:]), "ve")
    st.cur["wall_rd"] = b.cnt["pe"]
    if also_tok is not None:
        _tok_mms(st, wall, xn_fm, also_tok, t0, None)
        st.cur["wall_rd"] = b.cnt["pe"]


def _proj_tok(st, wall, Wdram, xn_fm, out_tok, t0, bi, logg=None):
    b = st.b
    b.wait(b.gp, "pe", st.cur["wall_rd"])
    vlast = b.loadc("wall", wall.ap().rearrange("p (jd e) -> p jd e", jd=DTC),
                    Wdram.ap().rearrange("(jd p) e -> p jd e", p=P))
    b.wait(b.pe, "wall", vlast)
    _tok_mms(st, wall, xn_fm, out_tok, t0, logg)
    st.cur["wall_rd"] = b.cnt["pe"]


def _tok_mms(st, wall, xn_fm, out_tok, t0, logg):
    b = st.b
    nc = b.nc
    S = st.S
    NCC = SC // P
    with (nc.psum_tensor(_psname(st, "pk"), [P, D], F32) as p0,
          nc.psum_tensor(_psname(st, "pk"), [P, D], F32) as p1):
        pp = [p0, p1]
        b.wait(b.pe, "ve")
        b.wait(b.pe, "se")
        cons = {}
        for tt in range(NCC):
            k = tt % 2
            if k in cons:
                b.pe.wait_ge(b.sem[cons[k][0]], cons[k][1])
            for jd in range(DTC):
                lhs = xn_fm[:, jd * S + t0 + tt * P:jd * S + t0 + (tt + 1) * P]
                b.pe.matmul(pp[k][:, 0:512], lhs, wall[:, jd * D:jd * D + 512],
                            start=(jd == 0), stop=(jd == DTC - 1))
                mm = b.pe.matmul(pp[k][:, 512:D], lhs, wall[:, jd * D + 512:(jd + 1) * D],
                                 start=(jd == 0), stop=(jd == DTC - 1))
            pv = b.tick(mm, "pe")
            if logg is None:
                b.wait(b.ve, "pe", pv)
                cons[k] = ("ve", b.tick(
                    b.ve.tensor_copy(out_tok[:, tt * D:(tt + 1) * D], pp[k][:]), "ve"))
            else:
                lg, gs1, gs2 = logg
                b.wait(b.ve, "pe", pv)
                b.wait(b.ve, "se")
                b.ve.drain()
                vv = b.tick(b.ve.tensor_add(gs1[:], pp[k][:], st.bg_bc[:]), "ve")
                cons[k] = ("ve", vv)
                b.wait(b.se, "ve", vv)
                b.se.activation(gs2[:], gs1[:], AF.Sigmoid)
                b.se.drain()
                b.tick(b.se.activation(lg[:, tt * D:(tt + 1) * D], gs2[:], AF.Ln), "se")


def _gla_chunk(st, bi, sc, cc):
    b = st.b
    nc = b.nc
    c = st.cur
    qfm, kfm, ktok = c["qfm"], c["kfm"], c["ktok"]
    G24, pdb4, vt4 = c["G24"], c["pdb4"], c["vt4"]
    state, tmps, tmp2 = c["state"], c["tmps"], c["tmp2"]
    gs1, gs2 = c["gs1"], c["gs2"]
    At, otok, ofm = c["At"], c["otok"], c["ofm"]
    tq0 = cc * P
    vt = vt4[:, cc * D:(cc + 1) * D]
    G2 = G24[:, cc * D:(cc + 1) * D]
    pdbl = pdb4[0:64, cc * D:(cc + 1) * D]

    b.ve.drain()
    tv = b.tick(b.ve.tensor_mul(tmps[:], state[:], pdbl), "ve")
    with nc.psum_tensor(_psname(st, "pdu"), [P, D], F32) as pdup:
        b.wait(b.pe, "ve", tv)
        for half in range(2):
            sl = slice(half * 512, (half + 1) * 512)
            mm = b.pe.matmul(pdup[:, sl], st.dup_bf[:], tmps[:, sl],
                             start=True, stop=True)
        pv = b.tick(mm, "pe")
        b.wait(b.ve, "pe", pv)
        b.ve.drain()
        t2 = b.tick(b.ve.tensor_copy(tmp2[:], pdup[:]), "ve")
    with nc.psum_tensor(_psname(st, "pst"), [P, H * GC], F32) as pst:
        b.pe.wait_ge(b.sem["ve"], t2)
        for h in range(H):
            cp = (h % 2) * (H // 2) * GC + (h // 2) * GC
            kf = kfm[(h % 2) * 64:(h % 2) * 64 + 64,
                     (h // 2) * SC + tq0:(h // 2) * SC + tq0 + P]
            qf = qfm[(h % 2) * 64:(h % 2) * 64 + 64,
                     (h // 2) * SC + tq0:(h // 2) * SC + tq0 + P]
            mm = b.pe.matmul(pst[:, cp:cp + GC], kf, qf,
                             start=(h in (0, 1, 8, 9)), stop=(h in (6, 7, 14, 15)))
        pv = b.tick(mm, "pe")
        b.wait(b.ve, "pe", pv)
        b.ve.drain()
        av = b.tick(b.ve.tensor_mul(At[:], pst[:], st.maskc[:]), "ve")
    with (nc.psum_tensor(_psname(st, "poa"), [P, D], F32) as poa,
          nc.psum_tensor(_psname(st, "pob"), [P, D], F32) as pob):
        b.wait(b.pe, "ve", av)
        for h in range(H):
            mm = b.pe.matmul(poa[:, h * HD:(h + 1) * HD],
                             At[:, ((h % 2) * (H // 2) + h // 2) * GC:
                                ((h % 2) * (H // 2) + h // 2 + 1) * GC],
                             vt[:, h * HD:(h + 1) * HD],
                             start=(h % 8 == 0), stop=(h % 8 == 7))
        for h in range(H):
            ip = (h % 2) * 512 + (h // 2) * HD
            qf = qfm[(h % 2) * 64:(h % 2) * 64 + 64,
                     (h // 2) * SC + tq0:(h // 2) * SC + tq0 + P]
            mm2 = b.pe.matmul(pob[:, ip:ip + HD], qf,
                              tmp2[(h % 2) * 64:(h % 2) * 64 + 64,
                                   h * HD:(h + 1) * HD],
                              start=(h in (0, 1)), stop=(h in (14, 15)))
        pv = b.tick(mm2, "pe")
        b.wait(b.ve, "pe", pv)
        b.ve.drain()
        pob3 = pob.ap().rearrange("p (par h2 e) -> p par h2 e", par=2, e=HD)
        gsc3 = gs2.ap().rearrange("p (h2 par e) -> p par h2 e", par=2, e=HD)
        b.ve.tensor_copy(gsc3, pob3)
        b.ve.drain()
        b.ve.tensor_add(gs1[:], poa[:], gs2[:])
        b.ve.drain()
        b.tick(b.ve.tensor_mul(otok[:], gs1[:], G2[:]), "ve")
        with nc.psum_tensor(_psname(st, "pu"), [64, D], F32) as pu:
            for half in range(2):
                sl = slice(half * 512, (half + 1) * 512)
                b.pe.matmul(pu[:, sl], st.ident_bf[0:64, 0:64], tmps[:, sl],
                            start=True, stop=False)
            for h in range(H):
                mm = b.pe.matmul(pu[:, h * HD:(h + 1) * HD],
                                 ktok[:, cc * D + h * HD:cc * D + (h + 1) * HD],
                                 vt[:, h * HD:(h + 1) * HD],
                                 start=False, stop=(h % 8 == 7))
            pv = b.tick(mm, "pe")
            b.wait(b.ve, "pe", pv)
            b.ve.drain()
            b.tick(b.ve.tensor_mul(state[:], pu[:], pdbl), "ve")
    with (nc.psum_tensor(_psname(st, "ptg"), [P, 512], BF16) as pt0,
          nc.psum_tensor(_psname(st, "ptg"), [P, 512], BF16) as pt1):
        pts = [pt0, pt1]
        b.wait(b.pe, "ve")
        ve_evac = {}
        ofm3 = ofm.ap().rearrange("p (j s) -> p j s", j=DTC)
        for jb in range(2):
            k = jb % 2
            if k in ve_evac:
                b.pe.wait_ge(b.sem["ve"], ve_evac[k])
            for j4 in range(4):
                j = jb * 4 + j4
                mm = b.pe.transpose(pts[k][:, j4 * P:(j4 + 1) * P],
                                    otok[:, j * P:(j + 1) * P], st.ident_bf[:])
            pv = b.tick(mm, "pe")
            b.wait(b.ve, "pe", pv)
            ve_evac[k] = b.tick(b.ve.tensor_copy(
                ofm3[:, jb * 4:(jb + 1) * 4, tq0:tq0 + P],
                pts[k].ap().rearrange("p (j c) -> p j c", j=4)), "ve")


def _gla_wo(st, bi, sc):
    b = st.b
    nc = b.nc
    c = st.cur
    S = st.S
    NTB = st.NTB
    wall, ofm, attn_s = c["wall"], c["ofm"], c["attn_s"]
    b.wait(b.gp, "pe", c["wall_rd"])
    vlast = b.loadc("wall", wall.ap().rearrange("p (jd e) -> p jd e", jd=DTC),
                    st.s_wbf["L0_attn_Wo"].ap().rearrange("(jd p) e -> p jd e", p=P))
    b.wait(b.pe, "wall", vlast)
    b.wait(b.pe, "ve")
    b.wait(b.pe, "se")
    with (nc.psum_tensor(_psname(st, "pwo"), [P, 2 * SC], F32) as p0,
          nc.psum_tensor(_psname(st, "pwo"), [P, 2 * SC], F32) as p1):
        pp = [p0, p1]
        ve_evac = {}
        for jo in range(0, DTC, 2):
            k = (jo // 2) % 2
            if k in ve_evac:
                b.pe.wait_ge(b.sem["ve"], ve_evac[k])
            for j2 in range(2):
                for je in range(DTC):
                    mm = b.pe.matmul(
                        pp[k][:, j2 * SC:(j2 + 1) * SC],
                        wall[:, je * D + (jo + j2) * P:je * D + (jo + j2 + 1) * P],
                        ofm[:, je * SC:(je + 1) * SC],
                        start=(je == 0), stop=(je == DTC - 1))
            pv = b.tick(mm, "pe")
            b.wait(b.ve, "pe", pv)
            ve_evac[k] = b.tick(
                b.ve.tensor_copy(attn_s[:, jo * SC:(jo + 2) * SC], pp[k][:]), "ve")
        c["wall_rd"] = b.cnt["pe"]
    with nc.psum_tensor(_psname(st, "prt"), [P, D], BF16) as prt:
        for tt in range(SC // P):
            b.wait(b.pe, "ve")
            for jo in range(DTC):
                mm = b.pe.transpose(prt[:, jo * P:(jo + 1) * P],
                                    attn_s[:, jo * SC + tt * P:jo * SC + (tt + 1) * P],
                                    st.ident_bf[:])
            pv = b.tick(mm, "pe")
            b.wait(b.ve, "pe", pv)
            gt = bi * NTB + sc * (SC // P) + tt
            b.ve.drain()
            b.tick(b.ve.tensor_add(st.xres[:, gt * D:(gt + 1) * D],
                                   st.xres[:, gt * D:(gt + 1) * D], prt[:]), "ve")


def _ffn0(st, bi):
    st.b.phase_barrier()
    b = st.b
    nc = b.nc
    S = st.S
    NTB = st.NTB
    sk = st.sk
    xn_fm = st.cur["xn_fm"]

    def A(name, shape, dt):
        return sk.enter_context(nc.sbuf_tensor(f"f0{name}{bi}", shape, dt))

    wf1 = [A(f"w1_{i}", [P, D], BF16) for i in (0, 1)]
    wf2 = [A(f"w2_{i}", [P, D], BF16) for i in (0, 1)]
    stash = A("stash", [P, FT_ * 512], BF16)
    fstage = A("fst", [P, 2048], BF16)
    prev = {}
    prev2 = {}
    for ob in range(S // 512):
        t0 = ob * 512
        with (nc.psum_tensor(_psname(st, "ph"), [P, 512], F32) as ph0,
              nc.psum_tensor(_psname(st, "ph"), [P, 512], F32) as ph1):
            php = [ph0, ph1]
            b.wait(b.pe, "ve")
            b.wait(b.pe, "se")
            for jf in range(FT_):
                k = jf % 2
                if k in prev:
                    b.sq.wait_ge(b.sem["pe"], prev[k][1])
                vw = b.load(f"wf1{k}", wf1[k][:], st.s_w1t0.ap()[jf, :, :])
                b.wait(b.pe, f"wf1{k}", vw)
                if k in prev:
                    b.pe.wait_ge(b.sem["se"], prev[k][0])
                for jd in range(DTC):
                    mm = b.pe.matmul(php[k][:], wf1[k][:, jd * P:(jd + 1) * P],
                                     xn_fm[:, jd * S + t0:jd * S + t0 + 512],
                                     start=(jd == 0), stop=(jd == DTC - 1))
                pv = b.tick(mm, "pe")
                b.wait(b.se, "pe", pv)
                sv = b.tick(b.se.activation(stash[:, jf * 512:(jf + 1) * 512], php[k][:],
                                            AF.Gelu, bias=st.b1c0[:, jf:jf + 1]), "se")
                prev[k] = (sv, pv)
        for sub in range(2):
            with nc.psum_tensor(_psname(st, "pob"), [P, 2048], F32) as pob:
                b.wait(b.pe, "se")
                b.wait(b.pe, "ve")
                for jf in range(FT_):
                    k = jf % 2
                    if k in prev2:
                        b.sq.wait_ge(b.sem["pe"], prev2[k])
                    vw = b.load(f"wf2{k}", wf2[k][:], st.s_w2t0.ap()[jf, :, :])
                    b.wait(b.pe, f"wf2{k}", vw)
                    for jd in range(DTC):
                        mm = b.pe.matmul(
                            pob[:, jd * 256:(jd + 1) * 256],
                            wf2[k][:, jd * P:(jd + 1) * P],
                            stash[:, jf * 512 + sub * 256:jf * 512 + (sub + 1) * 256],
                            start=(jf == 0 and jd % 2 == 0),
                            stop=(jf == FT_ - 1 and jd % 2 == 1))
                    prev2[k] = b.tick(mm, "pe")
                b.wait(b.se, "pe")
                b.wait(b.se, "ve")
                for jd in range(DTC):
                    sv = b.tick(b.se.activation(
                        fstage[:, jd * 256:(jd + 1) * 256], pob[:, jd * 256:(jd + 1) * 256],
                        AF.Identity, bias=st.b2c0[:, jd:jd + 1]), "se")
            with nc.psum_tensor(_psname(st, "prf"), [P, D], BF16) as prf:
                b.wait(b.pe, "se", sv)
                for tt in range(2):
                    b.wait(b.pe, "ve")
                    for jd in range(DTC):
                        mm = b.pe.transpose(prf[:, jd * P:(jd + 1) * P],
                                            fstage[:, jd * 256 + tt * P:jd * 256 + (tt + 1) * P],
                                            st.ident_bf[:])
                    pv = b.tick(mm, "pe")
                    b.wait(b.ve, "pe", pv)
                    gt = bi * NTB + (t0 + sub * 256 + tt * P) // P
                    b.ve.drain()
                    b.tick(b.ve.tensor_add(st.xres[:, gt * D:(gt + 1) * D],
                                           st.xres[:, gt * D:(gt + 1) * D], prf[:]), "ve")


def _emit_output_from_xres(st):
    b = st.b
    nc = b.nc
    NT = st.NT
    with ExitStack() as sk:
        ot = sk.enter_context(nc.sbuf_tensor("odump", [P, D], F32))
        for t in range(NT):
            if t:
                b.wait(b.ve, "out")
            b.ve.drain()
            vv = b.tick(b.ve.tensor_copy(
                ot[:], st.xres[:, t * D:(t + 1) * D]), "ve")
            b.sq.wait_ge(b.sem["ve"], vv)
            bi, tt = divmod(t, st.NTB)
            b.store("out", st.x_out.ap()[bi, tt * P:(tt + 1) * P, :], ot[:])


def _layer1(st, bi):
    b = st.b
    nc = b.nc
    S = st.S
    with ExitStack() as sk:
        st.sk = sk
        st.cur = {}
        with ExitStack() as skA:
            st.cur["xn_fm"] = skA.enter_context(
                nc.sbuf_tensor(f"xnfm1{bi}", [P, DTC * S], BF16))
            saved, st.sk = st.sk, skA
            _ln_tiles(st, bi, 1)
            if "attn1" not in st.skip:
                _attn1_proj(st, bi)
                _attn1_core(st, bi)
            st.sk = saved
        with ExitStack() as skB:
            st.cur["xq_fm"] = skB.enter_context(
                nc.sbuf_tensor(f"xqfm{bi}", [P, DTC * S], BF16))
            saved, st.sk = st.sk, skB
            if "ffn1" not in st.skip:
                _ffn1(st, bi)
            else:
                _merge_no_ffn(st, bi)
            st.sk = saved


def _merge_no_ffn(st, bi):
    b = st.b
    b.phase_barrier()
    nc = b.nc
    NTB = st.NTB
    with ExitStack() as sk:
        o2l = sk.enter_context(nc.sbuf_tensor(f"no2l{bi}", [P, D], BF16))
        o2f = sk.enter_context(nc.sbuf_tensor(f"no2f{bi}", [P, D], BF16))
        wo = [sk.enter_context(nc.sbuf_tensor(f"nwo{bi}{i}", [P, D], BF16))
              for i in (0, 1)]
        outf = sk.enter_context(nc.sbuf_tensor(f"nouf{bi}", [P, D], F32))
        prevw = {}
        for t in range(NTB):
            if "atr" in b.cnt:
                b.wait(b.sq, "pe")
            b.wait(b.sq, "atw")
            vf = b.load("atr", o2l[:], st.s_atok.ap()[bi, t * P:(t + 1) * P, :])
            with (nc.psum_tensor(_psname(st, "nmt"), [P, D], BF16) as ptr,
                  nc.psum_tensor(_psname(st, "nmb"), [P, D], F32) as pbs):
                b.wait(b.pe, "ve")
                b.wait(b.pe, "atr", vf)
                for j in range(DTC):
                    mm = b.pe.transpose(ptr[:, j * P:(j + 1) * P],
                                        o2l[:, j * P:(j + 1) * P], st.ident_bf[:])
                pv = b.tick(mm, "pe")
                b.wait(b.ve, "pe", pv)
                b.ve.drain()
                vv = b.tick(b.ve.tensor_copy(o2f[:], ptr[:]), "ve")
                b.wait(b.pe, "ve", vv)
                for je in range(DTC):
                    k = je % 2
                    if k in prevw:
                        b.sq.wait_ge(b.sem["pe"], prevw[k])
                    vw = b.load(f"wo{k}", wo[k][:],
                                st.s_wT["L1_attn_Wo"].ap()[je * P:(je + 1) * P, :])
                    b.wait(b.pe, f"wo{k}", vw)
                    for half in range(2):
                        mm = b.pe.matmul(pbs[:, half * 512:(half + 1) * 512],
                                         o2f[:, je * P:(je + 1) * P],
                                         wo[k][:, half * 512:(half + 1) * 512],
                                         start=(je == 0), stop=(je == DTC - 1))
                    prevw[k] = b.tick(mm, "pe")
                b.wait(b.ve, "pe")
                if "out" in b.cnt:
                    b.wait(b.ve, "out")
                b.ve.drain()
                gtx = bi * NTB + t
                vv = b.tick(b.ve.tensor_add(outf[:], pbs[:],
                                            st.xres[:, gtx * D:(gtx + 1) * D]), "ve")
                b.sq.wait_ge(b.sem["ve"], vv)
                b.store("out", st.x_out.ap()[bi, t * P:(t + 1) * P, :], outf[:])


def _ffn1(st, bi):
    st.b.phase_barrier()
    b = st.b
    nc = b.nc
    S = st.S
    NTB = st.NTB
    sk = st.sk

    def A(name, shape, dt):
        return sk.enter_context(nc.sbuf_tensor(f"f1{name}{bi}", shape, dt))

    xq_fm = st.cur["xq_fm"]
    xt = A("xt", [P, D], F32)
    sq = A("sq", [P, D], F32)
    stat = A("st", [P, 4], F32)
    rbf = A("rbf", [P, D], BF16)
    rf32 = A("rf32", [P, D], F32) if FP64_SIM else sq
    q1s = A("q1s", [P, D], BF16)
    with (nc.psum_tensor(_psname(st, "pfl"), [P, 512], BF16) as pt0,
          nc.psum_tensor(_psname(st, "pfl"), [P, 512], BF16) as pt1):
        pts = [pt0, pt1]
        b.wait(b.pe, "ve")
        b.wait(b.pe, "se")
        ve_evac = {}
        for t in range(NTB):
            gt = bi * NTB + t
            b.ve.drain()
            b.wait(b.ve, "se")
            b.ve.tensor_copy(xt[:], st.xres[:, gt * D:(gt + 1) * D])
            b.ve.drain()
            b.ve.reduce_sum(stat[:, 0:1], xt[:], axis=AX.X)
            b.ve.drain()
            b.ve.tensor_scalar_mul(stat[:, 1:2], stat[:, 0:1], -1.0 / D)
            b.ve.drain()
            b.ve.scalar_tensor_tensor(sq[:], xt[:], stat[:, 1:2], xt[:],
                                      ALU.add, ALU.mult, accum_out=stat[:, 0:1])
            b.ve.drain()
            b.ve.tensor_scalar(stat[:, 2:3], stat[:, 0:1], 1.0 / D, LN_EPS,
                               ALU.mult, ALU.add)
            b.ve.drain()
            vv = b.tick(b.ve.reciprocal(stat[:, 3:4], stat[:, 2:3]), "ve")
            b.wait(b.se, "ve", vv)
            sv = b.tick(b.se.activation(stat[:, 2:3], stat[:, 3:4], AF.Sqrt), "se")
            b.wait(b.ve, "se", sv)
            b.ve.tensor_scalar(sq[:], xt[:], stat[:, 1:2], stat[:, 2:3],
                               ALU.add, ALU.mult)
            b.ve.drain()
            b.ve.reduce_max(stat[:, 0:1], sq[:], axis=AX.X, apply_absolute_value=True)
            b.ve.drain()
            b.ve.tensor_single_scalar(stat[:, 0:1], stat[:, 0:1], 1e-5, ALU.max)
            b.ve.drain()
            b.ve.reciprocal(stat[:, 1:2], stat[:, 0:1])
            b.ve.drain()
            b.ve.tensor_scalar_mul(stat[:, 1:2], stat[:, 1:2], 127.0)
            b.ve.drain()
            b.ve.tensor_scalar(stat[:, 2:3], stat[:, 0:1], st.wscol[0][:],
                               1.0 / 127.0, ALU.mult, ALU.mult)
            b.ve.drain()
            b.tick(b.ve.tensor_scalar(sq[:], sq[:], stat[:, 1:2], None, ALU.mult), "ve")
            b.emit_round(sq[:], sq[:], xt, rbf, rf32, D)
            b.ve.drain()
            b.ve.tensor_scalar(q1s[:], sq[:], stat[:, 2:3], None, ALU.mult)
            vq = b.tick(b.ve.drain(), "ve")
            b.wait(b.pe, "ve", vq)
            xq3 = xq_fm.ap().rearrange("p (j s) -> p j s", j=DTC)
            for jb in range(2):
                k = jb % 2
                if k in ve_evac:
                    b.pe.wait_ge(b.sem["ve"], ve_evac[k])
                for j4 in range(4):
                    j = jb * 4 + j4
                    mm = b.pe.transpose(pts[k][:, j4 * P:(j4 + 1) * P],
                                        q1s[:, j * P:(j + 1) * P], st.ident_bf[:])
                pv = b.tick(mm, "pe")
                b.wait(b.ve, "pe", pv)
                ve_evac[k] = b.tick(b.ve.tensor_copy(
                    xq3[:, jb * 4:(jb + 1) * 4, t * P:(t + 1) * P],
                    pts[k].ap().rearrange("p (j c) -> p j c", j=4)), "ve")
    b.wait(b.pe, "ve")

    wf1 = [A(f"w1_{i}", [P, D], BF16) for i in (0, 1)]
    wf2 = [A(f"w2_{i}", [P, D], BF16) for i in (0, 1)]
    wo = [A(f"wo{i}", [P, D], BF16) for i in (0, 1)]
    hfm2 = [A(f"hfm{i}", [P, 512], BF16) for i in (0, 1)]
    ght = A("ght", [P, 4 * FF], BF16)
    scf = A("scf", [P, 2048], F32)
    scs = A("scs", [P, 2048], F32)
    scb = A("scb", [P, 2048], BF16)
    scv = A("scv", [P, 2048], F32) if FP64_SIM else scs
    sca = A("sca", [P, 2048], F32)
    q2s = A("q2s", [P, FF], BF16)
    qst = A("qst", [P, 4], F32)
    o2l = A("o2l", [P, D], BF16)
    o2f4 = A("o2f4", [P, 4 * D], BF16)
    prev = {}
    prevw = {}
    prevo = {}
    for ob in range(S // 512):
        t0 = ob * 512
        with (nc.psum_tensor(_psname(st, "p1h"), [P, 512], F32) as ph0,
              nc.psum_tensor(_psname(st, "p1h"), [P, 512], F32) as ph1,
              nc.psum_tensor(_psname(st, "p1t"), [P, 512], BF16) as pt0,
              nc.psum_tensor(_psname(st, "p1t"), [P, 512], BF16) as pt1):
            php = [ph0, ph1]
            ptp = [pt0, pt1]
            b.wait(b.pe, "ve")
            b.wait(b.pe, "se")
            tr_evac = {}
            ght3 = ght.ap().rearrange("p (t f) -> p t f", t=4)

            def emit_tr(jfp, svp):
                b.wait(b.pe, "se", svp)
                kk = jfp % 2
                if kk in tr_evac:
                    b.pe.wait_ge(b.sem["ve"], tr_evac[kk])
                for tt in range(4):
                    mm = b.pe.transpose(
                        ptp[kk][:, tt * P:(tt + 1) * P],
                        hfm2[jfp % 2][:, tt * P:(tt + 1) * P], st.ident_bf[:])
                pv2 = b.tick(mm, "pe")
                b.wait(b.ve, "pe", pv2)
                tr_evac[kk] = b.tick(b.ve.tensor_copy(
                    ght3[:, :, jfp * P:(jfp + 1) * P],
                    ptp[kk].ap().rearrange("p (t c) -> p t c", t=4)), "ve")
            pend = None
            for jf in range(FT_):
                k = jf % 2
                if k in prev:
                    b.sq.wait_ge(b.sem["pe"], prev[k][1])
                vw = b.load(f"wf1{k}", wf1[k][:], st.s_w1t1.ap()[jf, :, :])
                b.wait(b.pe, f"wf1{k}", vw)
                if k in prev:
                    b.pe.wait_ge(b.sem["se"], prev[k][0])
                for jd in range(DTC):
                    mm = b.pe.matmul(php[k][:], wf1[k][:, jd * P:(jd + 1) * P],
                                     xq_fm[:, jd * S + t0:jd * S + t0 + 512],
                                     start=(jd == 0), stop=(jd == DTC - 1))
                pv = b.tick(mm, "pe")
                b.wait(b.se, "pe", pv)
                sv = b.tick(b.se.activation(hfm2[k][:], php[k][:], AF.Gelu,
                                            bias=st.b1c1[:, jf:jf + 1]), "se")
                prev[k] = (sv, pv)
                if pend is not None:
                    emit_tr(*pend)
                pend = (jf, sv)
            emit_tr(*pend)
        for tt in range(4):
            gt_sl = ght[:, tt * FF:(tt + 1) * FF]
            b.ve.drain()
            b.wait(b.ve, "se")
            b.ve.reduce_max(qst[:, 0:1], gt_sl[:, 0:2048], axis=AX.X,
                            apply_absolute_value=True)
            b.ve.drain()
            b.ve.reduce_max(qst[:, 1:2], gt_sl[:, 2048:], axis=AX.X,
                            apply_absolute_value=True)
            b.ve.drain()
            b.ve.tensor_tensor(qst[:, 0:1], qst[:, 0:1], qst[:, 1:2], ALU.max)
            b.ve.drain()
            b.ve.tensor_single_scalar(qst[:, 0:1], qst[:, 0:1], 1e-5, ALU.max)
            b.ve.drain()
            b.ve.reciprocal(qst[:, 1:2], qst[:, 0:1])
            b.ve.drain()
            b.ve.tensor_scalar_mul(qst[:, 1:2], qst[:, 1:2], 127.0)
            b.ve.drain()
            b.ve.tensor_scalar(qst[:, 2:3], qst[:, 0:1], st.wscol[1][:],
                               1.0 / 127.0, ALU.mult, ALU.mult)
            b.tick(b.ve.drain(), "ve")
            for hh in range(2):
                hsl = slice(hh * 2048, (hh + 1) * 2048)
                b.wait(b.se, "ve")
                b.wait(b.se, "pe")
                if "out" in b.cnt:
                    b.wait(b.se, "out")
                b.tick(b.se.activation(sca[:], gt_sl[:, hsl], AF.Identity,
                                       scale=qst[:, 1:2]), "se")
                b.se.drain()
                b.wait(b.ve, "se")
                b.emit_round(scf[:], sca[:], scs, scb, scv, 2048)
                b.wait(b.se, "ve")
                qv = b.tick(b.se.activation(q2s[:, hsl], scf[:], AF.Identity,
                                            scale=qst[:, 2:3]), "se")
            with (nc.psum_tensor(_psname(st, "p1u"), [P, 512], BF16) as pu0,
                  nc.psum_tensor(_psname(st, "p1u"), [P, 512], BF16) as pu1):
                pup = [pu0, pu1]
                b.wait(b.pe, "se", qv)
                ev = {}
                for jb in range(FT_ // 4):
                    k = jb % 2
                    if k in ev:
                        b.pe.wait_ge(b.sem["ve"], ev[k])
                    for j4 in range(4):
                        jf = jb * 4 + j4
                        mm = b.pe.transpose(pup[k][:, j4 * P:(j4 + 1) * P],
                                            q2s[:, jf * P:(jf + 1) * P],
                                            st.ident_bf[:])
                    pv = b.tick(mm, "pe")
                    b.wait(b.ve, "pe", pv)
                    ev[k] = b.tick(b.ve.tensor_copy(
                        ght[:, tt * FF + jb * 512:tt * FF + (jb + 1) * 512],
                        pup[k][:]), "ve")
        if "attn1" not in st.skip:
            with (nc.psum_tensor(_psname(st, "p1r"), [P, D], BF16) as pr0,
                  nc.psum_tensor(_psname(st, "p1r"), [P, D], BF16) as pr1):
                prp = [pr0, pr1]
                b.wait(b.pe, "ve")
                b.wait(b.pe, "se")
                trv = {}
                for tt in range(4):
                    k = tt % 2
                    if "atr" in b.cnt:
                        b.wait(b.sq, "pe")
                    b.wait(b.sq, "atw")
                    vf = b.load("atr", o2l[:],
                                st.s_atok.ap()[bi, (ob * 4 + tt) * P:
                                               (ob * 4 + tt + 1) * P, :])
                    b.wait(b.pe, "atr", vf)
                    if k in trv:
                        b.pe.wait_ge(b.sem["ve"], trv[k])
                    for j in range(DTC):
                        mm = b.pe.transpose(prp[k][:, j * P:(j + 1) * P],
                                            o2l[:, j * P:(j + 1) * P],
                                            st.ident_bf[:])
                    pv = b.tick(mm, "pe")
                    b.wait(b.ve, "pe", pv)
                    b.ve.drain()
                    trv[k] = b.tick(b.ve.tensor_copy(
                        o2f4[:, tt * D:(tt + 1) * D], prp[k][:]), "ve")
        with (nc.psum_tensor(_psname(st, "p1o0"), [P, D], F32) as po0,
              nc.psum_tensor(_psname(st, "p1o1"), [P, D], F32) as po1,
              nc.psum_tensor(_psname(st, "p1o2"), [P, D], F32) as po2,
              nc.psum_tensor(_psname(st, "p1o3"), [P, D], F32) as po3):
            pos = [po0, po1, po2, po3]
            b.wait(b.pe, "ve")
            b.wait(b.pe, "se")
            for jf in range(FT_):
                k = jf % 2
                if k in prevw:
                    b.sq.wait_ge(b.sem["pe"], prevw[k])
                vw = b.load(f"wf2{k}", wf2[k][:], st.s_w2t1.ap()[jf, :, :])
                b.wait(b.pe, f"wf2{k}", vw)
                for tt in range(4):
                    for half in range(2):
                        mm = b.pe.matmul(
                            pos[tt][:, half * 512:(half + 1) * 512],
                            ght[:, tt * FF + jf * P:tt * FF + (jf + 1) * P],
                            wf2[k][:, half * 512:(half + 1) * 512],
                            start=(jf == 0), stop=False)
                prevw[k] = b.tick(mm, "pe")
            if "attn1" not in st.skip:
                for je in range(DTC):
                    k = je % 2
                    if k in prevo:
                        b.sq.wait_ge(b.sem["pe"], prevo[k])
                    vw = b.load(f"wo{k}", wo[k][:],
                                st.s_wT["L1_attn_Wo"].ap()[je * P:(je + 1) * P, :])
                    b.wait(b.pe, f"wo{k}", vw)
                    for tt in range(4):
                        for half in range(2):
                            mm = b.pe.matmul(
                                pos[tt][:, half * 512:(half + 1) * 512],
                                o2f4[:, tt * D + je * P:tt * D + (je + 1) * P],
                                wo[k][:, half * 512:(half + 1) * 512],
                                start=False, stop=False)
                    prevo[k] = b.tick(mm, "pe")
            for tt in range(4):
                gt = bi * NTB + ob * 4 + tt
                for half in range(2):
                    mm = b.pe.matmul(pos[tt][:, half * 512:(half + 1) * 512],
                                     st.ones_row[:],
                                     st.b2r1[:, half * 512:(half + 1) * 512],
                                     start=False, stop=True)
                pv = b.tick(mm, "pe")
                b.wait(b.ve, "pe", pv)
                if "out" in b.cnt:
                    b.wait(b.ve, "out")
                b.ve.drain()
                osl = sca[:, (tt % 2) * D:(tt % 2 + 1) * D]
                vv = b.tick(b.ve.tensor_add(osl, pos[tt][:],
                                            st.xres[:, gt * D:(gt + 1) * D]), "ve")
                b.sq.wait_ge(b.sem["ve"], vv)
                b.store("out", st.x_out.ap()[bi, (ob * 4 + tt) * P:
                                             (ob * 4 + tt + 1) * P, :], osl)


def _attn1_proj(st, bi):
    st.b.phase_barrier()
    b = st.b
    nc = b.nc
    S = st.S
    xn_fm = st.cur["xn_fm"]
    sk = st.sk

    def A(name, shape, dt):
        return sk.enter_context(nc.sbuf_tensor(f"a1{name}{bi}", shape, dt))

    kfm = A("kfm", [P, DTC * S], BF16)
    vtok = A("vtok", [P, (S // P) * H * 65], BF16)
    wall = A("wall", [P, DTC * D], BF16)
    st.cur.update(dict(kfm=kfm, vtok=vtok, wqall=wall))

    NTB = st.NTB
    b.ve.drain()
    b.tick(b.ve.memset(
        vtok.ap().rearrange("p (t h e) -> p t h e", t=NTB, h=H)[:, :, :, 64:65], 1.0), "ve")

    b.wait(b.sq, "pe")
    vlast = b.load("wallh", wall.ap().rearrange("p (jd e) -> p jd e", jd=DTC),
                   st.s_wT["L1_attn_Wk"].ap().rearrange("(jd p) e -> p jd e", p=P))
    b.wait(b.pe, "wallh", vlast)
    with (nc.psum_tensor(_psname(st, "pa"), [P, 512], F32) as p0,
          nc.psum_tensor(_psname(st, "pa"), [P, 512], F32) as p1):
        pp = [p0, p1]
        b.wait(b.pe, "ve")
        b.wait(b.pe, "se")
        ev = {}
        idx = 0
        for je in range(DTC):
            for tc in range(S // 512):
                k = idx % 2
                idx += 1
                if k in ev:
                    b.pe.wait_ge(b.sem["ve"], ev[k])
                for jd in range(DTC):
                    mm = b.pe.matmul(
                        pp[k][:], wall[:, jd * D + je * P:jd * D + (je + 1) * P],
                        xn_fm[:, jd * S + tc * 512:jd * S + (tc + 1) * 512],
                        start=(jd == 0), stop=(jd == DTC - 1))
                pv = b.tick(mm, "pe")
                b.wait(b.ve, "pe", pv)
                ev[k] = b.tick(b.ve.tensor_copy(
                    kfm[:, je * S + tc * 512:je * S + (tc + 1) * 512], pp[k][:]), "ve")
    b.wait(b.sq, "pe")
    vlast = b.load("wallh", wall.ap().rearrange("p (jd e) -> p jd e", jd=DTC),
                   st.s_wT["L1_attn_Wv"].ap().rearrange("(jd p) e -> p jd e", p=P))
    b.wait(b.pe, "wallh", vlast)
    with (nc.psum_tensor(_psname(st, "pa"), [P, D], F32) as p0,
          nc.psum_tensor(_psname(st, "pa"), [P, D], F32) as p1):
        pp = [p0, p1]
        b.wait(b.pe, "ve")
        b.wait(b.pe, "se")
        ev = {}
        for tt in range(NTB):
            k = tt % 2
            if k in ev:
                b.pe.wait_ge(b.sem["ve"], ev[k])
            for jd in range(DTC):
                lhs = xn_fm[:, jd * S + tt * P:jd * S + (tt + 1) * P]
                b.pe.matmul(pp[k][:, 0:512], lhs, wall[:, jd * D:jd * D + 512],
                            start=(jd == 0), stop=(jd == DTC - 1))
                mm = b.pe.matmul(pp[k][:, 512:D], lhs,
                                 wall[:, jd * D + 512:(jd + 1) * D],
                                 start=(jd == 0), stop=(jd == DTC - 1))
            pv = b.tick(mm, "pe")
            b.wait(b.ve, "pe", pv)
            dst = vtok.ap().rearrange("p (t h e) -> p t h e", t=NTB, h=H)[
                :, tt, :, 0:64]
            ev[k] = b.tick(b.ve.tensor_copy(
                dst, pp[k].ap().rearrange("p (h e) -> p h e", h=H)), "ve")
    b.wait(b.sq, "pe")
    vlast = b.load("wallh", wall.ap().rearrange("p (jd e) -> p jd e", jd=DTC),
                   st.s_wT["L1_attn_Wq"].ap().rearrange("(jd p) e -> p jd e", p=P))
    st.cur["wq_v"] = vlast


def _attn1_core(st, bi):
    st.b.phase_barrier()
    b = st.b
    nc = b.nc
    S = st.S
    NTB = st.NTB
    NKT = S // P
    NQC = S // QCH
    sk = st.sk
    kfm, vtok, wall = st.cur["kfm"], st.cur["vtok"], st.cur["wqall"]
    b.wait(b.pe, "wallh", st.cur["wq_v"])

    def A(name, shape, dt):
        return sk.enter_context(nc.sbuf_tensor(f"c1{name}{bi}", shape, dt))

    qch = A("qch", [P, DTC * QCH], BF16)
    At2 = [A(f"At{i}", [P, NKT * QCH], BF16) for i in (0, 1)]
    otsb2 = [A(f"otsb{i}", [P, QCH], BF16) for i in (0, 1)]
    otok = A("otok", [P, (QCH // P) * H * 65], BF16)
    rd = A("rd", [P, H], F32)
    dots = A("dots", [P, H], F32)
    vns = A("vns", [P, H], F32)
    cc2 = A("cc2", [P, H], F32)
    sc1 = A("sc1", [P, D], BF16)

    prev_qgate = None
    for qc in range(NQC):
        q0 = qc * QCH
        with (nc.psum_tensor(_psname(st, "pq"), [P, 2 * QCH], F32) as p0,
              nc.psum_tensor(_psname(st, "pq"), [P, 2 * QCH], F32) as p1):
            pp = [p0, p1]
            if prev_qgate is None:
                b.wait(b.pe, "ve")
            else:
                b.pe.wait_ge(b.sem["ve"], prev_qgate)
            b.wait(b.pe, "se")
            ev = {}
            for je in range(0, DTC, 2):
                k = (je // 2) % 2
                if k in ev:
                    b.pe.wait_ge(b.sem["se"], ev[k])
                for j2 in range(2):
                    for jd in range(DTC):
                        mm = b.pe.matmul(
                            pp[k][:, j2 * QCH:(j2 + 1) * QCH],
                            wall[:, jd * D + (je + j2) * P:jd * D + (je + j2 + 1) * P],
                            st.cur["xn_fm"][:, jd * S + q0:jd * S + q0 + QCH],
                            start=(jd == 0), stop=(jd == DTC - 1))
                pv = b.tick(mm, "pe")
                b.wait(b.se, "pe", pv)
                b.se.drain()
                ev[k] = b.tick(b.se.activation(
                    qch[:, je * QCH:(je + 2) * QCH], pp[k][:], AF.Identity), "se")
        with (nc.psum_tensor(_psname(st, "pss"), [P, 4 * QCH], F32) as ps0,
              nc.psum_tensor(_psname(st, "pss"), [P, 4 * QCH], F32) as ps1,
              nc.psum_tensor(_psname(st, "pot"), [65, QCH], F32) as pt0_,
              nc.psum_tensor(_psname(st, "pot"), [65, QCH], F32) as pt1_,
              nc.psum_tensor(_psname(st, "ptq"), [P, 2 * 66], BF16) as pq0,
              nc.psum_tensor(_psname(st, "ptq"), [P, 2 * 66], BF16) as pq1):
            psp = [ps0, ps1]
            potp = [pt0_, pt1_]
            pqt = [pq0, pq1]
            tr_ev = {}
            se_done = {}
            pot_ev = {}
            exps_of = {}
            b.wait(b.pe, "se")

            def emit_scores(h):
                qf = qch[(h % 2) * 64:(h % 2) * 64 + 64,
                         (h // 2) * QCH:(h // 2) * QCH + QCH]
                Ath = At2[h % 2]
                for kg in range(NKT // 4):
                    k = kg % 2
                    if k in se_done:
                        b.pe.wait_ge(b.sem["se"], se_done[k])
                    for kt4 in range(4):
                        kt = kg * 4 + kt4
                        kf = kfm[(h % 2) * 64:(h % 2) * 64 + 64,
                                 (h // 2) * S + kt * P:(h // 2) * S + (kt + 1) * P]
                        mm = b.pe.matmul(psp[k][:, kt4 * QCH:(kt4 + 1) * QCH],
                                         kf, qf, start=True, stop=True)
                    pv = b.tick(mm, "pe")
                    b.wait(b.se, "pe", pv)
                    se_done[k] = b.tick(b.se.activation(
                        Ath[:, kg * 4 * QCH:(kg + 1) * 4 * QCH], psp[k][:],
                        AF.Exp, bias=st.shift_col[:], scale=0.125), "se")
                exps_of[h] = se_done[k]

            def emit_ot(hp, pvp):
                b.wait(b.ve, "pe", pvp)
                b.ve.drain()
                vv = b.tick(b.ve.tensor_copy(otsb2[hp % 2][0:65, :],
                                             potp[hp % 2][:]), "ve")
                pot_ev[hp % 2] = vv
                b.wait(b.pe, "ve", vv)
                k = hp % 2
                if k in tr_ev:
                    b.pe.wait_ge(b.sem["ve"], tr_ev[k])
                for tt in range(QCH // P):
                    mm = b.pe.transpose(
                        pqt[k][:, tt * 66:tt * 66 + 65],
                        otsb2[hp % 2][0:65, tt * P:(tt + 1) * P],
                        st.ident_bf[0:65, 0:65])
                pv = b.tick(mm, "pe")
                b.wait(b.ve, "pe", pv)
                tr_ev[k] = b.tick(b.ve.tensor_copy(
                    otok.ap().rearrange("p (t x) -> p t x", t=QCH // P)[
                        :, :, hp * 65:(hp + 1) * 65],
                    pqt[k].ap().rearrange("p (t c) -> p t c", c=66)[:, :, 0:65]), "ve")
            emit_scores(0)
            pend = None
            for h in range(H):
                if h + 1 < H:
                    emit_scores(h + 1)
                b.pe.wait_ge(b.sem["se"], exps_of[h])
                if h % 2 in pot_ev:
                    b.wait(b.pe, "ve", pot_ev[h % 2])
                Ath = At2[h % 2]
                for kt in range(NKT):
                    va = vtok.ap().rearrange(
                        "p (t h e) -> p t h e", t=NTB, h=H)[:, kt, h, :]
                    mm = b.pe.matmul(potp[h % 2][:], va,
                                     Ath[:, kt * QCH:(kt + 1) * QCH],
                                     start=(kt == 0), stop=(kt == NKT - 1))
                pv = b.tick(mm, "pe")
                if pend is not None:
                    emit_ot(*pend)
                pend = (h, pv)
            emit_ot(*pend)
            prev_qgate = b.cnt["ve"]
            for tt in range(QCH // P):
                gt = q0 // P + tt
                po3 = otok.ap().rearrange("p (t h e) -> p t h e", t=QCH // P,
                                          h=H)[:, tt]
                b.ve.drain()
                if "atw" in b.cnt:
                    b.wait(b.ve, "atw")
                b.ve.reciprocal(rd[:], po3[:, :, 64:65])
                b.ve.drain()
                v3 = vtok.ap().rearrange("p (t h e) -> p t h e", t=NTB, h=H)[
                    :, gt, :, 0:64]
                b.ve.tensor_tensor(sc1.ap().rearrange("p (h e) -> p h e", h=H),
                                   po3[:, :, 0:64], v3, ALU.mult)
                b.ve.drain()
                b.ve.reduce_sum(dots[:], sc1.ap().rearrange("p (h e) -> p h e", h=H),
                                axis=AX.X)
                b.ve.drain()
                b.ve.tensor_tensor(sc1.ap().rearrange("p (h e) -> p h e", h=H),
                                   v3, v3, ALU.mult)
                b.ve.drain()
                b.ve.reduce_sum(vns[:], sc1.ap().rearrange("p (h e) -> p h e", h=H),
                                axis=AX.X)
                b.ve.drain()
                b.ve.tensor_scalar_add(vns[:], vns[:], XSA_EPS)
                b.ve.drain()
                b.ve.reciprocal(vns[:], vns[:])
                b.ve.drain()
                b.ve.tensor_mul(cc2[:], dots[:], vns[:])
                b.ve.drain()
                b.ve.tensor_scalar(cc2[:], cc2[:], st.xsa_col[:], None, ALU.mult)
                b.ve.drain()
                b.ve.tensor_tensor(sc1.ap().rearrange("p (h e) -> p h e", h=H),
                                   cc2.ap().unsqueeze(2).broadcast_to([P, H, 64]),
                                   v3, ALU.mult)
                b.ve.drain()
                b.ve.tensor_tensor(sc1.ap().rearrange("p (h e) -> p h e", h=H),
                                   po3[:, :, 0:64],
                                   sc1.ap().rearrange("p (h e) -> p h e", h=H), ALU.add)
                b.ve.drain()
                vv = b.tick(b.ve.tensor_tensor(
                    sc1.ap().rearrange("p (h e) -> p h e", h=H),
                    sc1.ap().rearrange("p (h e) -> p h e", h=H),
                    rd.ap().unsqueeze(2).broadcast_to([P, H, 64]), ALU.mult), "ve")
                b.sq.wait_ge(b.sem["ve"], vv)
                b.store("atw", st.s_atok.ap()[bi, gt * P:(gt + 1) * P, :], sc1[:])




# version 14
# speedup vs baseline: 1.1283x; 1.1283x over previous
import os
from contextlib import ExitStack

import numpy as np
import ml_dtypes

import concourse.bass as bass
import concourse.mybir as mybir
from concourse.bass_utils import run_bass_kernel_spmd

F32 = mybir.dt.float32
BF16 = mybir.dt.bfloat16
AF = mybir.ActivationFunctionType
ALU = mybir.AluOpType
AX = mybir.AxisListType

FP64_SIM = os.environ.get("BASS_SIM_FP64") == "1"

H = 16
HD = 64
D = 1024
FF = 4096
P = 128
DTC = 8
FT_ = 32
GC = 128
SC = 256
QCH = 256
LN_EPS = 1e-5
XSA_EPS = 1e-8
W_KEYS = [
    "L0_attn_Wq", "L0_attn_Wk", "L0_attn_Wv", "L0_attn_Wg", "L0_attn_bg",
    "L0_attn_Wo", "L0_ffn_W1", "L0_ffn_b1", "L0_ffn_W2", "L0_ffn_b2",
    "L1_attn_Wq", "L1_attn_Wk", "L1_attn_Wv", "L1_attn_Wo", "L1_attn_xsa_scale",
    "L1_ffn_W1", "L1_ffn_b1", "L1_ffn_W2", "L1_ffn_b2",
]


def _consts():
    bf = ml_dtypes.bfloat16
    ident_bf = np.eye(P, dtype=np.float32).astype(bf)
    ident_f32 = np.eye(P, dtype=np.float32)
    mask_u = np.triu(np.ones((GC, GC), dtype=np.float32)).astype(bf)
    mask_tiled = np.tile(mask_u, (1, H))
    m1 = (0.5 * np.ones((GC, GC)) - np.triu(np.ones((GC, GC)))).astype(np.float32)
    ones_row_bf = np.ones((1, P), dtype=np.float32).astype(bf)
    ones_col_f32 = np.ones((P, 1), dtype=np.float32)
    dup = np.zeros((64, P), dtype=np.float32)
    dup[np.arange(64), np.arange(64)] = 1.0
    dup[np.arange(64), 64 + np.arange(64)] = 1.0
    sel = np.zeros((P, P), dtype=np.float32)
    sel[127, :] = 1.0
    return {
        "c_ident_bf": ident_bf,
        "c_ident_f32": ident_f32,
        "c_mask": mask_tiled,
        "c_m1": m1,
        "c_ones_row_bf": ones_row_bf,
        "c_ones_col_f32": ones_col_f32,
        "c_dup": dup.astype(bf),
        "c_sel": sel.astype(bf),
    }


class Bld:

    def __init__(self, nc, stack):
        self.nc = nc
        self.stack = stack
        self.pe = nc.tensor
        self.ve = nc.vector
        self.se = nc.scalar
        self.sq = nc.sync
        self.gp = nc.gpsimd
        self.sem = {}
        self.cnt = {}
        self.kind = {}
        for name in ("pe", "ve", "se"):
            self.sem[name] = stack.enter_context(nc.semaphore("s_" + name))
            self.cnt[name] = 0

    def dsem(self, name, kind):
        if name not in self.sem:
            self.sem[name] = self.stack.enter_context(self.nc.semaphore("d_" + name))
            self.cnt[name] = 0
            self.kind[name] = kind
        assert self.kind[name] == kind
        return self.sem[name]

    def tick(self, instr, name, n=1):
        instr.then_inc(self.sem[name], n)
        self.cnt[name] += n
        return self.cnt[name]

    def wait(self, eng, name, val=None):
        eng.wait_ge(self.sem[name], self.cnt[name] if val is None else val)

    def wait_all(self, engs, names):
        for e in engs:
            for n in names:
                self.wait(e, n)

    def phase_barrier(self):
        for e in (self.pe, self.ve, self.se, self.sq, self.gp):
            for n in ("pe", "ve", "se"):
                self.wait(e, n)
        for nm, kd in list(self.kind.items()):
            self.wait(self.sq, nm)
            self.wait(self.gp, nm)

    def sb(self, name, shape, dt):
        return self.stack.enter_context(self.nc.sbuf_tensor(name, shape, dt))

    def load(self, name, dst_ap, src_ap):
        self.dsem(name, "h")
        return self.tick(self.sq.dma_start(dst_ap, src_ap), name, 16)

    def loadc(self, name, dst_ap, src_ap):
        self.dsem(name, "g")
        return self.tick(self.gp.dma_start(dst_ap, src_ap), name, 16)

    def store(self, name, dst_ap, src_ap):
        self.dsem(name, "h")
        return self.tick(self.sq.dma_start(dst_ap, src_ap), name, 16)

    def storeg(self, name, dst_ap, src_ap):
        self.dsem(name, "g")
        return self.tick(self.gp.dma_start(dst_ap, src_ap), name, 16)

    def transpose_block(self, dst_ps_ap, src_sb_ap, ident_ap):
        return self.pe.transpose(dst_ps_ap, src_sb_ap, ident_ap)

    def emit_round(self, dst_ap, src_ap, scr_sgn, scr_bf, scr_f32, n):
        b = self
        if not FP64_SIM:
            b.wait(b.se, "ve")
            b.tick(b.se.activation(scr_sgn[:, :n], src_ap, AF.Sign), "se")
            b.wait(b.ve, "se")
            b.ve.scalar_tensor_tensor(scr_bf[:, :n], scr_sgn[:, :n], 128.0,
                                      src_ap, ALU.mult, ALU.add)
            b.ve.drain()
            b.tick(b.ve.scalar_tensor_tensor(dst_ap, scr_sgn[:, :n], -128.0,
                                             scr_bf[:, :n], ALU.mult, ALU.add), "ve")
        else:
            v = scr_f32
            b.ve.tensor_scalar_add(v[:, :n], src_ap, 128.5)
            b.ve.drain()
            for c in (256.0, 128.0, 64.0, 32.0, 16.0, 8.0, 4.0, 2.0, 1.0):
                b.ve.tensor_single_scalar(scr_sgn[:, :n], v[:, :n], c, ALU.is_ge)
                b.ve.drain()
                b.ve.scalar_tensor_tensor(v[:, :n], scr_sgn[:, :n], -c, v[:, :n],
                                          ALU.mult, ALU.add)
                b.ve.drain()
            b.ve.tensor_scalar(scr_sgn[:, :n], src_ap, 128.5, -128.0, ALU.add, ALU.add)
            b.ve.drain()
            b.tick(b.ve.tensor_sub(dst_ap, scr_sgn[:, :n], v[:, :n]), "ve")


def kernel(**inputs):
    x = np.ascontiguousarray(np.asarray(inputs["L0_attn_x"], dtype=np.float32))
    B, S, Dm = x.shape
    n_cores = 8
    b_loc = B // n_cores
    consts = _consts()
    nc = build_program(b_loc, S)
    in_maps = []
    for c in range(n_cores):
        m = {"x_in": x[c * b_loc:(c + 1) * b_loc]}
        for k in W_KEYS:
            m[k] = np.ascontiguousarray(np.asarray(inputs[k], dtype=np.float32))
        m.update(consts)
        in_maps.append(m)
    res = run_bass_kernel_spmd(nc, in_maps, core_ids=list(range(n_cores)))
    return np.concatenate([r["x_out"] for r in res.results], axis=0)


def build_program(b_loc, S, skip=()):
    nc = bass.Bass("TRN2", target_bir_lowering=False)
    stack = ExitStack()
    b = Bld(nc, stack)
    emit_all(b, b_loc, S, set(skip))
    stack.close()
    return nc


def emit_all(b, b_loc, S, skip=()):
    nc = b.nc
    NTB = S // P
    NT = b_loc * NTB
    NSC = S // SC
    FT = FF // P
    DT = D // P

    x_in = nc.dram_tensor("x_in", [b_loc, S, D], F32, kind="ExternalInput")
    x_out = nc.dram_tensor("x_out", [b_loc, S, D], F32, kind="ExternalOutput")
    Wd = {}
    shp = {
        "L0_attn_Wq": [D, D], "L0_attn_Wk": [D, D], "L0_attn_Wv": [D, D],
        "L0_attn_Wg": [D, D], "L0_attn_bg": [D], "L0_attn_Wo": [D, D],
        "L0_ffn_W1": [FF, D], "L0_ffn_b1": [FF], "L0_ffn_W2": [D, FF],
        "L0_ffn_b2": [D],
        "L1_attn_Wq": [D, D], "L1_attn_Wk": [D, D], "L1_attn_Wv": [D, D],
        "L1_attn_Wo": [D, D], "L1_attn_xsa_scale": [1],
        "L1_ffn_W1": [FF, D], "L1_ffn_b1": [FF], "L1_ffn_W2": [D, FF],
        "L1_ffn_b2": [D],
    }
    for k, s_ in shp.items():
        Wd[k] = nc.dram_tensor(k, s_, F32, kind="ExternalInput")
    cd = {
        "c_ident_bf": nc.dram_tensor("c_ident_bf", [P, P], BF16, kind="ExternalInput"),
        "c_ident_f32": nc.dram_tensor("c_ident_f32", [P, P], F32, kind="ExternalInput"),
        "c_mask": nc.dram_tensor("c_mask", [P, H * GC], BF16, kind="ExternalInput"),
        "c_m1": nc.dram_tensor("c_m1", [GC, GC], F32, kind="ExternalInput"),
        "c_ones_row_bf": nc.dram_tensor("c_ones_row_bf", [1, P], BF16, kind="ExternalInput"),
        "c_ones_col_f32": nc.dram_tensor("c_ones_col_f32", [P, 1], F32, kind="ExternalInput"),
        "c_dup": nc.dram_tensor("c_dup", [64, P], BF16, kind="ExternalInput"),
        "c_sel": nc.dram_tensor("c_sel", [P, P], BF16, kind="ExternalInput"),
    }
    s_w1t0 = nc.dram_tensor("s_w1t0", [FT, P, D], BF16)
    s_w1t1 = nc.dram_tensor("s_w1t1", [FT, P, D], BF16)
    s_w2t0 = nc.dram_tensor("s_w2t0", [FT, P, D], BF16)
    s_w2t1 = nc.dram_tensor("s_w2t1", [FT, P, D], BF16)
    s_wT = {k: nc.dram_tensor("s_" + k + "T", [D, D], BF16)
            for k in ("L1_attn_Wq", "L1_attn_Wk", "L1_attn_Wv", "L1_attn_Wo")}
    s_wbf = {k: nc.dram_tensor("s_" + k + "B", [D, D], BF16)
             for k in ("L0_attn_Wq", "L0_attn_Wk", "L0_attn_Wv", "L0_attn_Wg",
                       "L0_attn_Wo")}
    s_atok = nc.dram_tensor("s_atok", [b_loc, S, D], BF16)

    xres = b.sb("xres", [P, NT * D], BF16)
    ident_bf = b.sb("ident_bf", [P, P], BF16)
    ident_f32 = b.sb("ident_f32", [P, P], F32)
    ones_row = b.sb("ones_row", [1, P], BF16)
    ones_col = b.sb("ones_col", [P, 1], F32)
    b1c1 = b.sb("b1c1", [P, FT], F32)
    b2r1 = b.sb("b2r1", [1, D], BF16)
    xsa_col = b.sb("xsa_col", [P, 1], F32)
    thcol = [b.sb(f"thcol{l}", [P, 1], F32) for l in (0, 1)]
    thcol2 = [b.sb(f"thcol2{l}", [P, 1], F32) for l in (0, 1)]
    thneg = b.sb("thneg", [P, 1], F32)
    wscol = [b.sb(f"wscol{l}", [P, 1], F32) for l in (0, 1)]
    rowtmp = b.sb("rowtmp", [1, 8], F32)
    shift_col = b.sb("shift_col", [P, 1], F32)

    b.tick(b.ve.memset(shift_col[:], -14.0), "ve")
    b.stack.enter_context(nc.allow_non_contiguous_dma(reason="one-time small bias loads"))
    b.load("c0", ident_bf[:], cd["c_ident_bf"].ap())
    b.load("c0", ident_f32[:], cd["c_ident_f32"].ap())
    b.load("c0", ones_row[:], cd["c_ones_row_bf"].ap())
    b.load("c0", ones_col[:], cd["c_ones_col_f32"].ap())
    b.load("c0", b1c1[:], Wd["L1_ffn_b1"].ap().rearrange("(i p) -> p i", p=P))
    b.loadc("cg", b2r1[:], Wd["L1_ffn_b2"].ap().rearrange("(o f) -> o f", o=1))


    ones_row_f = b.sb("ones_row_f", [1, P], F32)
    xsa_sb = b.sb("xsa_sb", [1, 1], F32)
    b.load("c0", ones_row_f[:], cd["c_ones_col_f32"].ap().rearrange("p o -> o p"))
    b.load("c0", xsa_sb[:], Wd["L1_attn_xsa_scale"].ap().rearrange("(o x) -> o x", o=1))
    b.wait_all([b.pe, b.ve, b.se, b.sq, b.gp], ["c0", "cg"])
    with nc.psum_tensor("pxsa", [P, 1], F32) as pxsa:
        b.tick(b.pe.matmul(pxsa[:], ones_row_f[:], xsa_sb[:], start=True, stop=True), "pe")
        b.wait(b.ve, "pe")
        b.tick(b.ve.tensor_scalar_mul(xsa_col[:], pxsa[:], -1.0), "ve")

    cast_sk = ExitStack()
    _prep_cast_all(b, [(Wd[k], s_wbf[k]) for k in s_wbf], cast_sk)

    def wide_a(scratch):
        return lambda i, stg: (scratch.ap()[i, :, :], stg[:])

    def wide_b(scratch):
        return lambda i, stg: (
            scratch.ap()[:, :, i * P:(i + 1) * P].rearrange("j p c -> p j c"),
            stg.ap().rearrange("p (j c) -> p j c", c=P))
    _prep_ternary(b, Wd["L0_ffn_W1"], wide_a(s_w1t0),
                  1e-8, thcol[0], None, "a0", ones_col, ones_row_f, ident_bf, thneg)
    _prep_ternary(b, Wd["L0_ffn_W2"], wide_b(s_w2t0),
                  1e-8, thcol2[0], None, "b0", ones_col, ones_row_f, ident_bf, thneg)
    for k in s_wT:
        _prep_transpose(b, Wd[k], s_wT[k], ident_bf, "t" + k[-2:])
    _prep_ternary(b, Wd["L1_ffn_W1"], wide_a(s_w1t1),
                  1e-5, thcol[1], wscol[0], "a1", ones_col, ones_row_f, ident_bf, thneg)
    _prep_ternary(b, Wd["L1_ffn_W2"], wide_b(s_w2t1),
                  1e-5, thcol2[1], wscol[1], "b1", ones_col, ones_row_f, ident_bf, thneg)
    for nm in list(b.kind):
        if nm.startswith("pst"):
            b.wait(b.sq, nm)
            b.wait(b.gp, nm)
    for e in (b.pe, b.ve, b.se, b.sq, b.gp):
        for nm in ("pe", "ve", "se"):
            b.wait(e, nm)
    cast_sk.close()

    st = _State(b, locals())
    st.uid = 0
    with ExitStack() as sk0:
        st.maskc = maskc = sk0.enter_context(nc.sbuf_tensor("maskc", [P, H * GC], BF16))
        st.m1c = m1c = sk0.enter_context(nc.sbuf_tensor("m1c", [GC, GC], F32))
        st.bg_bc = bg_bc = sk0.enter_context(nc.sbuf_tensor("bg_bc", [P, D], F32))
        st.b1c0 = b1c0 = sk0.enter_context(nc.sbuf_tensor("b1c0", [P, FT], F32))
        st.b2c0 = b2c0 = sk0.enter_context(nc.sbuf_tensor("b2c0", [P, DT], F32))
        st.dup_bf = dup_bf = sk0.enter_context(nc.sbuf_tensor("dup_bf", [64, P], BF16))
        st.sel_bf = sel_bf = sk0.enter_context(nc.sbuf_tensor("sel_bf", [P, P], BF16))
        bg_row = sk0.enter_context(nc.sbuf_tensor("bg_row", [1, D], F32))
        b.phase_barrier()
        with nc.allow_non_contiguous_dma(reason="one-time small bias loads"):
            b.load("c0", maskc[:], cd["c_mask"].ap())
            b.load("c0", m1c[:], cd["c_m1"].ap())
            b.load("c0", b1c0[:], Wd["L0_ffn_b1"].ap().rearrange("(i p) -> p i", p=P))
            b.load("c0", b2c0[:], Wd["L0_ffn_b2"].ap().rearrange("(i p) -> p i", p=P))
            b.load("c0", bg_row[:], Wd["L0_attn_bg"].ap().rearrange("(o f) -> o f", o=1))
            b.load("c0", dup_bf[:], cd["c_dup"].ap())
            b.load("c0", sel_bf[:], cd["c_sel"].ap())
        b.wait_all([b.pe, b.ve, b.se], ["c0"])
        with nc.psum_tensor("pbg2", [P, D], F32) as pbg2:
            b.wait(b.pe, "ve")
            for j in range(D // 512):
                mm = b.pe.matmul(pbg2[:, j * 512:(j + 1) * 512], ones_row_f[:],
                                 bg_row[:, j * 512:(j + 1) * 512], start=True, stop=True)
            b.tick(mm, "pe")
            b.wait(b.ve, "pe")
            b.tick(b.ve.tensor_copy(bg_bc[:], pbg2[:]), "ve")
        for bi in range(b_loc):
            _layer0(st, bi)
        b.phase_barrier()
    if "l1" not in st.skip:
        for bi in range(b_loc):
            _layer1(st, bi)

    for nm, kd in list(b.kind.items()):
        b.wait(b.sq if kd == "h" else b.gp, nm)


class _State:
    def __init__(self, b, env):
        self.b = b
        for k, v in env.items():
            setattr(self, k, v)


def _prep_ternary(b, Wdram, wide_fn, eps, th_col, ws_col, tag,
                  ones_col, ones_row_f, ident_bf, thneg):
    nc = b.nc
    R, C = Wdram.ap().shape
    ntr, ncc = R // P, C // P
    nbuf = 4 if C <= 2048 else 2
    lsk = ExitStack()

    def LS(name, shape, dt):
        return lsk.enter_context(nc.sbuf_tensor(name, shape, dt))
    wnat = [LS(f"wn{tag}{i}", [P, C], F32) for i in range(nbuf)]
    scr = LS(f"wsc{tag}", [P, C], BF16)
    sums = LS(f"wsm{tag}", [P, ntr], F32)
    stg = [LS(f"wst{tag}{i}", [P, C], BF16) for i in (0, 1)]
    row1 = LS(f"wr1{tag}", [1, 1], F32)
    row2 = LS(f"wr2{tag}", [1, 1], F32)
    ve_done = [0] * nbuf
    for i in range(ntr):
        k = i % nbuf
        if i >= nbuf:
            b.sq.wait_ge(b.sem["ve"], ve_done[k])
        v = b.load(f"wn{tag}{k}", wnat[k][:], Wdram.ap()[i * P:(i + 1) * P, :])
        b.wait(b.ve, f"wn{tag}{k}", v)
        ve_done[k] = b.tick(
            b.ve.reduce_sum(sums[:, i:i + 1], wnat[k][:], axis=AX.X,
                            apply_absolute_value=True), "ve")
    with nc.psum_tensor("pws" + tag, [1, ntr], F32) as pws:
        b.wait(b.pe, "ve")
        b.tick(b.pe.matmul(pws[:], ones_col[:], sums[:], start=True, stop=True), "pe")
        b.wait(b.ve, "pe")
        b.ve.reduce_sum(row1[:], pws[:], axis=AX.X)
    b.ve.drain()
    b.ve.tensor_scalar(row2[:], row1[:], 1.0 / (R * C), eps, ALU.mult, ALU.max)
    b.tick(b.ve.drain(), "ve")
    with nc.psum_tensor("pth" + tag, [P, 1], F32) as pth:
        b.wait(b.pe, "ve")
        b.tick(b.pe.matmul(pth[:], ones_row_f[:], row2[:], start=True, stop=True), "pe")
        b.wait(b.ve, "pe")
        if ws_col is not None:
            b.ve.tensor_copy(ws_col[:], pth[:])
        b.ve.tensor_scalar_mul(th_col[:], pth[:], 0.5)
        b.ve.drain()
        b.tick(b.ve.tensor_scalar_mul(thneg[:], th_col[:], -1.0), "ve")
        b.ve.drain()
    with (nc.psum_tensor("ptr" + tag + "0", [P, 512], BF16) as pt0,
          nc.psum_tensor("ptr" + tag + "1", [P, 512], BF16) as pt1):
        pts = [pt0, pt1]
        b.wait(b.pe, "ve")
        b.wait(b.pe, "se")
        dh_done = {}
        ve_evac = {}
        ve_th = [0] * nbuf
        pe_tr_last = 0
        for i in range(ntr):
            k = i % nbuf
            b.sq.wait_ge(b.sem["ve"], ve_th[k] if i >= nbuf else ve_done[k])
            v = b.load(f"wn{tag}{k}", wnat[k][:], Wdram.ap()[i * P:(i + 1) * P, :])
            b.wait(b.ve, f"wn{tag}{k}", v)
            if i >= 1:
                b.wait(b.ve, "pe", pe_tr_last)
            b.ve.tensor_single_scalar(scr[:], wnat[k][:], thneg[:], ALU.is_lt)
            b.ve.drain()
            ve_th[k] = b.tick(
                b.ve.scalar_tensor_tensor(scr[:], wnat[k][:], th_col[:], scr[:],
                                          ALU.is_gt, ALU.subtract), "ve")
            b.wait(b.pe, "ve", ve_th[k])
            ks = i % 2
            last_evac = 0
            nb4 = max(1, ncc // 4)
            for jb in range(nb4):
                kp = jb % 2
                if kp in ve_evac:
                    b.pe.wait_ge(b.sem["ve"], ve_evac[kp])
                for j4 in range(min(4, ncc)):
                    j = jb * 4 + j4
                    mm = b.pe.transpose(pts[kp][:, j4 * P:(j4 + 1) * P],
                                        scr[:, j * P:(j + 1) * P], ident_bf[:])
                pv = b.tick(mm, "pe")
                if jb == nb4 - 1:
                    pe_tr_last = pv
                b.wait(b.ve, "pe", pv)
                if jb == 0 and ks in dh_done:
                    b.ve.wait_ge(b.sem[f"pst{tag}{ks}"], dh_done[ks])
                w4 = min(4, ncc) * P
                ve_evac[kp] = last_evac = b.tick(
                    b.ve.tensor_copy(stg[ks][:, jb * 512:jb * 512 + w4],
                                     pts[kp][:, 0:w4]), "ve")
            b.sq.wait_ge(b.sem["ve"], last_evac)
            if ks in dh_done:
                b.sq.wait_ge(b.sem[f"pst{tag}{ks}"], dh_done[ks])
            dst_ap, src_ap = wide_fn(i, stg[ks])
            dh_done[ks] = b.store(f"pst{tag}{ks}", dst_ap, src_ap)
    for e in (b.sq, b.gp):
        b.wait(e, "ve")
        b.wait(e, "pe")
        b.wait(e, f"pst{tag}0")
        b.wait(e, f"pst{tag}1")
    lsk.close()


def _prep_cast_all(b, pairs, sk):
    nc = b.nc
    bufs = [sk.enter_context(nc.sbuf_tensor(f"cbw{i}", [P, D], BF16))
            for i in range(4)]
    n = 0
    for w, dsc in pairs:
        for i in range(D // P):
            k = n % 4
            if n >= 4:
                b.wait(b.gp, f"pstc{k}")
            v = b.loadc(f"cwl{k}", bufs[k][:], w.ap()[i * P:(i + 1) * P, :])
            b.wait(b.gp, f"cwl{k}", v)
            b.storeg(f"pstc{k}", dsc.ap()[i * P:(i + 1) * P, :], bufs[k][:])
            n += 1


def _prep_transpose(b, Wdram, scratch, ident_bf, tag):
    nc = b.nc
    ntr = D // P
    lsk = ExitStack()
    wnat = [lsk.enter_context(nc.sbuf_tensor(f"tn{tag}{i}", [P, D], BF16))
            for i in (0, 1)]
    stg = [lsk.enter_context(nc.sbuf_tensor(f"tg{tag}{i}", [P, D], BF16))
           for i in (0, 1)]
    pe_done = [0, 0]
    dh_done = {}
    ve_evac = {}
    with (nc.psum_tensor("ptt" + tag + "0", [P, 512], BF16) as pt0,
          nc.psum_tensor("ptt" + tag + "1", [P, 512], BF16) as pt1):
        pts = [pt0, pt1]
        b.wait(b.pe, "ve")
        b.wait(b.pe, "se")
        for i in range(ntr):
            buf = wnat[i % 2]
            if i >= 2:
                b.gp.wait_ge(b.sem["pe"], pe_done[i % 2])
            v = b.loadc(f"tn{tag}{i % 2}", buf[:], Wdram.ap()[i * P:(i + 1) * P, :])
            b.wait(b.pe, f"tn{tag}{i % 2}", v)
            ks = i % 2
            last_evac = 0
            for jb in range(ntr // 4):
                k = jb % 2
                if k in ve_evac:
                    b.pe.wait_ge(b.sem["ve"], ve_evac[k])
                for j4 in range(4):
                    j = jb * 4 + j4
                    mm = b.pe.transpose(pts[k][:, j4 * P:(j4 + 1) * P],
                                        buf[:, j * P:(j + 1) * P], ident_bf[:])
                pv = b.tick(mm, "pe")
                if jb == ntr // 4 - 1:
                    pe_done[i % 2] = pv
                b.wait(b.ve, "pe", pv)
                if jb == 0 and ks in dh_done:
                    b.ve.wait_ge(b.sem[f"pst{tag}{ks}"], dh_done[ks])
                ve_evac[k] = last_evac = b.tick(
                    b.ve.tensor_copy(stg[ks][:, jb * 512:(jb + 1) * 512],
                                     pts[k][:]), "ve")
            b.sq.wait_ge(b.sem["ve"], last_evac)
            if ks in dh_done:
                b.sq.wait_ge(b.sem[f"pst{tag}{ks}"], dh_done[ks])
            dh_done[ks] = b.store(
                f"pst{tag}{ks}",
                scratch.ap()[:, i * P:(i + 1) * P].rearrange(
                    "(j p) c -> p j c", p=P),
                stg[ks].ap().rearrange("p (j c) -> p j c", c=P))
    for e in (b.sq, b.gp):
        b.wait(e, "ve")
        b.wait(e, "pe")
        b.wait(e, f"pst{tag}0")
        b.wait(e, f"pst{tag}1")
    lsk.close()

def _ln_tiles(st, bi, layer):
    b = st.b
    nc = b.nc
    S = st.S
    NTB = st.NTB
    xn_fm = st.cur["xn_fm"]
    lsk = ExitStack()
    xt = [lsk.enter_context(nc.sbuf_tensor(f"xt{layer}{bi}{i}", [P, D], F32))
          for i in (0, 1)]
    sq = lsk.enter_context(nc.sbuf_tensor(f"sq{layer}{bi}", [P, D], F32))
    stat = [lsk.enter_context(nc.sbuf_tensor(f"st{layer}{bi}{i}", [P, 4], F32))
            for i in (0, 1)]
    xn_tok = [lsk.enter_context(nc.sbuf_tensor(f"xk{layer}{bi}{i}", [P, D], BF16))
              for i in (0, 1)]
    with (nc.psum_tensor(f"ptl{layer}{bi}0", [P, 512], BF16) as pt0,
          nc.psum_tensor(f"ptl{layer}{bi}1", [P, 512], BF16) as pt1):
        pts = [pt0, pt1]
        b.wait(b.pe, "ve")
        b.wait(b.pe, "se")
        evac = {}
        done = {}
        sqv = {}

        def emit_stats(t):
            k = t % 2
            if layer == 0:
                if t >= 2:
                    b.sq.wait_ge(b.sem["ve"], done[t - 2][0])
                    b.sq.wait_ge(b.sem["se"], done[t - 2][1])
                else:
                    b.wait(b.sq, "pe")
                    b.wait(b.sq, "ve")
                    b.wait(b.sq, "se")
                v = b.load(f"xld{k}", xt[k][:], st.x_in.ap()[bi, t * P:(t + 1) * P, :])
                b.wait(b.ve, f"xld{k}", v)
            else:
                if t >= 2:
                    b.wait(b.ve, "se", done[t - 2][1])
                b.ve.drain()
                b.ve.tensor_copy(xt[k][:], st.xres[:, (bi * NTB + t) * D:
                                                   (bi * NTB + t + 1) * D])
                b.ve.drain()
            stt = stat[k]
            b.ve.reduce_sum(stt[:, 0:1], xt[k][:], axis=AX.X)
            b.ve.drain()
            b.ve.tensor_scalar_mul(stt[:, 1:2], stt[:, 0:1], -1.0 / D)
            b.ve.drain()
            vs = b.tick(b.ve.scalar_tensor_tensor(sq[:], xt[k][:], stt[:, 1:2],
                                                  xt[k][:], ALU.add, ALU.mult,
                                                  accum_out=stt[:, 0:1]), "ve")
            b.ve.drain()
            b.ve.tensor_scalar(stt[:, 2:3], stt[:, 0:1], 1.0 / D, LN_EPS,
                               ALU.mult, ALU.add)
            b.ve.drain()
            vv = b.tick(b.ve.reciprocal(stt[:, 3:4], stt[:, 2:3]), "ve")
            b.wait(b.se, "ve", vv)
            b.se.drain()
            sv = b.tick(b.se.activation(stt[:, 2:3], stt[:, 3:4], AF.Sqrt), "se")
            b.wait(b.ve, "se", sv)
            b.ve.drain()
            bv = b.tick(b.ve.tensor_mul(stt[:, 0:1], stt[:, 1:2], stt[:, 2:3]), "ve")
            sqv[t] = (vs, bv)

        def emit_apply(t):
            k = t % 2
            stt = stat[k]
            b.wait(b.se, "ve", sqv[t][1])
            b.se.drain()
            sa = b.tick(b.se.activation(xn_tok[k][:], xt[k][:], AF.Identity,
                                        bias=stt[:, 0:1], scale=stt[:, 2:3]), "se")
            if layer == 0:
                b.se.drain()
                sa = b.tick(b.se.activation(
                    st.xres[:, (bi * NTB + t) * D:(bi * NTB + t + 1) * D],
                    xt[k][:], AF.Identity), "se")
            done[t] = (sqv[t][0], sa)
            b.wait(b.pe, "se", sa)
            xn3 = xn_fm.ap().rearrange("p (j s) -> p j s", j=DTC)
            for jb in range(2):
                kp = jb % 2
                if kp in evac:
                    en, val = evac[kp]
                    b.pe.wait_ge(b.sem[en], val)
                for j4 in range(4):
                    j = jb * 4 + j4
                    mm = b.pe.transpose(pts[kp][:, j4 * P:(j4 + 1) * P],
                                        xn_tok[k][:, j * P:(j + 1) * P],
                                        st.ident_bf[:])
                pv = b.tick(mm, "pe")
                dst = xn3[:, jb * 4:(jb + 1) * 4, t * P:(t + 1) * P]
                src = pts[kp].ap().rearrange("p (j c) -> p j c", j=4)
                if jb == 0:
                    b.wait(b.ve, "pe", pv)
                    evac[kp] = ("ve", b.tick(b.ve.tensor_copy(dst, src), "ve"))
                else:
                    b.wait(b.se, "pe", pv)
                    b.se.drain()
                    evac[kp] = ("se", b.tick(
                        b.se.activation(dst, src, AF.Identity), "se"))

        for t in range(NTB):
            emit_stats(t)
            if t >= 1:
                emit_apply(t - 1)
        emit_apply(NTB - 1)
    b.wait(b.pe, "ve")
    b.wait(b.pe, "se")
    lsk.close()


def _layer0(st, bi):
    b = st.b
    nc = b.nc
    S = st.S
    NTB = st.NTB
    NSC = st.NSC
    skip = st.skip
    with ExitStack() as sk:
        st.sk = sk
        st.cur = {}
        st.cur["xn_fm"] = sk.enter_context(
            nc.sbuf_tensor(f"xnfm0{bi}", [P, DTC * S], BF16))
        _ln_tiles(st, bi, 0)
        if "gla" not in skip:
            _gla(st, bi)
        if "ffn0" not in skip:
            _ffn0(st, bi)
    if "l1" in st.skip and bi == st.b_loc - 1:
        _emit_output_from_xres(st)


def _gla(st, bi):
    st.b.phase_barrier()
    b = st.b
    nc = b.nc
    S = st.S
    NSC = st.NSC
    sk = st.sk
    xn_fm = st.cur["xn_fm"]
    lsk = ExitStack()

    def A(name, shape, dt):
        return lsk.enter_context(nc.sbuf_tensor(f"g{name}{bi}", shape, dt))

    NCC = SC // GC
    wall = A("wall", [P, DTC * D], BF16)
    qfm = A("qfm", [P, DTC * SC], BF16)
    kfm = A("kfm", [P, DTC * SC], BF16)
    ktok = A("ktok", [P, NCC * D], BF16)
    vtok = A("vtok", [P, NCC * D], BF16)
    logg = A("logg", [P, NCC * D], F32)
    gs1 = A("gs1", [P, D], F32)
    gs2 = A("gs2", [P, D], F32)
    Fg4 = A("Fg4", [P, NCC * D], BF16)
    G24 = A("G24", [P, NCC * D], BF16)
    pdb4 = A("pdb4", [P, NCC * D], BF16)
    vt4 = A("vt4", [P, NCC * D], BF16)
    state = A("state", [64, D], F32)
    tmps = A("tmps", [64, D], BF16)
    tmp2 = A("tmp2", [P, D], BF16)
    At = A("At", [P, H * GC], BF16)
    otok = A("otok", [P, D], BF16)
    ofm = A("ofm", [P, DTC * SC], BF16)
    attn_s = A("attns", [P, DTC * SC], BF16)
    st.cur.update(dict(wall=wall, qfm=qfm, kfm=kfm, ktok=ktok, vtok=vtok,
                       logg=logg, gs1=gs1, gs2=gs2, Fg4=Fg4, G24=G24,
                       pdb4=pdb4, vt4=vt4, state=state, tmps=tmps, tmp2=tmp2,
                       At=At, otok=otok, ofm=ofm, attn_s=attn_s, wall_rd=0))

    b.ve.drain()
    b.tick(b.ve.memset(state[:], 0.0), "ve")

    for sc in range(NSC):
        t0 = sc * SC
        _proj_fm(st, wall, st.s_wbf["L0_attn_Wq"], xn_fm, qfm, t0, bi)
        _proj_fm(st, wall, st.s_wbf["L0_attn_Wk"], xn_fm, kfm, t0, bi, also_tok=ktok)
        _proj_tok(st, wall, st.s_wbf["L0_attn_Wv"], xn_fm, vtok, t0, bi)
        _proj_tok(st, wall, st.s_wbf["L0_attn_Wg"], xn_fm, None, t0, bi,
                  logg=(logg, gs1, gs2))
        _gla_gates(st, sc)
        for cc in range(NCC):
            _gla_chunk(st, bi, sc, cc)
        _gla_wo(st, bi, sc)
    b.phase_barrier()
    lsk.close()


def _gla_gates(st, sc):
    b = st.b
    nc = b.nc
    c = st.cur
    logg, vtok = c["logg"], c["vtok"]
    Fg4, G24, pdb4, vt4 = c["Fg4"], c["G24"], c["pdb4"], c["vt4"]
    NCC = SC // GC
    with (nc.psum_tensor(_psname(st, "psm"), [P, D], F32) as pm0,
          nc.psum_tensor(_psname(st, "psm"), [P, D], F32) as pm1,
          nc.psum_tensor(_psname(st, "pdb"), [P, D], F32) as pdbp):
        pmp = [pm0, pm1]
        b.wait(b.se, "ve")
        b.wait(b.se, "pe")
        b.wait(b.pe, "se")
        b.wait(b.pe, "ve")
        exps_done = {}
        pdb_ev = None
        for cc in range(NCC):
            k = cc % 2
            if k in exps_done:
                b.pe.wait_ge(b.sem["se"], exps_done[k])
            for half in range(2):
                sl = slice(half * 512, (half + 1) * 512)
                mm = b.pe.matmul(
                    pmp[k][:, sl], st.m1c[:],
                    logg[:, cc * D + half * 512:cc * D + (half + 1) * 512],
                    start=True, stop=True)
            pv = b.tick(mm, "pe")
            b.wait(b.se, "pe", pv)
            b.se.drain()
            b.se.activation(Fg4[:, cc * D:(cc + 1) * D], pmp[k][:], AF.Exp)
            sg = b.tick(b.se.activation(G24[:, cc * D:(cc + 1) * D], pmp[k][:],
                                        AF.Exp, scale=-1.0), "se")
            exps_done[k] = sg
            b.wait(b.pe, "se", sg)
            if pdb_ev is not None:
                b.pe.wait_ge(b.sem["ve"], pdb_ev)
            for half in range(2):
                sl = slice(half * 512, (half + 1) * 512)
                mm = b.pe.matmul(
                    pdbp[:, sl], st.sel_bf[:],
                    G24[:, cc * D + half * 512:cc * D + (half + 1) * 512],
                    start=True, stop=True)
            pv = b.tick(mm, "pe")
            b.wait(b.ve, "pe", pv)
            b.ve.drain()
            b.ve.tensor_copy(pdb4[:, cc * D:(cc + 1) * D], pdbp[:])
            b.ve.drain()
            pdb_ev = b.tick(b.ve.tensor_mul(vt4[:, cc * D:(cc + 1) * D],
                                            vtok[:, cc * D:(cc + 1) * D],
                                            Fg4[:, cc * D:(cc + 1) * D]), "ve")


def _psname(st, base):
    st.uid += 1
    return f"{base}{st.uid}"


def _proj_fm(st, wall, Wdram, xn_fm, out_fm, t0, bi, also_tok=None):
    b = st.b
    nc = b.nc
    S = st.S
    b.wait(b.gp, "pe", st.cur["wall_rd"])
    vlast = b.loadc("wall", wall.ap().rearrange("p (jd e) -> p jd e", jd=DTC),
                    Wdram.ap().rearrange("(jd p) e -> p jd e", p=P))
    b.wait(b.pe, "wall", vlast)
    with (nc.psum_tensor(_psname(st, "pq"), [P, 2 * SC], F32) as p0,
          nc.psum_tensor(_psname(st, "pq"), [P, 2 * SC], F32) as p1):
        pp = [p0, p1]
        b.wait(b.pe, "ve")
        b.wait(b.pe, "se")
        ve_evac = {}
        for jq in range(0, DTC, 2):
            k = (jq // 2) % 2
            if k in ve_evac:
                b.pe.wait_ge(b.sem["ve"], ve_evac[k])
            for j2 in range(2):
                for jd in range(DTC):
                    mm = b.pe.matmul(
                        pp[k][:, j2 * SC:(j2 + 1) * SC],
                        wall[:, jd * D + (jq + j2) * P:jd * D + (jq + j2 + 1) * P],
                        xn_fm[:, jd * S + t0:jd * S + t0 + SC],
                        start=(jd == 0), stop=(jd == DTC - 1))
            pv = b.tick(mm, "pe")
            b.wait(b.ve, "pe", pv)
            ve_evac[k] = b.tick(
                b.ve.tensor_copy(out_fm[:, jq * SC:(jq + 2) * SC], pp[k][:]), "ve")
    st.cur["wall_rd"] = b.cnt["pe"]
    if also_tok is not None:
        _tok_mms(st, wall, xn_fm, also_tok, t0, None)
        st.cur["wall_rd"] = b.cnt["pe"]


def _proj_tok(st, wall, Wdram, xn_fm, out_tok, t0, bi, logg=None):
    b = st.b
    b.wait(b.gp, "pe", st.cur["wall_rd"])
    vlast = b.loadc("wall", wall.ap().rearrange("p (jd e) -> p jd e", jd=DTC),
                    Wdram.ap().rearrange("(jd p) e -> p jd e", p=P))
    b.wait(b.pe, "wall", vlast)
    _tok_mms(st, wall, xn_fm, out_tok, t0, logg)
    st.cur["wall_rd"] = b.cnt["pe"]


def _tok_mms(st, wall, xn_fm, out_tok, t0, logg):
    b = st.b
    nc = b.nc
    S = st.S
    NCC = SC // P
    with (nc.psum_tensor(_psname(st, "pk"), [P, D], F32) as p0,
          nc.psum_tensor(_psname(st, "pk"), [P, D], F32) as p1):
        pp = [p0, p1]
        b.wait(b.pe, "ve")
        b.wait(b.pe, "se")
        cons = {}
        for tt in range(NCC):
            k = tt % 2
            if k in cons:
                b.pe.wait_ge(b.sem[cons[k][0]], cons[k][1])
            for jd in range(DTC):
                lhs = xn_fm[:, jd * S + t0 + tt * P:jd * S + t0 + (tt + 1) * P]
                b.pe.matmul(pp[k][:, 0:512], lhs, wall[:, jd * D:jd * D + 512],
                            start=(jd == 0), stop=(jd == DTC - 1))
                mm = b.pe.matmul(pp[k][:, 512:D], lhs, wall[:, jd * D + 512:(jd + 1) * D],
                                 start=(jd == 0), stop=(jd == DTC - 1))
            pv = b.tick(mm, "pe")
            if logg is None:
                b.wait(b.ve, "pe", pv)
                cons[k] = ("ve", b.tick(
                    b.ve.tensor_copy(out_tok[:, tt * D:(tt + 1) * D], pp[k][:]), "ve"))
            else:
                lg, gs1, gs2 = logg
                b.wait(b.ve, "pe", pv)
                b.wait(b.ve, "se")
                b.ve.drain()
                vv = b.tick(b.ve.tensor_add(gs1[:], pp[k][:], st.bg_bc[:]), "ve")
                cons[k] = ("ve", vv)
                b.wait(b.se, "ve", vv)
                b.se.activation(gs2[:], gs1[:], AF.Sigmoid)
                b.se.drain()
                b.tick(b.se.activation(lg[:, tt * D:(tt + 1) * D], gs2[:], AF.Ln), "se")


def _gla_chunk(st, bi, sc, cc):
    b = st.b
    nc = b.nc
    c = st.cur
    qfm, kfm, ktok = c["qfm"], c["kfm"], c["ktok"]
    G24, pdb4, vt4 = c["G24"], c["pdb4"], c["vt4"]
    state, tmps, tmp2 = c["state"], c["tmps"], c["tmp2"]
    gs1, gs2 = c["gs1"], c["gs2"]
    At, otok, ofm = c["At"], c["otok"], c["ofm"]
    tq0 = cc * P
    vt = vt4[:, cc * D:(cc + 1) * D]
    G2 = G24[:, cc * D:(cc + 1) * D]
    pdbl = pdb4[0:64, cc * D:(cc + 1) * D]

    b.ve.drain()
    tv = b.tick(b.ve.tensor_mul(tmps[:], state[:], pdbl), "ve")
    with nc.psum_tensor(_psname(st, "pdu"), [P, D], F32) as pdup:
        b.wait(b.pe, "ve", tv)
        for half in range(2):
            sl = slice(half * 512, (half + 1) * 512)
            mm = b.pe.matmul(pdup[:, sl], st.dup_bf[:], tmps[:, sl],
                             start=True, stop=True)
        pv = b.tick(mm, "pe")
        b.wait(b.ve, "pe", pv)
        b.ve.drain()
        t2 = b.tick(b.ve.tensor_copy(tmp2[:], pdup[:]), "ve")
    with nc.psum_tensor(_psname(st, "pst"), [P, H * GC], F32) as pst:
        b.pe.wait_ge(b.sem["ve"], t2)
        for h in range(H):
            cp = (h % 2) * (H // 2) * GC + (h // 2) * GC
            kf = kfm[(h % 2) * 64:(h % 2) * 64 + 64,
                     (h // 2) * SC + tq0:(h // 2) * SC + tq0 + P]
            qf = qfm[(h % 2) * 64:(h % 2) * 64 + 64,
                     (h // 2) * SC + tq0:(h // 2) * SC + tq0 + P]
            mm = b.pe.matmul(pst[:, cp:cp + GC], kf, qf,
                             start=(h in (0, 1, 8, 9)), stop=(h in (6, 7, 14, 15)))
        pv = b.tick(mm, "pe")
        b.wait(b.ve, "pe", pv)
        b.ve.drain()
        av = b.tick(b.ve.tensor_mul(At[:], pst[:], st.maskc[:]), "ve")
    with (nc.psum_tensor(_psname(st, "poa"), [P, D], F32) as poa,
          nc.psum_tensor(_psname(st, "pob"), [P, D], F32) as pob):
        b.wait(b.pe, "ve", av)
        for h in range(H):
            mm = b.pe.matmul(poa[:, h * HD:(h + 1) * HD],
                             At[:, ((h % 2) * (H // 2) + h // 2) * GC:
                                ((h % 2) * (H // 2) + h // 2 + 1) * GC],
                             vt[:, h * HD:(h + 1) * HD],
                             start=(h % 8 == 0), stop=(h % 8 == 7))
        for h in range(H):
            ip = (h % 2) * 512 + (h // 2) * HD
            qf = qfm[(h % 2) * 64:(h % 2) * 64 + 64,
                     (h // 2) * SC + tq0:(h // 2) * SC + tq0 + P]
            mm2 = b.pe.matmul(pob[:, ip:ip + HD], qf,
                              tmp2[(h % 2) * 64:(h % 2) * 64 + 64,
                                   h * HD:(h + 1) * HD],
                              start=(h in (0, 1)), stop=(h in (14, 15)))
        pv = b.tick(mm2, "pe")
        b.wait(b.ve, "pe", pv)
        b.ve.drain()
        pob3 = pob.ap().rearrange("p (par h2 e) -> p par h2 e", par=2, e=HD)
        gsc3 = gs2.ap().rearrange("p (h2 par e) -> p par h2 e", par=2, e=HD)
        b.ve.tensor_copy(gsc3, pob3)
        b.ve.drain()
        b.ve.tensor_add(gs1[:], poa[:], gs2[:])
        b.ve.drain()
        b.tick(b.ve.tensor_mul(otok[:], gs1[:], G2[:]), "ve")
        with nc.psum_tensor(_psname(st, "pu"), [64, D], F32) as pu:
            for half in range(2):
                sl = slice(half * 512, (half + 1) * 512)
                b.pe.matmul(pu[:, sl], st.ident_bf[0:64, 0:64], tmps[:, sl],
                            start=True, stop=False)
            for h in range(H):
                mm = b.pe.matmul(pu[:, h * HD:(h + 1) * HD],
                                 ktok[:, cc * D + h * HD:cc * D + (h + 1) * HD],
                                 vt[:, h * HD:(h + 1) * HD],
                                 start=False, stop=(h % 8 == 7))
            pv = b.tick(mm, "pe")
            b.wait(b.ve, "pe", pv)
            b.ve.drain()
            b.tick(b.ve.tensor_mul(state[:], pu[:], pdbl), "ve")
    with (nc.psum_tensor(_psname(st, "ptg"), [P, 512], BF16) as pt0,
          nc.psum_tensor(_psname(st, "ptg"), [P, 512], BF16) as pt1):
        pts = [pt0, pt1]
        b.wait(b.pe, "ve")
        ve_evac = {}
        ofm3 = ofm.ap().rearrange("p (j s) -> p j s", j=DTC)
        for jb in range(2):
            k = jb % 2
            if k in ve_evac:
                b.pe.wait_ge(b.sem["ve"], ve_evac[k])
            for j4 in range(4):
                j = jb * 4 + j4
                mm = b.pe.transpose(pts[k][:, j4 * P:(j4 + 1) * P],
                                    otok[:, j * P:(j + 1) * P], st.ident_bf[:])
            pv = b.tick(mm, "pe")
            b.wait(b.ve, "pe", pv)
            ve_evac[k] = b.tick(b.ve.tensor_copy(
                ofm3[:, jb * 4:(jb + 1) * 4, tq0:tq0 + P],
                pts[k].ap().rearrange("p (j c) -> p j c", j=4)), "ve")


def _gla_wo(st, bi, sc):
    b = st.b
    nc = b.nc
    c = st.cur
    S = st.S
    NTB = st.NTB
    wall, ofm, attn_s = c["wall"], c["ofm"], c["attn_s"]
    b.wait(b.gp, "pe", c["wall_rd"])
    vlast = b.loadc("wall", wall.ap().rearrange("p (jd e) -> p jd e", jd=DTC),
                    st.s_wbf["L0_attn_Wo"].ap().rearrange("(jd p) e -> p jd e", p=P))
    b.wait(b.pe, "wall", vlast)
    b.wait(b.pe, "ve")
    b.wait(b.pe, "se")
    with (nc.psum_tensor(_psname(st, "pwo"), [P, 2 * SC], F32) as p0,
          nc.psum_tensor(_psname(st, "pwo"), [P, 2 * SC], F32) as p1):
        pp = [p0, p1]
        ve_evac = {}
        for jo in range(0, DTC, 2):
            k = (jo // 2) % 2
            if k in ve_evac:
                b.pe.wait_ge(b.sem["ve"], ve_evac[k])
            for j2 in range(2):
                for je in range(DTC):
                    mm = b.pe.matmul(
                        pp[k][:, j2 * SC:(j2 + 1) * SC],
                        wall[:, je * D + (jo + j2) * P:je * D + (jo + j2 + 1) * P],
                        ofm[:, je * SC:(je + 1) * SC],
                        start=(je == 0), stop=(je == DTC - 1))
            pv = b.tick(mm, "pe")
            b.wait(b.ve, "pe", pv)
            ve_evac[k] = b.tick(
                b.ve.tensor_copy(attn_s[:, jo * SC:(jo + 2) * SC], pp[k][:]), "ve")
        c["wall_rd"] = b.cnt["pe"]
    with nc.psum_tensor(_psname(st, "prt"), [P, D], BF16) as prt:
        for tt in range(SC // P):
            b.wait(b.pe, "ve")
            for jo in range(DTC):
                mm = b.pe.transpose(prt[:, jo * P:(jo + 1) * P],
                                    attn_s[:, jo * SC + tt * P:jo * SC + (tt + 1) * P],
                                    st.ident_bf[:])
            pv = b.tick(mm, "pe")
            b.wait(b.ve, "pe", pv)
            gt = bi * NTB + sc * (SC // P) + tt
            b.ve.drain()
            b.tick(b.ve.tensor_add(st.xres[:, gt * D:(gt + 1) * D],
                                   st.xres[:, gt * D:(gt + 1) * D], prt[:]), "ve")


def _ffn0(st, bi):
    st.b.phase_barrier()
    b = st.b
    nc = b.nc
    S = st.S
    NTB = st.NTB
    sk = st.sk
    xn_fm = st.cur["xn_fm"]

    def A(name, shape, dt):
        return sk.enter_context(nc.sbuf_tensor(f"f0{name}{bi}", shape, dt))

    wf1 = [A(f"w1_{i}", [P, 2 * D], BF16) for i in (0, 1)]
    wf2 = [A(f"w2_{i}", [P, 2 * D], BF16) for i in (0, 1)]
    stash = A("stash", [P, FT_ * 512], BF16)
    fstage = [A(f"fst{i}", [P, 2048], BF16) for i in (0, 1)]
    prev = {}
    prev2 = {}
    for ob in range(S // 512):
        t0 = ob * 512
        with (nc.psum_tensor(_psname(st, "ph"), [P, 512], F32) as ph0,
              nc.psum_tensor(_psname(st, "ph"), [P, 512], F32) as ph1):
            php = [ph0, ph1]
            b.wait(b.pe, "ve")
            b.wait(b.pe, "se")
            prev_se = {}
            for jf2 in range(FT_ // 2):
                k = jf2 % 2
                if k in prev:
                    b.sq.wait_ge(b.sem["pe"], prev[k][1])
                vw = b.load(f"wf1{k}",
                            wf1[k].ap().rearrange("p (j e) -> p j e", j=2),
                            st.s_w1t0.ap().rearrange(
                                "f p e -> p f e")[:, 2 * jf2:2 * jf2 + 2, :])
                b.wait(b.pe, f"wf1{k}", vw)
                for j2 in range(2):
                    jf = 2 * jf2 + j2
                    kp = jf % 2
                    if kp in prev_se:
                        b.pe.wait_ge(b.sem["se"], prev_se[kp])
                    for jd in range(DTC):
                        mm = b.pe.matmul(
                            php[kp][:], wf1[k][:, j2 * D + jd * P:j2 * D + (jd + 1) * P],
                            xn_fm[:, jd * S + t0:jd * S + t0 + 512],
                            start=(jd == 0), stop=(jd == DTC - 1))
                    pv = b.tick(mm, "pe")
                    b.wait(b.se, "pe", pv)
                    sv = b.tick(b.se.activation(
                        stash[:, jf * 512:(jf + 1) * 512], php[kp][:],
                        AF.Gelu, bias=st.b1c0[:, jf:jf + 1]), "se")
                    prev_se[kp] = sv
                prev[k] = (sv, pv)
        with (nc.psum_tensor(_psname(st, "pob"), [P, 2048], F32) as pob0,
              nc.psum_tensor(_psname(st, "pob"), [P, 2048], F32) as pob1):
            pobs = [pob0, pob1]
            b.wait(b.pe, "se")
            b.wait(b.pe, "ve")
            for jf2 in range(FT_ // 2):
                k = jf2 % 2
                if k in prev2:
                    b.sq.wait_ge(b.sem["pe"], prev2[k])
                vw = b.load(f"wf2{k}",
                            wf2[k].ap().rearrange("p (j e) -> p j e", j=2),
                            st.s_w2t0.ap().rearrange(
                                "f p e -> p f e")[:, 2 * jf2:2 * jf2 + 2, :])
                b.wait(b.pe, f"wf2{k}", vw)
                for j2 in range(2):
                    jf = 2 * jf2 + j2
                    for sub in range(2):
                        for jd in range(DTC):
                            mm = b.pe.matmul(
                                pobs[sub][:, jd * 256:(jd + 1) * 256],
                                wf2[k][:, j2 * D + jd * P:j2 * D + (jd + 1) * P],
                                stash[:, jf * 512 + sub * 256:jf * 512 + (sub + 1) * 256],
                                start=(jf == 0 and jd % 2 == 0),
                                stop=(jf == FT_ - 1 and jd % 2 == 1))
                prev2[k] = b.tick(mm, "pe")
            b.wait(b.se, "pe")
            for sub in range(2):
                for jd in range(DTC):
                    sv = b.tick(b.se.activation(
                        fstage[sub][:, jd * 256:(jd + 1) * 256],
                        pobs[sub][:, jd * 256:(jd + 1) * 256],
                        AF.Identity, bias=st.b2c0[:, jd:jd + 1]), "se")
        with nc.psum_tensor(_psname(st, "prf"), [P, D], BF16) as prf:
            b.wait(b.pe, "se", sv)
            for sub in range(2):
                for tt in range(2):
                    b.wait(b.pe, "ve")
                    for jd in range(DTC):
                        mm = b.pe.transpose(
                            prf[:, jd * P:(jd + 1) * P],
                            fstage[sub][:, jd * 256 + tt * P:jd * 256 + (tt + 1) * P],
                            st.ident_bf[:])
                    pv = b.tick(mm, "pe")
                    b.wait(b.ve, "pe", pv)
                    gt = bi * NTB + (t0 + sub * 256 + tt * P) // P
                    b.ve.drain()
                    b.tick(b.ve.tensor_add(st.xres[:, gt * D:(gt + 1) * D],
                                           st.xres[:, gt * D:(gt + 1) * D], prf[:]), "ve")


def _emit_output_from_xres(st):
    b = st.b
    nc = b.nc
    NT = st.NT
    with ExitStack() as sk:
        ot = sk.enter_context(nc.sbuf_tensor("odump", [P, D], F32))
        for t in range(NT):
            if t:
                b.wait(b.ve, "out")
            b.ve.drain()
            vv = b.tick(b.ve.tensor_copy(
                ot[:], st.xres[:, t * D:(t + 1) * D]), "ve")
            b.sq.wait_ge(b.sem["ve"], vv)
            bi, tt = divmod(t, st.NTB)
            b.store("out", st.x_out.ap()[bi, tt * P:(tt + 1) * P, :], ot[:])


def _layer1(st, bi):
    b = st.b
    nc = b.nc
    S = st.S
    with ExitStack() as sk:
        st.sk = sk
        st.cur = {}
        with ExitStack() as skA:
            st.cur["xn_fm"] = skA.enter_context(
                nc.sbuf_tensor(f"xnfm1{bi}", [P, DTC * S], BF16))
            saved, st.sk = st.sk, skA
            _ln_tiles(st, bi, 1)
            if "attn1" not in st.skip:
                _attn1_proj(st, bi)
                _attn1_core(st, bi)
            st.sk = saved
        with ExitStack() as skB:
            st.cur["xq_fm"] = skB.enter_context(
                nc.sbuf_tensor(f"xqfm{bi}", [P, DTC * S], BF16))
            saved, st.sk = st.sk, skB
            if "ffn1" not in st.skip:
                _ffn1(st, bi)
            else:
                _merge_no_ffn(st, bi)
            st.sk = saved


def _merge_no_ffn(st, bi):
    b = st.b
    b.phase_barrier()
    nc = b.nc
    NTB = st.NTB
    with ExitStack() as sk:
        o2l = sk.enter_context(nc.sbuf_tensor(f"no2l{bi}", [P, D], BF16))
        o2f = sk.enter_context(nc.sbuf_tensor(f"no2f{bi}", [P, D], BF16))
        wo = [sk.enter_context(nc.sbuf_tensor(f"nwo{bi}{i}", [P, D], BF16))
              for i in (0, 1)]
        outf = sk.enter_context(nc.sbuf_tensor(f"nouf{bi}", [P, D], F32))
        prevw = {}
        for t in range(NTB):
            if "atr" in b.cnt:
                b.wait(b.sq, "pe")
            b.wait(b.sq, "atw")
            vf = b.load("atr", o2l[:], st.s_atok.ap()[bi, t * P:(t + 1) * P, :])
            with (nc.psum_tensor(_psname(st, "nmt"), [P, D], BF16) as ptr,
                  nc.psum_tensor(_psname(st, "nmb"), [P, D], F32) as pbs):
                b.wait(b.pe, "ve")
                b.wait(b.pe, "atr", vf)
                for j in range(DTC):
                    mm = b.pe.transpose(ptr[:, j * P:(j + 1) * P],
                                        o2l[:, j * P:(j + 1) * P], st.ident_bf[:])
                pv = b.tick(mm, "pe")
                b.wait(b.ve, "pe", pv)
                b.ve.drain()
                vv = b.tick(b.ve.tensor_copy(o2f[:], ptr[:]), "ve")
                b.wait(b.pe, "ve", vv)
                for je in range(DTC):
                    k = je % 2
                    if k in prevw:
                        b.sq.wait_ge(b.sem["pe"], prevw[k])
                    vw = b.load(f"wo{k}", wo[k][:],
                                st.s_wT["L1_attn_Wo"].ap()[je * P:(je + 1) * P, :])
                    b.wait(b.pe, f"wo{k}", vw)
                    for half in range(2):
                        mm = b.pe.matmul(pbs[:, half * 512:(half + 1) * 512],
                                         o2f[:, je * P:(je + 1) * P],
                                         wo[k][:, half * 512:(half + 1) * 512],
                                         start=(je == 0), stop=(je == DTC - 1))
                    prevw[k] = b.tick(mm, "pe")
                b.wait(b.ve, "pe")
                if "out" in b.cnt:
                    b.wait(b.ve, "out")
                b.ve.drain()
                gtx = bi * NTB + t
                vv = b.tick(b.ve.tensor_add(outf[:], pbs[:],
                                            st.xres[:, gtx * D:(gtx + 1) * D]), "ve")
                b.sq.wait_ge(b.sem["ve"], vv)
                b.store("out", st.x_out.ap()[bi, t * P:(t + 1) * P, :], outf[:])


def _ffn1(st, bi):
    st.b.phase_barrier()
    b = st.b
    nc = b.nc
    S = st.S
    NTB = st.NTB
    sk = st.sk

    def A(name, shape, dt):
        return sk.enter_context(nc.sbuf_tensor(f"f1{name}{bi}", shape, dt))

    xq_fm = st.cur["xq_fm"]
    xt = A("xt", [P, D], F32)
    sq = A("sq", [P, D], F32)
    stat = A("st", [P, 4], F32)
    rbf = A("rbf", [P, D], BF16)
    rf32 = A("rf32", [P, D], F32) if FP64_SIM else sq
    q1s = A("q1s", [P, D], BF16)
    with (nc.psum_tensor(_psname(st, "pfl"), [P, 512], BF16) as pt0,
          nc.psum_tensor(_psname(st, "pfl"), [P, 512], BF16) as pt1):
        pts = [pt0, pt1]
        b.wait(b.pe, "ve")
        b.wait(b.pe, "se")
        ve_evac = {}
        for t in range(NTB):
            gt = bi * NTB + t
            b.ve.drain()
            b.wait(b.ve, "se")
            b.ve.tensor_copy(xt[:], st.xres[:, gt * D:(gt + 1) * D])
            b.ve.drain()
            b.ve.reduce_sum(stat[:, 0:1], xt[:], axis=AX.X)
            b.ve.drain()
            b.ve.tensor_scalar_mul(stat[:, 1:2], stat[:, 0:1], -1.0 / D)
            b.ve.drain()
            b.ve.scalar_tensor_tensor(sq[:], xt[:], stat[:, 1:2], xt[:],
                                      ALU.add, ALU.mult, accum_out=stat[:, 0:1])
            b.ve.drain()
            b.ve.tensor_scalar(stat[:, 2:3], stat[:, 0:1], 1.0 / D, LN_EPS,
                               ALU.mult, ALU.add)
            b.ve.drain()
            vv = b.tick(b.ve.reciprocal(stat[:, 3:4], stat[:, 2:3]), "ve")
            b.wait(b.se, "ve", vv)
            sv = b.tick(b.se.activation(stat[:, 2:3], stat[:, 3:4], AF.Sqrt), "se")
            b.wait(b.ve, "se", sv)
            b.ve.tensor_scalar(sq[:], xt[:], stat[:, 1:2], stat[:, 2:3],
                               ALU.add, ALU.mult)
            b.ve.drain()
            b.ve.reduce_max(stat[:, 0:1], sq[:], axis=AX.X, apply_absolute_value=True)
            b.ve.drain()
            b.ve.tensor_single_scalar(stat[:, 0:1], stat[:, 0:1], 1e-5, ALU.max)
            b.ve.drain()
            b.ve.reciprocal(stat[:, 1:2], stat[:, 0:1])
            b.ve.drain()
            b.ve.tensor_scalar_mul(stat[:, 1:2], stat[:, 1:2], 127.0)
            b.ve.drain()
            b.ve.tensor_scalar(stat[:, 2:3], stat[:, 0:1], st.wscol[0][:],
                               1.0 / 127.0, ALU.mult, ALU.mult)
            b.ve.drain()
            b.tick(b.ve.tensor_scalar(sq[:], sq[:], stat[:, 1:2], None, ALU.mult), "ve")
            b.emit_round(sq[:], sq[:], xt, rbf, rf32, D)
            b.ve.drain()
            b.ve.tensor_scalar(q1s[:], sq[:], stat[:, 2:3], None, ALU.mult)
            vq = b.tick(b.ve.drain(), "ve")
            b.wait(b.pe, "ve", vq)
            xq3 = xq_fm.ap().rearrange("p (j s) -> p j s", j=DTC)
            for jb in range(2):
                k = jb % 2
                if k in ve_evac:
                    b.pe.wait_ge(b.sem["ve"], ve_evac[k])
                for j4 in range(4):
                    j = jb * 4 + j4
                    mm = b.pe.transpose(pts[k][:, j4 * P:(j4 + 1) * P],
                                        q1s[:, j * P:(j + 1) * P], st.ident_bf[:])
                pv = b.tick(mm, "pe")
                b.wait(b.ve, "pe", pv)
                ve_evac[k] = b.tick(b.ve.tensor_copy(
                    xq3[:, jb * 4:(jb + 1) * 4, t * P:(t + 1) * P],
                    pts[k].ap().rearrange("p (j c) -> p j c", j=4)), "ve")
    b.wait(b.pe, "ve")

    wfw = [A(f"wf_{i}", [P, 2 * D], BF16) for i in (0, 1)]
    hfm2 = [A(f"hfm{i}", [P, 512], BF16) for i in (0, 1)]
    ght = A("ght", [P, 4 * FF], BF16)
    scf = A("scf", [P, 2048], F32)
    scs = A("scs", [P, 2048], F32)
    scb = A("scb", [P, 2048], BF16)
    scv = A("scv", [P, 2048], F32) if FP64_SIM else scs
    sca = A("sca", [P, 2048], F32)
    q2s = A("q2s", [P, FF], BF16)
    qst = A("qst", [P, 4], F32)
    o2l = A("o2l", [P, D], BF16)
    o2f4 = A("o2f4", [P, 4 * D], BF16)
    prev = {}
    prevw = {}
    prevo = {}
    for ob in range(S // 512):
        t0 = ob * 512
        with (nc.psum_tensor(_psname(st, "p1h"), [P, 512], F32) as ph0,
              nc.psum_tensor(_psname(st, "p1h"), [P, 512], F32) as ph1,
              nc.psum_tensor(_psname(st, "p1t"), [P, 512], BF16) as pt0,
              nc.psum_tensor(_psname(st, "p1t"), [P, 512], BF16) as pt1):
            php = [ph0, ph1]
            ptp = [pt0, pt1]
            b.wait(b.pe, "ve")
            b.wait(b.pe, "se")
            tr_evac = {}
            ght3 = ght.ap().rearrange("p (t f) -> p t f", t=4)

            def emit_tr(jfp, svp):
                b.wait(b.pe, "se", svp)
                kk = jfp % 2
                if kk in tr_evac:
                    b.pe.wait_ge(b.sem["ve"], tr_evac[kk])
                for tt in range(4):
                    mm = b.pe.transpose(
                        ptp[kk][:, tt * P:(tt + 1) * P],
                        hfm2[jfp % 2][:, tt * P:(tt + 1) * P], st.ident_bf[:])
                pv2 = b.tick(mm, "pe")
                b.wait(b.ve, "pe", pv2)
                tr_evac[kk] = b.tick(b.ve.tensor_copy(
                    ght3[:, :, jfp * P:(jfp + 1) * P],
                    ptp[kk].ap().rearrange("p (t c) -> p t c", t=4)), "ve")
            pend = None
            prev_se = {}
            for jf2 in range(FT_ // 2):
                k = jf2 % 2
                if k in prev:
                    b.sq.wait_ge(b.sem["pe"], prev[k][1])
                if k in prevo:
                    b.sq.wait_ge(b.sem["pe"], prevo[k])
                vw = b.load(f"wf1{k}",
                            wfw[k].ap().rearrange("p (j e) -> p j e", j=2),
                            st.s_w1t1.ap().rearrange(
                                "f p e -> p f e")[:, 2 * jf2:2 * jf2 + 2, :])
                b.wait(b.pe, f"wf1{k}", vw)
                for j2 in range(2):
                    jf = 2 * jf2 + j2
                    kp = jf % 2
                    if kp in prev_se:
                        b.pe.wait_ge(b.sem["se"], prev_se[kp])
                    for jd in range(DTC):
                        mm = b.pe.matmul(
                            php[kp][:], wfw[k][:, j2 * D + jd * P:j2 * D + (jd + 1) * P],
                            xq_fm[:, jd * S + t0:jd * S + t0 + 512],
                            start=(jd == 0), stop=(jd == DTC - 1))
                    pv = b.tick(mm, "pe")
                    b.wait(b.se, "pe", pv)
                    sv = b.tick(b.se.activation(hfm2[kp][:], php[kp][:], AF.Gelu,
                                                bias=st.b1c1[:, jf:jf + 1]), "se")
                    prev_se[kp] = sv
                    if pend is not None:
                        emit_tr(*pend)
                    pend = (jf, sv)
                prev[k] = (sv, pv)
            emit_tr(*pend)
        for tt in range(4):
            gt_sl = ght[:, tt * FF:(tt + 1) * FF]
            b.ve.drain()
            b.wait(b.ve, "se")
            b.ve.reduce_max(qst[:, 0:1], gt_sl[:, 0:2048], axis=AX.X,
                            apply_absolute_value=True)
            b.ve.drain()
            b.ve.reduce_max(qst[:, 1:2], gt_sl[:, 2048:], axis=AX.X,
                            apply_absolute_value=True)
            b.ve.drain()
            b.ve.tensor_tensor(qst[:, 0:1], qst[:, 0:1], qst[:, 1:2], ALU.max)
            b.ve.drain()
            b.ve.tensor_single_scalar(qst[:, 0:1], qst[:, 0:1], 1e-5, ALU.max)
            b.ve.drain()
            b.ve.reciprocal(qst[:, 1:2], qst[:, 0:1])
            b.ve.drain()
            b.ve.tensor_scalar_mul(qst[:, 1:2], qst[:, 1:2], 127.0)
            b.ve.drain()
            b.ve.tensor_scalar(qst[:, 2:3], qst[:, 0:1], st.wscol[1][:],
                               1.0 / 127.0, ALU.mult, ALU.mult)
            b.tick(b.ve.drain(), "ve")
            for hh in range(2):
                hsl = slice(hh * 2048, (hh + 1) * 2048)
                b.wait(b.se, "ve")
                b.wait(b.se, "pe")
                if "out" in b.cnt:
                    b.wait(b.se, "out")
                b.tick(b.se.activation(sca[:], gt_sl[:, hsl], AF.Identity,
                                       scale=qst[:, 1:2]), "se")
                b.se.drain()
                b.wait(b.ve, "se")
                b.emit_round(scf[:], sca[:], scs, scb, scv, 2048)
                b.wait(b.se, "ve")
                qv = b.tick(b.se.activation(q2s[:, hsl], scf[:], AF.Identity,
                                            scale=qst[:, 2:3]), "se")
            with (nc.psum_tensor(_psname(st, "p1u"), [P, 512], BF16) as pu0,
                  nc.psum_tensor(_psname(st, "p1u"), [P, 512], BF16) as pu1):
                pup = [pu0, pu1]
                b.wait(b.pe, "se", qv)
                ev = {}
                for jb in range(FT_ // 4):
                    k = jb % 2
                    if k in ev:
                        b.pe.wait_ge(b.sem["ve"], ev[k])
                    for j4 in range(4):
                        jf = jb * 4 + j4
                        mm = b.pe.transpose(pup[k][:, j4 * P:(j4 + 1) * P],
                                            q2s[:, jf * P:(jf + 1) * P],
                                            st.ident_bf[:])
                    pv = b.tick(mm, "pe")
                    b.wait(b.ve, "pe", pv)
                    ev[k] = b.tick(b.ve.tensor_copy(
                        ght[:, tt * FF + jb * 512:tt * FF + (jb + 1) * 512],
                        pup[k][:]), "ve")
        if "attn1" not in st.skip:
            with (nc.psum_tensor(_psname(st, "p1r"), [P, D], BF16) as pr0,
                  nc.psum_tensor(_psname(st, "p1r"), [P, D], BF16) as pr1):
                prp = [pr0, pr1]
                b.wait(b.pe, "ve")
                b.wait(b.pe, "se")
                trv = {}
                for tt in range(4):
                    k = tt % 2
                    if "atr" in b.cnt:
                        b.wait(b.sq, "pe")
                    b.wait(b.sq, "atw")
                    vf = b.load("atr", o2l[:],
                                st.s_atok.ap()[bi, (ob * 4 + tt) * P:
                                               (ob * 4 + tt + 1) * P, :])
                    b.wait(b.pe, "atr", vf)
                    if k in trv:
                        b.pe.wait_ge(b.sem["ve"], trv[k])
                    for j in range(DTC):
                        mm = b.pe.transpose(prp[k][:, j * P:(j + 1) * P],
                                            o2l[:, j * P:(j + 1) * P],
                                            st.ident_bf[:])
                    pv = b.tick(mm, "pe")
                    b.wait(b.ve, "pe", pv)
                    b.ve.drain()
                    trv[k] = b.tick(b.ve.tensor_copy(
                        o2f4[:, tt * D:(tt + 1) * D], prp[k][:]), "ve")
        with (nc.psum_tensor(_psname(st, "p1o0"), [P, D], F32) as po0,
              nc.psum_tensor(_psname(st, "p1o1"), [P, D], F32) as po1,
              nc.psum_tensor(_psname(st, "p1o2"), [P, D], F32) as po2,
              nc.psum_tensor(_psname(st, "p1o3"), [P, D], F32) as po3):
            pos = [po0, po1, po2, po3]
            b.wait(b.pe, "ve")
            b.wait(b.pe, "se")
            for jf2 in range(FT_ // 2):
                k = jf2 % 2
                if k in prevw:
                    b.sq.wait_ge(b.sem["pe"], prevw[k])
                b.sq.wait_ge(b.sem["pe"], prev[k][1])
                vw = b.load(f"wf2{k}",
                            wfw[k].ap().rearrange("p (j e) -> p j e", j=2),
                            st.s_w2t1.ap().rearrange(
                                "f p e -> p f e")[:, 2 * jf2:2 * jf2 + 2, :])
                b.wait(b.pe, f"wf2{k}", vw)
                for j2 in range(2):
                    jf = 2 * jf2 + j2
                    for tt in range(4):
                        for half in range(2):
                            mm = b.pe.matmul(
                                pos[tt][:, half * 512:(half + 1) * 512],
                                ght[:, tt * FF + jf * P:tt * FF + (jf + 1) * P],
                                wfw[k][:, j2 * D + half * 512:j2 * D + (half + 1) * 512],
                                start=(jf == 0), stop=False)
                prevw[k] = b.tick(mm, "pe")
            if "attn1" not in st.skip:
                for je2 in range(DTC // 2):
                    k = je2 % 2
                    if k in prevo:
                        b.sq.wait_ge(b.sem["pe"], prevo[k])
                    b.sq.wait_ge(b.sem["pe"], prevw[k])
                    vw = b.load(f"wo{k}", wfw[k][:],
                                st.s_wT["L1_attn_Wo"].ap()[2 * je2 * P:
                                                           2 * (je2 + 1) * P, :])
                    b.wait(b.pe, f"wo{k}", vw)
                    for j2 in range(2):
                        je = 2 * je2 + j2
                        for tt in range(4):
                            for half in range(2):
                                mm = b.pe.matmul(
                                    pos[tt][:, half * 512:(half + 1) * 512],
                                    o2f4[:, tt * D + je * P:tt * D + (je + 1) * P],
                                    wfw[k][:, j2 * D + half * 512:j2 * D + (half + 1) * 512],
                                    start=False, stop=False)
                    prevo[k] = b.tick(mm, "pe")
            for tt in range(4):
                gt = bi * NTB + ob * 4 + tt
                for half in range(2):
                    mm = b.pe.matmul(pos[tt][:, half * 512:(half + 1) * 512],
                                     st.ones_row[:],
                                     st.b2r1[:, half * 512:(half + 1) * 512],
                                     start=False, stop=True)
                pv = b.tick(mm, "pe")
                b.wait(b.ve, "pe", pv)
                if "out" in b.cnt:
                    b.wait(b.ve, "out")
                b.ve.drain()
                osl = sca[:, (tt % 2) * D:(tt % 2 + 1) * D]
                vv = b.tick(b.ve.tensor_add(osl, pos[tt][:],
                                            st.xres[:, gt * D:(gt + 1) * D]), "ve")
                b.sq.wait_ge(b.sem["ve"], vv)
                b.store("out", st.x_out.ap()[bi, (ob * 4 + tt) * P:
                                             (ob * 4 + tt + 1) * P, :], osl)


def _attn1_proj(st, bi):
    st.b.phase_barrier()
    b = st.b
    nc = b.nc
    S = st.S
    xn_fm = st.cur["xn_fm"]
    sk = st.sk

    def A(name, shape, dt):
        return sk.enter_context(nc.sbuf_tensor(f"a1{name}{bi}", shape, dt))

    kfm = A("kfm", [P, DTC * S], BF16)
    vtok = A("vtok", [P, (S // P) * H * 65], BF16)
    wall = A("wall", [P, DTC * D], BF16)
    st.cur.update(dict(kfm=kfm, vtok=vtok, wqall=wall))

    NTB = st.NTB
    b.ve.drain()
    b.tick(b.ve.memset(
        vtok.ap().rearrange("p (t h e) -> p t h e", t=NTB, h=H)[:, :, :, 64:65], 1.0), "ve")

    b.wait(b.sq, "pe")
    vlast = b.load("wallh", wall.ap().rearrange("p (jd e) -> p jd e", jd=DTC),
                   st.s_wT["L1_attn_Wk"].ap().rearrange("(jd p) e -> p jd e", p=P))
    b.wait(b.pe, "wallh", vlast)
    with (nc.psum_tensor(_psname(st, "pa"), [P, 512], F32) as p0,
          nc.psum_tensor(_psname(st, "pa"), [P, 512], F32) as p1):
        pp = [p0, p1]
        b.wait(b.pe, "ve")
        b.wait(b.pe, "se")
        ev = {}
        idx = 0
        for je in range(DTC):
            for tc in range(S // 512):
                k = idx % 2
                idx += 1
                if k in ev:
                    b.pe.wait_ge(b.sem["ve"], ev[k])
                for jd in range(DTC):
                    mm = b.pe.matmul(
                        pp[k][:], wall[:, jd * D + je * P:jd * D + (je + 1) * P],
                        xn_fm[:, jd * S + tc * 512:jd * S + (tc + 1) * 512],
                        start=(jd == 0), stop=(jd == DTC - 1))
                pv = b.tick(mm, "pe")
                b.wait(b.ve, "pe", pv)
                ev[k] = b.tick(b.ve.tensor_copy(
                    kfm[:, je * S + tc * 512:je * S + (tc + 1) * 512], pp[k][:]), "ve")
    b.wait(b.sq, "pe")
    vlast = b.load("wallh", wall.ap().rearrange("p (jd e) -> p jd e", jd=DTC),
                   st.s_wT["L1_attn_Wv"].ap().rearrange("(jd p) e -> p jd e", p=P))
    b.wait(b.pe, "wallh", vlast)
    with (nc.psum_tensor(_psname(st, "pa"), [P, D], F32) as p0,
          nc.psum_tensor(_psname(st, "pa"), [P, D], F32) as p1):
        pp = [p0, p1]
        b.wait(b.pe, "ve")
        b.wait(b.pe, "se")
        ev = {}
        for tt in range(NTB):
            k = tt % 2
            if k in ev:
                b.pe.wait_ge(b.sem["ve"], ev[k])
            for jd in range(DTC):
                lhs = xn_fm[:, jd * S + tt * P:jd * S + (tt + 1) * P]
                b.pe.matmul(pp[k][:, 0:512], lhs, wall[:, jd * D:jd * D + 512],
                            start=(jd == 0), stop=(jd == DTC - 1))
                mm = b.pe.matmul(pp[k][:, 512:D], lhs,
                                 wall[:, jd * D + 512:(jd + 1) * D],
                                 start=(jd == 0), stop=(jd == DTC - 1))
            pv = b.tick(mm, "pe")
            b.wait(b.ve, "pe", pv)
            dst = vtok.ap().rearrange("p (t h e) -> p t h e", t=NTB, h=H)[
                :, tt, :, 0:64]
            ev[k] = b.tick(b.ve.tensor_copy(
                dst, pp[k].ap().rearrange("p (h e) -> p h e", h=H)), "ve")
    b.wait(b.sq, "pe")
    vlast = b.load("wallh", wall.ap().rearrange("p (jd e) -> p jd e", jd=DTC),
                   st.s_wT["L1_attn_Wq"].ap().rearrange("(jd p) e -> p jd e", p=P))
    st.cur["wq_v"] = vlast


def _attn1_core(st, bi):
    st.b.phase_barrier()
    b = st.b
    nc = b.nc
    S = st.S
    NTB = st.NTB
    NKT = S // P
    NQC = S // QCH
    sk = st.sk
    kfm, vtok, wall = st.cur["kfm"], st.cur["vtok"], st.cur["wqall"]
    b.wait(b.pe, "wallh", st.cur["wq_v"])

    def A(name, shape, dt):
        return sk.enter_context(nc.sbuf_tensor(f"c1{name}{bi}", shape, dt))

    qch = A("qch", [P, DTC * QCH], BF16)
    At2 = [A(f"At{i}", [P, NKT * QCH], BF16) for i in (0, 1)]
    otsb2 = [A(f"otsb{i}", [P, QCH], BF16) for i in (0, 1)]
    otok = A("otok", [P, (QCH // P) * H * 65], BF16)
    rd = A("rd", [P, H], F32)
    dots = A("dots", [P, H], F32)
    vns = A("vns", [P, H], F32)
    cc2 = A("cc2", [P, H], F32)
    sc1 = A("sc1", [P, D], BF16)

    prev_qgate = None
    for qc in range(NQC):
        q0 = qc * QCH
        with (nc.psum_tensor(_psname(st, "pq"), [P, 2 * QCH], F32) as p0,
              nc.psum_tensor(_psname(st, "pq"), [P, 2 * QCH], F32) as p1):
            pp = [p0, p1]
            if prev_qgate is None:
                b.wait(b.pe, "ve")
            else:
                b.pe.wait_ge(b.sem["ve"], prev_qgate)
            b.wait(b.pe, "se")
            ev = {}
            for je in range(0, DTC, 2):
                k = (je // 2) % 2
                if k in ev:
                    b.pe.wait_ge(b.sem["se"], ev[k])
                for j2 in range(2):
                    for jd in range(DTC):
                        mm = b.pe.matmul(
                            pp[k][:, j2 * QCH:(j2 + 1) * QCH],
                            wall[:, jd * D + (je + j2) * P:jd * D + (je + j2 + 1) * P],
                            st.cur["xn_fm"][:, jd * S + q0:jd * S + q0 + QCH],
                            start=(jd == 0), stop=(jd == DTC - 1))
                pv = b.tick(mm, "pe")
                b.wait(b.se, "pe", pv)
                b.se.drain()
                ev[k] = b.tick(b.se.activation(
                    qch[:, je * QCH:(je + 2) * QCH], pp[k][:], AF.Identity), "se")
        with (nc.psum_tensor(_psname(st, "pss"), [P, 4 * QCH], F32) as ps0,
              nc.psum_tensor(_psname(st, "pss"), [P, 4 * QCH], F32) as ps1,
              nc.psum_tensor(_psname(st, "pot"), [65, QCH], F32) as pt0_,
              nc.psum_tensor(_psname(st, "pot"), [65, QCH], F32) as pt1_,
              nc.psum_tensor(_psname(st, "ptq"), [P, 2 * 66], BF16) as pq0,
              nc.psum_tensor(_psname(st, "ptq"), [P, 2 * 66], BF16) as pq1):
            psp = [ps0, ps1]
            potp = [pt0_, pt1_]
            pqt = [pq0, pq1]
            tr_ev = {}
            se_done = {}
            pot_ev = {}
            exps_of = {}
            b.wait(b.pe, "se")

            def emit_scores(h):
                qf = qch[(h % 2) * 64:(h % 2) * 64 + 64,
                         (h // 2) * QCH:(h // 2) * QCH + QCH]
                Ath = At2[h % 2]
                for kg in range(NKT // 4):
                    k = kg % 2
                    if k in se_done:
                        b.pe.wait_ge(b.sem["se"], se_done[k])
                    for kt4 in range(4):
                        kt = kg * 4 + kt4
                        kf = kfm[(h % 2) * 64:(h % 2) * 64 + 64,
                                 (h // 2) * S + kt * P:(h // 2) * S + (kt + 1) * P]
                        mm = b.pe.matmul(psp[k][:, kt4 * QCH:(kt4 + 1) * QCH],
                                         kf, qf, start=True, stop=True)
                    pv = b.tick(mm, "pe")
                    b.wait(b.se, "pe", pv)
                    se_done[k] = b.tick(b.se.activation(
                        Ath[:, kg * 4 * QCH:(kg + 1) * 4 * QCH], psp[k][:],
                        AF.Exp, bias=st.shift_col[:], scale=0.125), "se")
                exps_of[h] = se_done[k]

            def emit_ot(hp, pvp):
                b.wait(b.ve, "pe", pvp)
                b.ve.drain()
                vv = b.tick(b.ve.tensor_copy(otsb2[hp % 2][0:65, :],
                                             potp[hp % 2][:]), "ve")
                pot_ev[hp % 2] = vv
                b.wait(b.pe, "ve", vv)
                k = hp % 2
                if k in tr_ev:
                    b.pe.wait_ge(b.sem["ve"], tr_ev[k])
                for tt in range(QCH // P):
                    mm = b.pe.transpose(
                        pqt[k][:, tt * 66:tt * 66 + 65],
                        otsb2[hp % 2][0:65, tt * P:(tt + 1) * P],
                        st.ident_bf[0:65, 0:65])
                pv = b.tick(mm, "pe")
                b.wait(b.ve, "pe", pv)
                tr_ev[k] = b.tick(b.ve.tensor_copy(
                    otok.ap().rearrange("p (t x) -> p t x", t=QCH // P)[
                        :, :, hp * 65:(hp + 1) * 65],
                    pqt[k].ap().rearrange("p (t c) -> p t c", c=66)[:, :, 0:65]), "ve")
            emit_scores(0)
            pend = None
            for h in range(H):
                if h + 1 < H:
                    emit_scores(h + 1)
                b.pe.wait_ge(b.sem["se"], exps_of[h])
                if h % 2 in pot_ev:
                    b.wait(b.pe, "ve", pot_ev[h % 2])
                Ath = At2[h % 2]
                for kt in range(NKT):
                    va = vtok.ap().rearrange(
                        "p (t h e) -> p t h e", t=NTB, h=H)[:, kt, h, :]
                    mm = b.pe.matmul(potp[h % 2][:], va,
                                     Ath[:, kt * QCH:(kt + 1) * QCH],
                                     start=(kt == 0), stop=(kt == NKT - 1))
                pv = b.tick(mm, "pe")
                if pend is not None:
                    emit_ot(*pend)
                pend = (h, pv)
            emit_ot(*pend)
            prev_qgate = b.cnt["ve"]
            for tt in range(QCH // P):
                gt = q0 // P + tt
                po3 = otok.ap().rearrange("p (t h e) -> p t h e", t=QCH // P,
                                          h=H)[:, tt]
                b.ve.drain()
                if "atw" in b.cnt:
                    b.wait(b.ve, "atw")
                b.ve.reciprocal(rd[:], po3[:, :, 64:65])
                b.ve.drain()
                v3 = vtok.ap().rearrange("p (t h e) -> p t h e", t=NTB, h=H)[
                    :, gt, :, 0:64]
                b.ve.tensor_tensor(sc1.ap().rearrange("p (h e) -> p h e", h=H),
                                   po3[:, :, 0:64], v3, ALU.mult)
                b.ve.drain()
                b.ve.reduce_sum(dots[:], sc1.ap().rearrange("p (h e) -> p h e", h=H),
                                axis=AX.X)
                b.ve.drain()
                b.ve.tensor_tensor(sc1.ap().rearrange("p (h e) -> p h e", h=H),
                                   v3, v3, ALU.mult)
                b.ve.drain()
                b.ve.reduce_sum(vns[:], sc1.ap().rearrange("p (h e) -> p h e", h=H),
                                axis=AX.X)
                b.ve.drain()
                b.ve.tensor_scalar_add(vns[:], vns[:], XSA_EPS)
                b.ve.drain()
                b.ve.reciprocal(vns[:], vns[:])
                b.ve.drain()
                b.ve.tensor_mul(cc2[:], dots[:], vns[:])
                b.ve.drain()
                b.ve.tensor_scalar(cc2[:], cc2[:], st.xsa_col[:], None, ALU.mult)
                b.ve.drain()
                b.ve.tensor_tensor(sc1.ap().rearrange("p (h e) -> p h e", h=H),
                                   cc2.ap().unsqueeze(2).broadcast_to([P, H, 64]),
                                   v3, ALU.mult)
                b.ve.drain()
                b.ve.tensor_tensor(sc1.ap().rearrange("p (h e) -> p h e", h=H),
                                   po3[:, :, 0:64],
                                   sc1.ap().rearrange("p (h e) -> p h e", h=H), ALU.add)
                b.ve.drain()
                vv = b.tick(b.ve.tensor_tensor(
                    sc1.ap().rearrange("p (h e) -> p h e", h=H),
                    sc1.ap().rearrange("p (h e) -> p h e", h=H),
                    rd.ap().unsqueeze(2).broadcast_to([P, H, 64]), ALU.mult), "ve")
                b.sq.wait_ge(b.sem["ve"], vv)
                b.store("atw", st.s_atok.ap()[bi, gt * P:(gt + 1) * P, :], sc1[:])




# version 15
# speedup vs baseline: 1.1709x; 1.0377x over previous
import os
from contextlib import ExitStack

import numpy as np
import ml_dtypes

import concourse.bass as bass
import concourse.mybir as mybir
from concourse.bass_utils import run_bass_kernel_spmd

F32 = mybir.dt.float32
BF16 = mybir.dt.bfloat16
AF = mybir.ActivationFunctionType
ALU = mybir.AluOpType
AX = mybir.AxisListType

FP64_SIM = os.environ.get("BASS_SIM_FP64") == "1"

H = 16
HD = 64
D = 1024
FF = 4096
P = 128
DTC = 8
FT_ = 32
GC = 128
SC = 256
QCH = 256
LN_EPS = 1e-5
XSA_EPS = 1e-8
W_KEYS = [
    "L0_attn_Wq", "L0_attn_Wk", "L0_attn_Wv", "L0_attn_Wg", "L0_attn_bg",
    "L0_attn_Wo", "L0_ffn_W1", "L0_ffn_b1", "L0_ffn_W2", "L0_ffn_b2",
    "L1_attn_Wq", "L1_attn_Wk", "L1_attn_Wv", "L1_attn_Wo", "L1_attn_xsa_scale",
    "L1_ffn_W1", "L1_ffn_b1", "L1_ffn_W2", "L1_ffn_b2",
]


def _consts():
    bf = ml_dtypes.bfloat16
    ident_bf = np.eye(P, dtype=np.float32).astype(bf)
    ident_f32 = np.eye(P, dtype=np.float32)
    mask_u = np.triu(np.ones((GC, GC), dtype=np.float32)).astype(bf)
    mask_tiled = np.tile(mask_u, (1, H))
    m1 = (0.5 * np.ones((GC, GC)) - np.triu(np.ones((GC, GC)))).astype(np.float32)
    ones_row_bf = np.ones((1, P), dtype=np.float32).astype(bf)
    ones_col_f32 = np.ones((P, 1), dtype=np.float32)
    dup = np.zeros((64, P), dtype=np.float32)
    dup[np.arange(64), np.arange(64)] = 1.0
    dup[np.arange(64), 64 + np.arange(64)] = 1.0
    sel = np.zeros((P, P), dtype=np.float32)
    sel[127, :] = 1.0
    return {
        "c_ident_bf": ident_bf,
        "c_ident_f32": ident_f32,
        "c_mask": mask_tiled,
        "c_m1": m1,
        "c_ones_row_bf": ones_row_bf,
        "c_ones_col_f32": ones_col_f32,
        "c_dup": dup.astype(bf),
        "c_sel": sel.astype(bf),
    }


class Bld:

    def __init__(self, nc, stack):
        self.nc = nc
        self.stack = stack
        self.pe = nc.tensor
        self.ve = nc.vector
        self.se = nc.scalar
        self.sq = nc.sync
        self.gp = nc.gpsimd
        self.sem = {}
        self.cnt = {}
        self.kind = {}
        for name in ("pe", "ve", "se"):
            self.sem[name] = stack.enter_context(nc.semaphore("s_" + name))
            self.cnt[name] = 0

    def dsem(self, name, kind):
        if name not in self.sem:
            self.sem[name] = self.stack.enter_context(self.nc.semaphore("d_" + name))
            self.cnt[name] = 0
            self.kind[name] = kind
        assert self.kind[name] == kind
        return self.sem[name]

    def tick(self, instr, name, n=1):
        instr.then_inc(self.sem[name], n)
        self.cnt[name] += n
        return self.cnt[name]

    def wait(self, eng, name, val=None):
        eng.wait_ge(self.sem[name], self.cnt[name] if val is None else val)

    def wait_all(self, engs, names):
        for e in engs:
            for n in names:
                self.wait(e, n)

    def phase_barrier(self):
        for e in (self.pe, self.ve, self.se, self.sq, self.gp):
            for n in ("pe", "ve", "se"):
                self.wait(e, n)
        for nm, kd in list(self.kind.items()):
            self.wait(self.sq, nm)
            self.wait(self.gp, nm)

    def sb(self, name, shape, dt):
        return self.stack.enter_context(self.nc.sbuf_tensor(name, shape, dt))

    def load(self, name, dst_ap, src_ap):
        self.dsem(name, "h")
        return self.tick(self.sq.dma_start(dst_ap, src_ap), name, 16)

    def loadc(self, name, dst_ap, src_ap):
        self.dsem(name, "g")
        return self.tick(self.gp.dma_start(dst_ap, src_ap), name, 16)

    def store(self, name, dst_ap, src_ap):
        self.dsem(name, "h")
        return self.tick(self.sq.dma_start(dst_ap, src_ap), name, 16)

    def storeg(self, name, dst_ap, src_ap):
        self.dsem(name, "g")
        return self.tick(self.gp.dma_start(dst_ap, src_ap), name, 16)

    def transpose_block(self, dst_ps_ap, src_sb_ap, ident_ap):
        return self.pe.transpose(dst_ps_ap, src_sb_ap, ident_ap)

    def emit_round(self, dst_ap, src_ap, scr_sgn, scr_bf, scr_f32, n):
        b = self
        if not FP64_SIM:
            b.wait(b.se, "ve")
            b.tick(b.se.activation(scr_sgn[:, :n], src_ap, AF.Sign), "se")
            b.wait(b.ve, "se")
            b.ve.scalar_tensor_tensor(scr_bf[:, :n], scr_sgn[:, :n], 128.0,
                                      src_ap, ALU.mult, ALU.add)
            b.ve.drain()
            b.tick(b.ve.scalar_tensor_tensor(dst_ap, scr_sgn[:, :n], -128.0,
                                             scr_bf[:, :n], ALU.mult, ALU.add), "ve")
        else:
            v = scr_f32
            b.ve.tensor_scalar_add(v[:, :n], src_ap, 128.5)
            b.ve.drain()
            for c in (256.0, 128.0, 64.0, 32.0, 16.0, 8.0, 4.0, 2.0, 1.0):
                b.ve.tensor_single_scalar(scr_sgn[:, :n], v[:, :n], c, ALU.is_ge)
                b.ve.drain()
                b.ve.scalar_tensor_tensor(v[:, :n], scr_sgn[:, :n], -c, v[:, :n],
                                          ALU.mult, ALU.add)
                b.ve.drain()
            b.ve.tensor_scalar(scr_sgn[:, :n], src_ap, 128.5, -128.0, ALU.add, ALU.add)
            b.ve.drain()
            b.tick(b.ve.tensor_sub(dst_ap, scr_sgn[:, :n], v[:, :n]), "ve")


def kernel(**inputs):
    x = np.ascontiguousarray(np.asarray(inputs["L0_attn_x"], dtype=np.float32))
    B, S, Dm = x.shape
    n_cores = 8
    b_loc = B // n_cores
    consts = _consts()
    nc = build_program(b_loc, S)
    in_maps = []
    for c in range(n_cores):
        m = {"x_in": x[c * b_loc:(c + 1) * b_loc]}
        for k in W_KEYS:
            m[k] = np.ascontiguousarray(np.asarray(inputs[k], dtype=np.float32))
        m.update(consts)
        in_maps.append(m)
    res = run_bass_kernel_spmd(nc, in_maps, core_ids=list(range(n_cores)))
    return np.concatenate([r["x_out"] for r in res.results], axis=0)


def build_program(b_loc, S, skip=()):
    nc = bass.Bass("TRN2", target_bir_lowering=False)
    stack = ExitStack()
    b = Bld(nc, stack)
    emit_all(b, b_loc, S, set(skip))
    stack.close()
    return nc


def emit_all(b, b_loc, S, skip=()):
    nc = b.nc
    NTB = S // P
    NT = b_loc * NTB
    NSC = S // SC
    FT = FF // P
    DT = D // P

    x_in = nc.dram_tensor("x_in", [b_loc, S, D], F32, kind="ExternalInput")
    x_out = nc.dram_tensor("x_out", [b_loc, S, D], F32, kind="ExternalOutput")
    Wd = {}
    shp = {
        "L0_attn_Wq": [D, D], "L0_attn_Wk": [D, D], "L0_attn_Wv": [D, D],
        "L0_attn_Wg": [D, D], "L0_attn_bg": [D], "L0_attn_Wo": [D, D],
        "L0_ffn_W1": [FF, D], "L0_ffn_b1": [FF], "L0_ffn_W2": [D, FF],
        "L0_ffn_b2": [D],
        "L1_attn_Wq": [D, D], "L1_attn_Wk": [D, D], "L1_attn_Wv": [D, D],
        "L1_attn_Wo": [D, D], "L1_attn_xsa_scale": [1],
        "L1_ffn_W1": [FF, D], "L1_ffn_b1": [FF], "L1_ffn_W2": [D, FF],
        "L1_ffn_b2": [D],
    }
    for k, s_ in shp.items():
        Wd[k] = nc.dram_tensor(k, s_, F32, kind="ExternalInput")
    cd = {
        "c_ident_bf": nc.dram_tensor("c_ident_bf", [P, P], BF16, kind="ExternalInput"),
        "c_ident_f32": nc.dram_tensor("c_ident_f32", [P, P], F32, kind="ExternalInput"),
        "c_mask": nc.dram_tensor("c_mask", [P, H * GC], BF16, kind="ExternalInput"),
        "c_m1": nc.dram_tensor("c_m1", [GC, GC], F32, kind="ExternalInput"),
        "c_ones_row_bf": nc.dram_tensor("c_ones_row_bf", [1, P], BF16, kind="ExternalInput"),
        "c_ones_col_f32": nc.dram_tensor("c_ones_col_f32", [P, 1], F32, kind="ExternalInput"),
        "c_dup": nc.dram_tensor("c_dup", [64, P], BF16, kind="ExternalInput"),
        "c_sel": nc.dram_tensor("c_sel", [P, P], BF16, kind="ExternalInput"),
    }
    s_w1t0 = nc.dram_tensor("s_w1t0", [FT, P, D], BF16)
    s_w1t1 = nc.dram_tensor("s_w1t1", [FT, P, D], BF16)
    s_w2t0 = nc.dram_tensor("s_w2t0", [FT, P, D], BF16)
    s_w2t1 = nc.dram_tensor("s_w2t1", [FT, P, D], BF16)
    s_wT = {k: nc.dram_tensor("s_" + k + "T", [D, D], BF16)
            for k in ("L1_attn_Wq", "L1_attn_Wk", "L1_attn_Wv", "L1_attn_Wo")}
    s_wbf = {k: nc.dram_tensor("s_" + k + "B", [D, D], BF16)
             for k in ("L0_attn_Wq", "L0_attn_Wk", "L0_attn_Wv", "L0_attn_Wg",
                       "L0_attn_Wo")}
    s_atok = nc.dram_tensor("s_atok", [b_loc, S, D], BF16)

    xres = b.sb("xres", [P, NT * D], BF16)
    ident_bf = b.sb("ident_bf", [P, P], BF16)
    ident_f32 = b.sb("ident_f32", [P, P], F32)
    ones_row = b.sb("ones_row", [1, P], BF16)
    ones_col = b.sb("ones_col", [P, 1], F32)
    b1c1 = b.sb("b1c1", [P, FT], F32)
    b2r1 = b.sb("b2r1", [1, D], BF16)
    xsa_col = b.sb("xsa_col", [P, 1], F32)
    thcol = [b.sb(f"thcol{l}", [P, 1], F32) for l in (0, 1)]
    thcol2 = [b.sb(f"thcol2{l}", [P, 1], F32) for l in (0, 1)]
    thneg = b.sb("thneg", [P, 1], F32)
    wscol = [b.sb(f"wscol{l}", [P, 1], F32) for l in (0, 1)]
    rowtmp = b.sb("rowtmp", [1, 8], F32)
    shift_col = b.sb("shift_col", [P, 1], F32)

    b.tick(b.ve.memset(shift_col[:], -14.0), "ve")
    b.stack.enter_context(nc.allow_non_contiguous_dma(reason="one-time small bias loads"))
    b.load("c0", ident_bf[:], cd["c_ident_bf"].ap())
    b.load("c0", ident_f32[:], cd["c_ident_f32"].ap())
    b.load("c0", ones_row[:], cd["c_ones_row_bf"].ap())
    b.load("c0", ones_col[:], cd["c_ones_col_f32"].ap())
    b.load("c0", b1c1[:], Wd["L1_ffn_b1"].ap().rearrange("(i p) -> p i", p=P))
    b.loadc("cg", b2r1[:], Wd["L1_ffn_b2"].ap().rearrange("(o f) -> o f", o=1))


    ones_row_f = b.sb("ones_row_f", [1, P], F32)
    xsa_sb = b.sb("xsa_sb", [1, 1], F32)
    b.load("c0", ones_row_f[:], cd["c_ones_col_f32"].ap().rearrange("p o -> o p"))
    b.load("c0", xsa_sb[:], Wd["L1_attn_xsa_scale"].ap().rearrange("(o x) -> o x", o=1))
    b.wait_all([b.pe, b.ve, b.se, b.sq, b.gp], ["c0", "cg"])
    with nc.psum_tensor("pxsa", [P, 1], F32) as pxsa:
        b.tick(b.pe.matmul(pxsa[:], ones_row_f[:], xsa_sb[:], start=True, stop=True), "pe")
        b.wait(b.ve, "pe")
        b.tick(b.ve.tensor_scalar_mul(xsa_col[:], pxsa[:], -1.0), "ve")

    cast_sk = ExitStack()
    _prep_cast_all(b, [(Wd[k], s_wbf[k]) for k in s_wbf], cast_sk)

    def wide_a(scratch):
        return lambda i, stg: (scratch.ap()[i, :, :], stg[:])

    def wide_b(scratch):
        return lambda i, stg: (
            scratch.ap()[:, :, i * P:(i + 1) * P].rearrange("j p c -> p j c"),
            stg.ap().rearrange("p (j c) -> p j c", c=P))
    _prep_ternary(b, Wd["L0_ffn_W1"], wide_a(s_w1t0),
                  1e-8, thcol[0], None, "a0", ones_col, ones_row_f, ident_bf, thneg)
    _prep_ternary(b, Wd["L0_ffn_W2"], wide_b(s_w2t0),
                  1e-8, thcol2[0], None, "b0", ones_col, ones_row_f, ident_bf, thneg)
    for k in s_wT:
        _prep_transpose(b, Wd[k], s_wT[k], ident_bf, "t" + k[-2:])
    _prep_ternary(b, Wd["L1_ffn_W1"], wide_a(s_w1t1),
                  1e-5, thcol[1], wscol[0], "a1", ones_col, ones_row_f, ident_bf, thneg)
    _prep_ternary(b, Wd["L1_ffn_W2"], wide_b(s_w2t1),
                  1e-5, thcol2[1], wscol[1], "b1", ones_col, ones_row_f, ident_bf, thneg)
    for nm in list(b.kind):
        if nm.startswith("pst"):
            b.wait(b.sq, nm)
            b.wait(b.gp, nm)
    for e in (b.pe, b.ve, b.se, b.sq, b.gp):
        for nm in ("pe", "ve", "se"):
            b.wait(e, nm)
    cast_sk.close()

    st = _State(b, locals())
    st.uid = 0
    with ExitStack() as sk0:
        st.maskc = maskc = sk0.enter_context(nc.sbuf_tensor("maskc", [P, H * GC], BF16))
        st.m1c = m1c = sk0.enter_context(nc.sbuf_tensor("m1c", [GC, GC], F32))
        st.bg_bc = bg_bc = sk0.enter_context(nc.sbuf_tensor("bg_bc", [P, D], F32))
        st.b1c0 = b1c0 = sk0.enter_context(nc.sbuf_tensor("b1c0", [P, FT], F32))
        st.b2c0 = b2c0 = sk0.enter_context(nc.sbuf_tensor("b2c0", [P, DT], F32))
        st.dup_bf = dup_bf = sk0.enter_context(nc.sbuf_tensor("dup_bf", [64, P], BF16))
        st.sel_bf = sel_bf = sk0.enter_context(nc.sbuf_tensor("sel_bf", [P, P], BF16))
        bg_row = sk0.enter_context(nc.sbuf_tensor("bg_row", [1, D], F32))
        b.phase_barrier()
        with nc.allow_non_contiguous_dma(reason="one-time small bias loads"):
            b.load("c0", maskc[:], cd["c_mask"].ap())
            b.load("c0", m1c[:], cd["c_m1"].ap())
            b.load("c0", b1c0[:], Wd["L0_ffn_b1"].ap().rearrange("(i p) -> p i", p=P))
            b.load("c0", b2c0[:], Wd["L0_ffn_b2"].ap().rearrange("(i p) -> p i", p=P))
            b.load("c0", bg_row[:], Wd["L0_attn_bg"].ap().rearrange("(o f) -> o f", o=1))
            b.load("c0", dup_bf[:], cd["c_dup"].ap())
            b.load("c0", sel_bf[:], cd["c_sel"].ap())
        b.wait_all([b.pe, b.ve, b.se], ["c0"])
        with nc.psum_tensor("pbg2", [P, D], F32) as pbg2:
            b.wait(b.pe, "ve")
            for j in range(D // 512):
                mm = b.pe.matmul(pbg2[:, j * 512:(j + 1) * 512], ones_row_f[:],
                                 bg_row[:, j * 512:(j + 1) * 512], start=True, stop=True)
            b.tick(mm, "pe")
            b.wait(b.ve, "pe")
            b.tick(b.ve.tensor_copy(bg_bc[:], pbg2[:]), "ve")
        for bi in range(b_loc):
            _layer0(st, bi)
        b.phase_barrier()
    if "l1" not in st.skip:
        for bi in range(b_loc):
            _layer1(st, bi)

    for nm, kd in list(b.kind.items()):
        b.wait(b.sq if kd == "h" else b.gp, nm)


class _State:
    def __init__(self, b, env):
        self.b = b
        for k, v in env.items():
            setattr(self, k, v)


def _prep_ternary(b, Wdram, wide_fn, eps, th_col, ws_col, tag,
                  ones_col, ones_row_f, ident_bf, thneg):
    nc = b.nc
    R, C = Wdram.ap().shape
    ntr, ncc = R // P, C // P
    nbuf = 4 if C <= 2048 else 2
    lsk = ExitStack()

    def LS(name, shape, dt):
        return lsk.enter_context(nc.sbuf_tensor(name, shape, dt))
    wnat = [LS(f"wn{tag}{i}", [P, C], F32) for i in range(nbuf)]
    scr = LS(f"wsc{tag}", [P, C], BF16)
    sums = LS(f"wsm{tag}", [P, ntr], F32)
    stg = [LS(f"wst{tag}{i}", [P, C], BF16) for i in (0, 1)]
    row1 = LS(f"wr1{tag}", [1, 1], F32)
    row2 = LS(f"wr2{tag}", [1, 1], F32)
    ve_done = [0] * nbuf
    for i in range(ntr):
        k = i % nbuf
        if i >= nbuf:
            b.sq.wait_ge(b.sem["ve"], ve_done[k])
        v = b.load(f"wn{tag}{k}", wnat[k][:], Wdram.ap()[i * P:(i + 1) * P, :])
        b.wait(b.ve, f"wn{tag}{k}", v)
        ve_done[k] = b.tick(
            b.ve.reduce_sum(sums[:, i:i + 1], wnat[k][:], axis=AX.X,
                            apply_absolute_value=True), "ve")
    with nc.psum_tensor("pws" + tag, [1, ntr], F32) as pws:
        b.wait(b.pe, "ve")
        b.tick(b.pe.matmul(pws[:], ones_col[:], sums[:], start=True, stop=True), "pe")
        b.wait(b.ve, "pe")
        b.ve.reduce_sum(row1[:], pws[:], axis=AX.X)
    b.ve.drain()
    b.ve.tensor_scalar(row2[:], row1[:], 1.0 / (R * C), eps, ALU.mult, ALU.max)
    b.tick(b.ve.drain(), "ve")
    with nc.psum_tensor("pth" + tag, [P, 1], F32) as pth:
        b.wait(b.pe, "ve")
        b.tick(b.pe.matmul(pth[:], ones_row_f[:], row2[:], start=True, stop=True), "pe")
        b.wait(b.ve, "pe")
        if ws_col is not None:
            b.ve.tensor_copy(ws_col[:], pth[:])
        b.ve.tensor_scalar_mul(th_col[:], pth[:], 0.5)
        b.ve.drain()
        b.tick(b.ve.tensor_scalar_mul(thneg[:], th_col[:], -1.0), "ve")
        b.ve.drain()
    with (nc.psum_tensor("ptr" + tag + "0", [P, 512], BF16) as pt0,
          nc.psum_tensor("ptr" + tag + "1", [P, 512], BF16) as pt1):
        pts = [pt0, pt1]
        b.wait(b.pe, "ve")
        b.wait(b.pe, "se")
        dh_done = {}
        ve_evac = {}
        ve_th = [0] * nbuf
        pe_tr_last = 0
        for i in range(ntr):
            k = i % nbuf
            b.sq.wait_ge(b.sem["ve"], ve_th[k] if i >= nbuf else ve_done[k])
            v = b.load(f"wn{tag}{k}", wnat[k][:], Wdram.ap()[i * P:(i + 1) * P, :])
            b.wait(b.ve, f"wn{tag}{k}", v)
            if i >= 1:
                b.wait(b.ve, "pe", pe_tr_last)
            b.ve.tensor_single_scalar(scr[:], wnat[k][:], thneg[:], ALU.is_lt)
            b.ve.drain()
            ve_th[k] = b.tick(
                b.ve.scalar_tensor_tensor(scr[:], wnat[k][:], th_col[:], scr[:],
                                          ALU.is_gt, ALU.subtract), "ve")
            b.wait(b.pe, "ve", ve_th[k])
            ks = i % 2
            last_evac = 0
            nb4 = max(1, ncc // 4)
            for jb in range(nb4):
                kp = jb % 2
                if kp in ve_evac:
                    b.pe.wait_ge(b.sem["ve"], ve_evac[kp])
                for j4 in range(min(4, ncc)):
                    j = jb * 4 + j4
                    mm = b.pe.transpose(pts[kp][:, j4 * P:(j4 + 1) * P],
                                        scr[:, j * P:(j + 1) * P], ident_bf[:])
                pv = b.tick(mm, "pe")
                if jb == nb4 - 1:
                    pe_tr_last = pv
                b.wait(b.ve, "pe", pv)
                if jb == 0 and ks in dh_done:
                    b.ve.wait_ge(b.sem[f"pst{tag}{ks}"], dh_done[ks])
                w4 = min(4, ncc) * P
                ve_evac[kp] = last_evac = b.tick(
                    b.ve.tensor_copy(stg[ks][:, jb * 512:jb * 512 + w4],
                                     pts[kp][:, 0:w4]), "ve")
            b.sq.wait_ge(b.sem["ve"], last_evac)
            if ks in dh_done:
                b.sq.wait_ge(b.sem[f"pst{tag}{ks}"], dh_done[ks])
            dst_ap, src_ap = wide_fn(i, stg[ks])
            dh_done[ks] = b.store(f"pst{tag}{ks}", dst_ap, src_ap)
    for e in (b.sq, b.gp):
        b.wait(e, "ve")
        b.wait(e, "pe")
        b.wait(e, f"pst{tag}0")
        b.wait(e, f"pst{tag}1")
    lsk.close()


def _prep_cast_all(b, pairs, sk):
    nc = b.nc
    bufs = [sk.enter_context(nc.sbuf_tensor(f"cbw{i}", [P, D], BF16))
            for i in range(4)]
    n = 0
    for w, dsc in pairs:
        for i in range(D // P):
            k = n % 4
            if n >= 4:
                b.wait(b.gp, f"pstc{k}")
            v = b.loadc(f"cwl{k}", bufs[k][:], w.ap()[i * P:(i + 1) * P, :])
            b.wait(b.gp, f"cwl{k}", v)
            b.storeg(f"pstc{k}", dsc.ap()[i * P:(i + 1) * P, :], bufs[k][:])
            n += 1


def _prep_transpose(b, Wdram, scratch, ident_bf, tag):
    nc = b.nc
    ntr = D // P
    lsk = ExitStack()
    wnat = [lsk.enter_context(nc.sbuf_tensor(f"tn{tag}{i}", [P, D], BF16))
            for i in (0, 1)]
    stg = [lsk.enter_context(nc.sbuf_tensor(f"tg{tag}{i}", [P, D], BF16))
           for i in (0, 1)]
    pe_done = [0, 0]
    dh_done = {}
    ve_evac = {}
    with (nc.psum_tensor("ptt" + tag + "0", [P, 512], BF16) as pt0,
          nc.psum_tensor("ptt" + tag + "1", [P, 512], BF16) as pt1):
        pts = [pt0, pt1]
        b.wait(b.pe, "ve")
        b.wait(b.pe, "se")
        for i in range(ntr):
            buf = wnat[i % 2]
            if i >= 2:
                b.gp.wait_ge(b.sem["pe"], pe_done[i % 2])
            v = b.loadc(f"tn{tag}{i % 2}", buf[:], Wdram.ap()[i * P:(i + 1) * P, :])
            b.wait(b.pe, f"tn{tag}{i % 2}", v)
            ks = i % 2
            last_evac = 0
            for jb in range(ntr // 4):
                k = jb % 2
                if k in ve_evac:
                    b.pe.wait_ge(b.sem["ve"], ve_evac[k])
                for j4 in range(4):
                    j = jb * 4 + j4
                    mm = b.pe.transpose(pts[k][:, j4 * P:(j4 + 1) * P],
                                        buf[:, j * P:(j + 1) * P], ident_bf[:])
                pv = b.tick(mm, "pe")
                if jb == ntr // 4 - 1:
                    pe_done[i % 2] = pv
                b.wait(b.ve, "pe", pv)
                if jb == 0 and ks in dh_done:
                    b.ve.wait_ge(b.sem[f"pst{tag}{ks}"], dh_done[ks])
                ve_evac[k] = last_evac = b.tick(
                    b.ve.tensor_copy(stg[ks][:, jb * 512:(jb + 1) * 512],
                                     pts[k][:]), "ve")
            b.sq.wait_ge(b.sem["ve"], last_evac)
            if ks in dh_done:
                b.sq.wait_ge(b.sem[f"pst{tag}{ks}"], dh_done[ks])
            dh_done[ks] = b.store(
                f"pst{tag}{ks}",
                scratch.ap()[:, i * P:(i + 1) * P].rearrange(
                    "(j p) c -> p j c", p=P),
                stg[ks].ap().rearrange("p (j c) -> p j c", c=P))
    for e in (b.sq, b.gp):
        b.wait(e, "ve")
        b.wait(e, "pe")
        b.wait(e, f"pst{tag}0")
        b.wait(e, f"pst{tag}1")
    lsk.close()

def _ln_tiles(st, bi, layer):
    b = st.b
    nc = b.nc
    S = st.S
    NTB = st.NTB
    xn_fm = st.cur["xn_fm"]
    lsk = ExitStack()
    xt = [lsk.enter_context(nc.sbuf_tensor(f"xt{layer}{bi}{i}", [P, D], F32))
          for i in (0, 1)]
    sq = lsk.enter_context(nc.sbuf_tensor(f"sq{layer}{bi}", [P, D], F32))
    stat = [lsk.enter_context(nc.sbuf_tensor(f"st{layer}{bi}{i}", [P, 4], F32))
            for i in (0, 1)]
    xn_tok = [lsk.enter_context(nc.sbuf_tensor(f"xk{layer}{bi}{i}", [P, D], BF16))
              for i in (0, 1)]
    with (nc.psum_tensor(f"ptl{layer}{bi}0", [P, 512], BF16) as pt0,
          nc.psum_tensor(f"ptl{layer}{bi}1", [P, 512], BF16) as pt1):
        pts = [pt0, pt1]
        b.wait(b.pe, "ve")
        b.wait(b.pe, "se")
        evac = {}
        done = {}
        sqv = {}

        def emit_stats(t):
            k = t % 2
            if layer == 0:
                if t >= 2:
                    b.sq.wait_ge(b.sem["ve"], done[t - 2][0])
                    b.sq.wait_ge(b.sem["se"], done[t - 2][1])
                else:
                    b.wait(b.sq, "pe")
                    b.wait(b.sq, "ve")
                    b.wait(b.sq, "se")
                v = b.load(f"xld{k}", xt[k][:], st.x_in.ap()[bi, t * P:(t + 1) * P, :])
                b.wait(b.ve, f"xld{k}", v)
            else:
                if t >= 2:
                    b.wait(b.ve, "se", done[t - 2][1])
                b.ve.drain()
                b.ve.tensor_copy(xt[k][:], st.xres[:, (bi * NTB + t) * D:
                                                   (bi * NTB + t + 1) * D])
                b.ve.drain()
            stt = stat[k]
            b.ve.reduce_sum(stt[:, 0:1], xt[k][:], axis=AX.X)
            b.ve.drain()
            b.ve.tensor_scalar_mul(stt[:, 1:2], stt[:, 0:1], -1.0 / D)
            b.ve.drain()
            vs = b.tick(b.ve.scalar_tensor_tensor(sq[:], xt[k][:], stt[:, 1:2],
                                                  xt[k][:], ALU.add, ALU.mult,
                                                  accum_out=stt[:, 0:1]), "ve")
            b.ve.drain()
            b.ve.tensor_scalar(stt[:, 2:3], stt[:, 0:1], 1.0 / D, LN_EPS,
                               ALU.mult, ALU.add)
            b.ve.drain()
            vv = b.tick(b.ve.reciprocal(stt[:, 3:4], stt[:, 2:3]), "ve")
            b.wait(b.se, "ve", vv)
            b.se.drain()
            sv = b.tick(b.se.activation(stt[:, 2:3], stt[:, 3:4], AF.Sqrt), "se")
            b.wait(b.ve, "se", sv)
            b.ve.drain()
            bv = b.tick(b.ve.tensor_mul(stt[:, 0:1], stt[:, 1:2], stt[:, 2:3]), "ve")
            sqv[t] = (vs, bv)

        def emit_apply(t):
            k = t % 2
            stt = stat[k]
            b.wait(b.se, "ve", sqv[t][1])
            b.se.drain()
            sa = b.tick(b.se.activation(xn_tok[k][:], xt[k][:], AF.Identity,
                                        bias=stt[:, 0:1], scale=stt[:, 2:3]), "se")
            if layer == 0:
                b.se.drain()
                sa = b.tick(b.se.activation(
                    st.xres[:, (bi * NTB + t) * D:(bi * NTB + t + 1) * D],
                    xt[k][:], AF.Identity), "se")
            done[t] = (sqv[t][0], sa)
            b.wait(b.pe, "se", sa)
            xn3 = xn_fm.ap().rearrange("p (j s) -> p j s", j=DTC)
            for jb in range(2):
                kp = jb % 2
                if kp in evac:
                    en, val = evac[kp]
                    b.pe.wait_ge(b.sem[en], val)
                for j4 in range(4):
                    j = jb * 4 + j4
                    mm = b.pe.transpose(pts[kp][:, j4 * P:(j4 + 1) * P],
                                        xn_tok[k][:, j * P:(j + 1) * P],
                                        st.ident_bf[:])
                pv = b.tick(mm, "pe")
                dst = xn3[:, jb * 4:(jb + 1) * 4, t * P:(t + 1) * P]
                src = pts[kp].ap().rearrange("p (j c) -> p j c", j=4)
                if jb == 0:
                    b.wait(b.ve, "pe", pv)
                    evac[kp] = ("ve", b.tick(b.ve.tensor_copy(dst, src), "ve"))
                else:
                    b.wait(b.se, "pe", pv)
                    b.se.drain()
                    evac[kp] = ("se", b.tick(
                        b.se.activation(dst, src, AF.Identity), "se"))

        for t in range(NTB):
            emit_stats(t)
            if t >= 1:
                emit_apply(t - 1)
        emit_apply(NTB - 1)
    b.wait(b.pe, "ve")
    b.wait(b.pe, "se")
    lsk.close()


def _layer0(st, bi):
    b = st.b
    nc = b.nc
    S = st.S
    NTB = st.NTB
    NSC = st.NSC
    skip = st.skip
    with ExitStack() as sk:
        st.sk = sk
        st.cur = {}
        st.cur["xn_fm"] = sk.enter_context(
            nc.sbuf_tensor(f"xnfm0{bi}", [P, DTC * S], BF16))
        _ln_tiles(st, bi, 0)
        if "gla" not in skip:
            _gla(st, bi)
        if "ffn0" not in skip:
            _ffn0(st, bi)
    if "l1" in st.skip and bi == st.b_loc - 1:
        _emit_output_from_xres(st)


def _gla(st, bi):
    st.b.phase_barrier()
    b = st.b
    nc = b.nc
    S = st.S
    NSC = st.NSC
    sk = st.sk
    xn_fm = st.cur["xn_fm"]
    lsk = ExitStack()

    def A(name, shape, dt):
        return lsk.enter_context(nc.sbuf_tensor(f"g{name}{bi}", shape, dt))

    NCC = SC // GC
    wall = A("wall", [P, DTC * D], BF16)
    qfm = A("qfm", [P, DTC * SC], BF16)
    kfm = A("kfm", [P, DTC * SC], BF16)
    ktok = A("ktok", [P, NCC * D], BF16)
    vtok = A("vtok", [P, NCC * D], BF16)
    logg = A("logg", [P, NCC * D], F32)
    gs1 = A("gs1", [P, D], F32)
    gs2 = A("gs2", [P, D], F32)
    Fg4 = A("Fg4", [P, NCC * D], BF16)
    G24 = A("G24", [P, NCC * D], BF16)
    pdb4 = A("pdb4", [P, NCC * D], BF16)
    vt4 = A("vt4", [P, NCC * D], BF16)
    state = A("state", [64, D], F32)
    tmps = A("tmps", [64, D], BF16)
    tmp2 = A("tmp2", [P, D], BF16)
    At = A("At", [P, H * GC], BF16)
    otok = A("otok", [P, D], BF16)
    ofm = A("ofm", [P, DTC * SC], BF16)
    attn_s = A("attns", [P, DTC * SC], BF16)
    st.cur.update(dict(wall=wall, qfm=qfm, kfm=kfm, ktok=ktok, vtok=vtok,
                       logg=logg, gs1=gs1, gs2=gs2, Fg4=Fg4, G24=G24,
                       pdb4=pdb4, vt4=vt4, state=state, tmps=tmps, tmp2=tmp2,
                       At=At, otok=otok, ofm=ofm, attn_s=attn_s, wall_rd=0))

    b.ve.drain()
    b.tick(b.ve.memset(state[:], 0.0), "ve")

    for sc in range(NSC):
        t0 = sc * SC
        _proj_fm(st, wall, st.s_wbf["L0_attn_Wq"], xn_fm, qfm, t0, bi)
        _proj_fm(st, wall, st.s_wbf["L0_attn_Wk"], xn_fm, kfm, t0, bi, also_tok=ktok)
        _proj_tok(st, wall, st.s_wbf["L0_attn_Wv"], xn_fm, vtok, t0, bi)
        _proj_tok(st, wall, st.s_wbf["L0_attn_Wg"], xn_fm, None, t0, bi,
                  logg=(logg, gs1, gs2))
        _gla_gates(st, sc)
        for cc in range(NCC):
            _gla_chunk(st, bi, sc, cc)
        _gla_wo(st, bi, sc)
    b.phase_barrier()
    lsk.close()


def _gla_gates(st, sc):
    b = st.b
    nc = b.nc
    c = st.cur
    logg, vtok = c["logg"], c["vtok"]
    Fg4, G24, pdb4, vt4 = c["Fg4"], c["G24"], c["pdb4"], c["vt4"]
    NCC = SC // GC
    with (nc.psum_tensor(_psname(st, "psm"), [P, D], F32) as pm0,
          nc.psum_tensor(_psname(st, "psm"), [P, D], F32) as pm1,
          nc.psum_tensor(_psname(st, "pdb"), [P, D], F32) as pdbp):
        pmp = [pm0, pm1]
        b.wait(b.se, "ve")
        b.wait(b.se, "pe")
        b.wait(b.pe, "se")
        b.wait(b.pe, "ve")
        exps_done = {}
        pdb_ev = None
        for cc in range(NCC):
            k = cc % 2
            if k in exps_done:
                b.pe.wait_ge(b.sem["se"], exps_done[k])
            for half in range(2):
                sl = slice(half * 512, (half + 1) * 512)
                mm = b.pe.matmul(
                    pmp[k][:, sl], st.m1c[:],
                    logg[:, cc * D + half * 512:cc * D + (half + 1) * 512],
                    start=True, stop=True)
            pv = b.tick(mm, "pe")
            b.wait(b.se, "pe", pv)
            b.se.drain()
            b.se.activation(Fg4[:, cc * D:(cc + 1) * D], pmp[k][:], AF.Exp)
            sg = b.tick(b.se.activation(G24[:, cc * D:(cc + 1) * D], pmp[k][:],
                                        AF.Exp, scale=-1.0), "se")
            exps_done[k] = sg
            b.wait(b.pe, "se", sg)
            if pdb_ev is not None:
                b.pe.wait_ge(b.sem["ve"], pdb_ev)
            for half in range(2):
                sl = slice(half * 512, (half + 1) * 512)
                mm = b.pe.matmul(
                    pdbp[:, sl], st.sel_bf[:],
                    G24[:, cc * D + half * 512:cc * D + (half + 1) * 512],
                    start=True, stop=True)
            pv = b.tick(mm, "pe")
            b.wait(b.ve, "pe", pv)
            b.ve.drain()
            b.ve.tensor_copy(pdb4[:, cc * D:(cc + 1) * D], pdbp[:])
            b.ve.drain()
            pdb_ev = b.tick(b.ve.tensor_mul(vt4[:, cc * D:(cc + 1) * D],
                                            vtok[:, cc * D:(cc + 1) * D],
                                            Fg4[:, cc * D:(cc + 1) * D]), "ve")


def _psname(st, base):
    st.uid += 1
    return f"{base}{st.uid}"


def _proj_fm(st, wall, Wdram, xn_fm, out_fm, t0, bi, also_tok=None):
    b = st.b
    nc = b.nc
    S = st.S
    b.wait(b.gp, "pe", st.cur["wall_rd"])
    vlast = b.loadc("wall", wall.ap().rearrange("p (jd e) -> p jd e", jd=DTC),
                    Wdram.ap().rearrange("(jd p) e -> p jd e", p=P))
    b.wait(b.pe, "wall", vlast)
    with (nc.psum_tensor(_psname(st, "pq"), [P, 2 * SC], F32) as p0,
          nc.psum_tensor(_psname(st, "pq"), [P, 2 * SC], F32) as p1):
        pp = [p0, p1]
        b.wait(b.pe, "ve")
        b.wait(b.pe, "se")
        ve_evac = {}
        for jq in range(0, DTC, 2):
            k = (jq // 2) % 2
            if k in ve_evac:
                b.pe.wait_ge(b.sem["ve"], ve_evac[k])
            for j2 in range(2):
                for jd in range(DTC):
                    mm = b.pe.matmul(
                        pp[k][:, j2 * SC:(j2 + 1) * SC],
                        wall[:, jd * D + (jq + j2) * P:jd * D + (jq + j2 + 1) * P],
                        xn_fm[:, jd * S + t0:jd * S + t0 + SC],
                        start=(jd == 0), stop=(jd == DTC - 1))
            pv = b.tick(mm, "pe")
            b.wait(b.ve, "pe", pv)
            ve_evac[k] = b.tick(
                b.ve.tensor_copy(out_fm[:, jq * SC:(jq + 2) * SC], pp[k][:]), "ve")
    st.cur["wall_rd"] = b.cnt["pe"]
    if also_tok is not None:
        _tok_mms(st, wall, xn_fm, also_tok, t0, None)
        st.cur["wall_rd"] = b.cnt["pe"]


def _proj_tok(st, wall, Wdram, xn_fm, out_tok, t0, bi, logg=None):
    b = st.b
    b.wait(b.gp, "pe", st.cur["wall_rd"])
    vlast = b.loadc("wall", wall.ap().rearrange("p (jd e) -> p jd e", jd=DTC),
                    Wdram.ap().rearrange("(jd p) e -> p jd e", p=P))
    b.wait(b.pe, "wall", vlast)
    _tok_mms(st, wall, xn_fm, out_tok, t0, logg)
    st.cur["wall_rd"] = b.cnt["pe"]


def _tok_mms(st, wall, xn_fm, out_tok, t0, logg):
    b = st.b
    nc = b.nc
    S = st.S
    NCC = SC // P
    with (nc.psum_tensor(_psname(st, "pk"), [P, D], F32) as p0,
          nc.psum_tensor(_psname(st, "pk"), [P, D], F32) as p1):
        pp = [p0, p1]
        b.wait(b.pe, "ve")
        b.wait(b.pe, "se")
        cons = {}
        for tt in range(NCC):
            k = tt % 2
            if k in cons:
                b.pe.wait_ge(b.sem[cons[k][0]], cons[k][1])
            for jd in range(DTC):
                lhs = xn_fm[:, jd * S + t0 + tt * P:jd * S + t0 + (tt + 1) * P]
                b.pe.matmul(pp[k][:, 0:512], lhs, wall[:, jd * D:jd * D + 512],
                            start=(jd == 0), stop=(jd == DTC - 1))
                mm = b.pe.matmul(pp[k][:, 512:D], lhs, wall[:, jd * D + 512:(jd + 1) * D],
                                 start=(jd == 0), stop=(jd == DTC - 1))
            pv = b.tick(mm, "pe")
            if logg is None:
                b.wait(b.ve, "pe", pv)
                cons[k] = ("ve", b.tick(
                    b.ve.tensor_copy(out_tok[:, tt * D:(tt + 1) * D], pp[k][:]), "ve"))
            else:
                lg, gs1, gs2 = logg
                b.wait(b.ve, "pe", pv)
                b.wait(b.ve, "se")
                b.ve.drain()
                vv = b.tick(b.ve.tensor_add(gs1[:], pp[k][:], st.bg_bc[:]), "ve")
                cons[k] = ("ve", vv)
                b.wait(b.se, "ve", vv)
                b.se.activation(gs2[:], gs1[:], AF.Sigmoid)
                b.se.drain()
                b.tick(b.se.activation(lg[:, tt * D:(tt + 1) * D], gs2[:], AF.Ln), "se")


def _gla_chunk(st, bi, sc, cc):
    b = st.b
    nc = b.nc
    c = st.cur
    qfm, kfm, ktok = c["qfm"], c["kfm"], c["ktok"]
    G24, pdb4, vt4 = c["G24"], c["pdb4"], c["vt4"]
    state, tmps, tmp2 = c["state"], c["tmps"], c["tmp2"]
    gs1, gs2 = c["gs1"], c["gs2"]
    At, otok, ofm = c["At"], c["otok"], c["ofm"]
    tq0 = cc * P
    vt = vt4[:, cc * D:(cc + 1) * D]
    G2 = G24[:, cc * D:(cc + 1) * D]
    pdbl = pdb4[0:64, cc * D:(cc + 1) * D]

    b.ve.drain()
    tv = b.tick(b.ve.tensor_mul(tmps[:], state[:], pdbl), "ve")
    with nc.psum_tensor(_psname(st, "pdu"), [P, D], F32) as pdup:
        b.wait(b.pe, "ve", tv)
        for half in range(2):
            sl = slice(half * 512, (half + 1) * 512)
            mm = b.pe.matmul(pdup[:, sl], st.dup_bf[:], tmps[:, sl],
                             start=True, stop=True)
        pv = b.tick(mm, "pe")
        b.wait(b.ve, "pe", pv)
        b.ve.drain()
        t2 = b.tick(b.ve.tensor_copy(tmp2[:], pdup[:]), "ve")
    with nc.psum_tensor(_psname(st, "pst"), [P, H * GC], F32) as pst:
        b.pe.wait_ge(b.sem["ve"], t2)
        for h in range(H):
            cp = (h % 2) * (H // 2) * GC + (h // 2) * GC
            kf = kfm[(h % 2) * 64:(h % 2) * 64 + 64,
                     (h // 2) * SC + tq0:(h // 2) * SC + tq0 + P]
            qf = qfm[(h % 2) * 64:(h % 2) * 64 + 64,
                     (h // 2) * SC + tq0:(h // 2) * SC + tq0 + P]
            mm = b.pe.matmul(pst[:, cp:cp + GC], kf, qf,
                             start=(h in (0, 1, 8, 9)), stop=(h in (6, 7, 14, 15)))
        pv = b.tick(mm, "pe")
        b.wait(b.ve, "pe", pv)
        b.ve.drain()
        av = b.tick(b.ve.tensor_mul(At[:], pst[:], st.maskc[:]), "ve")
    with (nc.psum_tensor(_psname(st, "poa"), [P, D], F32) as poa,
          nc.psum_tensor(_psname(st, "pob"), [P, D], F32) as pob):
        b.wait(b.pe, "ve", av)
        for h in range(H):
            mm = b.pe.matmul(poa[:, h * HD:(h + 1) * HD],
                             At[:, ((h % 2) * (H // 2) + h // 2) * GC:
                                ((h % 2) * (H // 2) + h // 2 + 1) * GC],
                             vt[:, h * HD:(h + 1) * HD],
                             start=(h % 8 == 0), stop=(h % 8 == 7))
        for h in range(H):
            ip = (h % 2) * 512 + (h // 2) * HD
            qf = qfm[(h % 2) * 64:(h % 2) * 64 + 64,
                     (h // 2) * SC + tq0:(h // 2) * SC + tq0 + P]
            mm2 = b.pe.matmul(pob[:, ip:ip + HD], qf,
                              tmp2[(h % 2) * 64:(h % 2) * 64 + 64,
                                   h * HD:(h + 1) * HD],
                              start=(h in (0, 1)), stop=(h in (14, 15)))
        pv = b.tick(mm2, "pe")
        b.wait(b.ve, "pe", pv)
        b.ve.drain()
        pob3 = pob.ap().rearrange("p (par h2 e) -> p par h2 e", par=2, e=HD)
        gsc3 = gs2.ap().rearrange("p (h2 par e) -> p par h2 e", par=2, e=HD)
        b.ve.tensor_copy(gsc3, pob3)
        b.ve.drain()
        b.ve.tensor_add(gs1[:], poa[:], gs2[:])
        b.ve.drain()
        b.tick(b.ve.tensor_mul(otok[:], gs1[:], G2[:]), "ve")
        with nc.psum_tensor(_psname(st, "pu"), [64, D], F32) as pu:
            for half in range(2):
                sl = slice(half * 512, (half + 1) * 512)
                b.pe.matmul(pu[:, sl], st.ident_bf[0:64, 0:64], tmps[:, sl],
                            start=True, stop=False)
            for h in range(H):
                mm = b.pe.matmul(pu[:, h * HD:(h + 1) * HD],
                                 ktok[:, cc * D + h * HD:cc * D + (h + 1) * HD],
                                 vt[:, h * HD:(h + 1) * HD],
                                 start=False, stop=(h % 8 == 7))
            pv = b.tick(mm, "pe")
            b.wait(b.ve, "pe", pv)
            b.ve.drain()
            b.tick(b.ve.tensor_mul(state[:], pu[:], pdbl), "ve")
    with (nc.psum_tensor(_psname(st, "ptg"), [P, 512], BF16) as pt0,
          nc.psum_tensor(_psname(st, "ptg"), [P, 512], BF16) as pt1):
        pts = [pt0, pt1]
        b.wait(b.pe, "ve")
        ve_evac = {}
        ofm3 = ofm.ap().rearrange("p (j s) -> p j s", j=DTC)
        for jb in range(2):
            k = jb % 2
            if k in ve_evac:
                b.pe.wait_ge(b.sem["ve"], ve_evac[k])
            for j4 in range(4):
                j = jb * 4 + j4
                mm = b.pe.transpose(pts[k][:, j4 * P:(j4 + 1) * P],
                                    otok[:, j * P:(j + 1) * P], st.ident_bf[:])
            pv = b.tick(mm, "pe")
            b.wait(b.ve, "pe", pv)
            ve_evac[k] = b.tick(b.ve.tensor_copy(
                ofm3[:, jb * 4:(jb + 1) * 4, tq0:tq0 + P],
                pts[k].ap().rearrange("p (j c) -> p j c", j=4)), "ve")


def _gla_wo(st, bi, sc):
    b = st.b
    nc = b.nc
    c = st.cur
    S = st.S
    NTB = st.NTB
    wall, ofm, attn_s = c["wall"], c["ofm"], c["attn_s"]
    b.wait(b.gp, "pe", c["wall_rd"])
    vlast = b.loadc("wall", wall.ap().rearrange("p (jd e) -> p jd e", jd=DTC),
                    st.s_wbf["L0_attn_Wo"].ap().rearrange("(jd p) e -> p jd e", p=P))
    b.wait(b.pe, "wall", vlast)
    b.wait(b.pe, "ve")
    b.wait(b.pe, "se")
    with (nc.psum_tensor(_psname(st, "pwo"), [P, 2 * SC], F32) as p0,
          nc.psum_tensor(_psname(st, "pwo"), [P, 2 * SC], F32) as p1):
        pp = [p0, p1]
        ve_evac = {}
        for jo in range(0, DTC, 2):
            k = (jo // 2) % 2
            if k in ve_evac:
                b.pe.wait_ge(b.sem["ve"], ve_evac[k])
            for j2 in range(2):
                for je in range(DTC):
                    mm = b.pe.matmul(
                        pp[k][:, j2 * SC:(j2 + 1) * SC],
                        wall[:, je * D + (jo + j2) * P:je * D + (jo + j2 + 1) * P],
                        ofm[:, je * SC:(je + 1) * SC],
                        start=(je == 0), stop=(je == DTC - 1))
            pv = b.tick(mm, "pe")
            b.wait(b.ve, "pe", pv)
            ve_evac[k] = b.tick(
                b.ve.tensor_copy(attn_s[:, jo * SC:(jo + 2) * SC], pp[k][:]), "ve")
        c["wall_rd"] = b.cnt["pe"]
    with nc.psum_tensor(_psname(st, "prt"), [P, D], BF16) as prt:
        for tt in range(SC // P):
            b.wait(b.pe, "ve")
            for jo in range(DTC):
                mm = b.pe.transpose(prt[:, jo * P:(jo + 1) * P],
                                    attn_s[:, jo * SC + tt * P:jo * SC + (tt + 1) * P],
                                    st.ident_bf[:])
            pv = b.tick(mm, "pe")
            b.wait(b.ve, "pe", pv)
            gt = bi * NTB + sc * (SC // P) + tt
            b.ve.drain()
            b.tick(b.ve.tensor_add(st.xres[:, gt * D:(gt + 1) * D],
                                   st.xres[:, gt * D:(gt + 1) * D], prt[:]), "ve")


def _ffn0(st, bi):
    st.b.phase_barrier()
    b = st.b
    nc = b.nc
    S = st.S
    NTB = st.NTB
    sk = st.sk
    xn_fm = st.cur["xn_fm"]

    def A(name, shape, dt):
        return sk.enter_context(nc.sbuf_tensor(f"f0{name}{bi}", shape, dt))

    wf1 = [A(f"w1_{i}", [P, 2 * D], BF16) for i in (0, 1)]
    wf2 = [A(f"w2_{i}", [P, 2 * D], BF16) for i in (0, 1)]
    stash = A("stash", [P, FT_ * 512], BF16)
    fstage = [A(f"fst{i}", [P, 2048], BF16) for i in (0, 1)]
    prev = {}
    prev2 = {}
    for ob in range(S // 512):
        t0 = ob * 512
        with (nc.psum_tensor(_psname(st, "ph"), [P, 512], F32) as ph0,
              nc.psum_tensor(_psname(st, "ph"), [P, 512], F32) as ph1):
            php = [ph0, ph1]
            b.wait(b.pe, "ve")
            b.wait(b.pe, "se")
            prev_se = {}
            for jf2 in range(FT_ // 2):
                k = jf2 % 2
                if k in prev:
                    b.sq.wait_ge(b.sem["pe"], prev[k][1])
                vw = b.load(f"wf1{k}",
                            wf1[k].ap().rearrange("p (j e) -> p j e", j=2),
                            st.s_w1t0.ap().rearrange(
                                "f p e -> p f e")[:, 2 * jf2:2 * jf2 + 2, :])
                b.wait(b.pe, f"wf1{k}", vw)
                for j2 in range(2):
                    jf = 2 * jf2 + j2
                    kp = jf % 2
                    if kp in prev_se:
                        b.pe.wait_ge(b.sem["se"], prev_se[kp])
                    for jd in range(DTC):
                        mm = b.pe.matmul(
                            php[kp][:], wf1[k][:, j2 * D + jd * P:j2 * D + (jd + 1) * P],
                            xn_fm[:, jd * S + t0:jd * S + t0 + 512],
                            start=(jd == 0), stop=(jd == DTC - 1))
                    pv = b.tick(mm, "pe")
                    b.wait(b.se, "pe", pv)
                    sv = b.tick(b.se.activation(
                        stash[:, jf * 512:(jf + 1) * 512], php[kp][:],
                        AF.Gelu, bias=st.b1c0[:, jf:jf + 1]), "se")
                    prev_se[kp] = sv
                prev[k] = (sv, pv)
        with (nc.psum_tensor(_psname(st, "pob"), [P, 2048], F32) as pob0,
              nc.psum_tensor(_psname(st, "pob"), [P, 2048], F32) as pob1):
            pobs = [pob0, pob1]
            b.wait(b.pe, "se")
            b.wait(b.pe, "ve")
            for jf2 in range(FT_ // 2):
                k = jf2 % 2
                if k in prev2:
                    b.sq.wait_ge(b.sem["pe"], prev2[k])
                vw = b.load(f"wf2{k}",
                            wf2[k].ap().rearrange("p (j e) -> p j e", j=2),
                            st.s_w2t0.ap().rearrange(
                                "f p e -> p f e")[:, 2 * jf2:2 * jf2 + 2, :])
                b.wait(b.pe, f"wf2{k}", vw)
                for j2 in range(2):
                    jf = 2 * jf2 + j2
                    for sub in range(2):
                        for jd in range(DTC):
                            mm = b.pe.matmul(
                                pobs[sub][:, jd * 256:(jd + 1) * 256],
                                wf2[k][:, j2 * D + jd * P:j2 * D + (jd + 1) * P],
                                stash[:, jf * 512 + sub * 256:jf * 512 + (sub + 1) * 256],
                                start=(jf == 0 and jd % 2 == 0),
                                stop=(jf == FT_ - 1 and jd % 2 == 1))
                prev2[k] = b.tick(mm, "pe")
            b.wait(b.se, "pe")
            for sub in range(2):
                for jd in range(DTC):
                    sv = b.tick(b.se.activation(
                        fstage[sub][:, jd * 256:(jd + 1) * 256],
                        pobs[sub][:, jd * 256:(jd + 1) * 256],
                        AF.Identity, bias=st.b2c0[:, jd:jd + 1]), "se")
        with nc.psum_tensor(_psname(st, "prf"), [P, D], BF16) as prf:
            b.wait(b.pe, "se", sv)
            for sub in range(2):
                for tt in range(2):
                    b.wait(b.pe, "ve")
                    for jd in range(DTC):
                        mm = b.pe.transpose(
                            prf[:, jd * P:(jd + 1) * P],
                            fstage[sub][:, jd * 256 + tt * P:jd * 256 + (tt + 1) * P],
                            st.ident_bf[:])
                    pv = b.tick(mm, "pe")
                    b.wait(b.ve, "pe", pv)
                    gt = bi * NTB + (t0 + sub * 256 + tt * P) // P
                    b.ve.drain()
                    b.tick(b.ve.tensor_add(st.xres[:, gt * D:(gt + 1) * D],
                                           st.xres[:, gt * D:(gt + 1) * D], prf[:]), "ve")


def _emit_output_from_xres(st):
    b = st.b
    nc = b.nc
    NT = st.NT
    with ExitStack() as sk:
        ot = sk.enter_context(nc.sbuf_tensor("odump", [P, D], F32))
        for t in range(NT):
            if t:
                b.wait(b.ve, "out")
            b.ve.drain()
            vv = b.tick(b.ve.tensor_copy(
                ot[:], st.xres[:, t * D:(t + 1) * D]), "ve")
            b.sq.wait_ge(b.sem["ve"], vv)
            bi, tt = divmod(t, st.NTB)
            b.store("out", st.x_out.ap()[bi, tt * P:(tt + 1) * P, :], ot[:])


def _layer1(st, bi):
    b = st.b
    nc = b.nc
    S = st.S
    with ExitStack() as sk:
        st.sk = sk
        st.cur = {}
        with ExitStack() as skA:
            st.cur["xn_fm"] = skA.enter_context(
                nc.sbuf_tensor(f"xnfm1{bi}", [P, DTC * S], BF16))
            saved, st.sk = st.sk, skA
            _ln_tiles(st, bi, 1)
            if "attn1" not in st.skip:
                _attn1_proj(st, bi)
                _attn1_core(st, bi)
            st.sk = saved
        with ExitStack() as skB:
            st.cur["xq_fm"] = skB.enter_context(
                nc.sbuf_tensor(f"xqfm{bi}", [P, DTC * S], BF16))
            saved, st.sk = st.sk, skB
            if "ffn1" not in st.skip:
                _ffn1(st, bi)
            else:
                _merge_no_ffn(st, bi)
            st.sk = saved


def _merge_no_ffn(st, bi):
    b = st.b
    b.phase_barrier()
    nc = b.nc
    NTB = st.NTB
    with ExitStack() as sk:
        o2l = sk.enter_context(nc.sbuf_tensor(f"no2l{bi}", [P, D], BF16))
        o2f = sk.enter_context(nc.sbuf_tensor(f"no2f{bi}", [P, D], BF16))
        wo = [sk.enter_context(nc.sbuf_tensor(f"nwo{bi}{i}", [P, D], BF16))
              for i in (0, 1)]
        outf = sk.enter_context(nc.sbuf_tensor(f"nouf{bi}", [P, D], F32))
        prevw = {}
        for t in range(NTB):
            if "atr" in b.cnt:
                b.wait(b.sq, "pe")
            b.wait(b.sq, "atw")
            vf = b.load("atr", o2l[:], st.s_atok.ap()[bi, t * P:(t + 1) * P, :])
            with (nc.psum_tensor(_psname(st, "nmt"), [P, D], BF16) as ptr,
                  nc.psum_tensor(_psname(st, "nmb"), [P, D], F32) as pbs):
                b.wait(b.pe, "ve")
                b.wait(b.pe, "atr", vf)
                for j in range(DTC):
                    mm = b.pe.transpose(ptr[:, j * P:(j + 1) * P],
                                        o2l[:, j * P:(j + 1) * P], st.ident_bf[:])
                pv = b.tick(mm, "pe")
                b.wait(b.ve, "pe", pv)
                b.ve.drain()
                vv = b.tick(b.ve.tensor_copy(o2f[:], ptr[:]), "ve")
                b.wait(b.pe, "ve", vv)
                for je in range(DTC):
                    k = je % 2
                    if k in prevw:
                        b.sq.wait_ge(b.sem["pe"], prevw[k])
                    vw = b.load(f"wo{k}", wo[k][:],
                                st.s_wT["L1_attn_Wo"].ap()[je * P:(je + 1) * P, :])
                    b.wait(b.pe, f"wo{k}", vw)
                    for half in range(2):
                        mm = b.pe.matmul(pbs[:, half * 512:(half + 1) * 512],
                                         o2f[:, je * P:(je + 1) * P],
                                         wo[k][:, half * 512:(half + 1) * 512],
                                         start=(je == 0), stop=(je == DTC - 1))
                    prevw[k] = b.tick(mm, "pe")
                b.wait(b.ve, "pe")
                if "out" in b.cnt:
                    b.wait(b.ve, "out")
                b.ve.drain()
                gtx = bi * NTB + t
                vv = b.tick(b.ve.tensor_add(outf[:], pbs[:],
                                            st.xres[:, gtx * D:(gtx + 1) * D]), "ve")
                b.sq.wait_ge(b.sem["ve"], vv)
                b.store("out", st.x_out.ap()[bi, t * P:(t + 1) * P, :], outf[:])


def _ffn1(st, bi):
    st.b.phase_barrier()
    b = st.b
    nc = b.nc
    S = st.S
    NTB = st.NTB
    sk = st.sk

    def A(name, shape, dt):
        return sk.enter_context(nc.sbuf_tensor(f"f1{name}{bi}", shape, dt))

    xq_fm = st.cur["xq_fm"]
    xt = A("xt", [P, D], F32)
    sq = A("sq", [P, D], F32)
    stat = A("st", [P, 4], F32)
    rbf = A("rbf", [P, D], BF16)
    rf32 = A("rf32", [P, D], F32) if FP64_SIM else sq
    q1s = A("q1s", [P, D], BF16)
    with (nc.psum_tensor(_psname(st, "pfl"), [P, 512], BF16) as pt0,
          nc.psum_tensor(_psname(st, "pfl"), [P, 512], BF16) as pt1):
        pts = [pt0, pt1]
        b.wait(b.pe, "ve")
        b.wait(b.pe, "se")
        ve_evac = {}
        for t in range(NTB):
            gt = bi * NTB + t
            b.ve.drain()
            b.wait(b.ve, "se")
            b.ve.tensor_copy(xt[:], st.xres[:, gt * D:(gt + 1) * D])
            b.ve.drain()
            b.ve.reduce_sum(stat[:, 0:1], xt[:], axis=AX.X)
            b.ve.drain()
            b.ve.tensor_scalar_mul(stat[:, 1:2], stat[:, 0:1], -1.0 / D)
            b.ve.drain()
            b.ve.scalar_tensor_tensor(sq[:], xt[:], stat[:, 1:2], xt[:],
                                      ALU.add, ALU.mult, accum_out=stat[:, 0:1])
            b.ve.drain()
            b.ve.tensor_scalar(stat[:, 2:3], stat[:, 0:1], 1.0 / D, LN_EPS,
                               ALU.mult, ALU.add)
            b.ve.drain()
            vv = b.tick(b.ve.reciprocal(stat[:, 3:4], stat[:, 2:3]), "ve")
            b.wait(b.se, "ve", vv)
            sv = b.tick(b.se.activation(stat[:, 2:3], stat[:, 3:4], AF.Sqrt), "se")
            b.wait(b.ve, "se", sv)
            b.ve.tensor_scalar(sq[:], xt[:], stat[:, 1:2], stat[:, 2:3],
                               ALU.add, ALU.mult)
            b.ve.drain()
            b.ve.reduce_max(stat[:, 0:1], sq[:], axis=AX.X, apply_absolute_value=True)
            b.ve.drain()
            b.ve.tensor_single_scalar(stat[:, 0:1], stat[:, 0:1], 1e-5, ALU.max)
            b.ve.drain()
            b.ve.reciprocal(stat[:, 1:2], stat[:, 0:1])
            b.ve.drain()
            b.ve.tensor_scalar_mul(stat[:, 1:2], stat[:, 1:2], 127.0)
            b.ve.drain()
            b.ve.tensor_scalar(stat[:, 2:3], stat[:, 0:1], st.wscol[0][:],
                               1.0 / 127.0, ALU.mult, ALU.mult)
            b.ve.drain()
            b.tick(b.ve.tensor_scalar(sq[:], sq[:], stat[:, 1:2], None, ALU.mult), "ve")
            b.emit_round(sq[:], sq[:], xt, rbf, rf32, D)
            b.ve.drain()
            b.ve.tensor_scalar(q1s[:], sq[:], stat[:, 2:3], None, ALU.mult)
            vq = b.tick(b.ve.drain(), "ve")
            b.wait(b.pe, "ve", vq)
            xq3 = xq_fm.ap().rearrange("p (j s) -> p j s", j=DTC)
            for jb in range(2):
                k = jb % 2
                if k in ve_evac:
                    b.pe.wait_ge(b.sem["ve"], ve_evac[k])
                for j4 in range(4):
                    j = jb * 4 + j4
                    mm = b.pe.transpose(pts[k][:, j4 * P:(j4 + 1) * P],
                                        q1s[:, j * P:(j + 1) * P], st.ident_bf[:])
                pv = b.tick(mm, "pe")
                b.wait(b.ve, "pe", pv)
                ve_evac[k] = b.tick(b.ve.tensor_copy(
                    xq3[:, jb * 4:(jb + 1) * 4, t * P:(t + 1) * P],
                    pts[k].ap().rearrange("p (j c) -> p j c", j=4)), "ve")
    b.wait(b.pe, "ve")

    wfw = [A(f"wf_{i}", [P, 2 * D], BF16) for i in (0, 1)]
    hfm2 = [A(f"hfm{i}", [P, 512], BF16) for i in (0, 1)]
    ght = A("ght", [P, 4 * FF], BF16)
    scf = A("scf", [P, 2048], F32)
    scs = A("scs", [P, 2048], F32)
    scb = A("scb", [P, 2048], BF16)
    scv = A("scv", [P, 2048], F32) if FP64_SIM else scs
    sca = A("sca", [P, 2048], F32)
    q2s = A("q2s", [P, FF], BF16)
    qst2 = [A(f"qst{i}", [P, 4], F32) for i in (0, 1)]
    o2l = A("o2l", [P, D], BF16)
    o2f4 = A("o2f4", [P, 4 * D], BF16)
    prev = {}
    prevw = {}
    prevo = {}
    prev_trq = 0
    for ob in range(S // 512):
        t0 = ob * 512
        with (nc.psum_tensor(_psname(st, "p1h"), [P, 512], F32) as ph0,
              nc.psum_tensor(_psname(st, "p1h"), [P, 512], F32) as ph1,
              nc.psum_tensor(_psname(st, "p1t"), [P, 512], BF16) as pt0,
              nc.psum_tensor(_psname(st, "p1t"), [P, 512], BF16) as pt1):
            php = [ph0, ph1]
            ptp = [pt0, pt1]
            b.wait(b.pe, "ve")
            b.wait(b.pe, "se")
            tr_evac = {}
            ght3 = ght.ap().rearrange("p (t f) -> p t f", t=4)

            def emit_tr(jfp, svp):
                b.wait(b.pe, "se", svp)
                kk = jfp % 2
                if kk in tr_evac:
                    b.pe.wait_ge(b.sem["ve"], tr_evac[kk])
                for tt in range(4):
                    mm = b.pe.transpose(
                        ptp[kk][:, tt * P:(tt + 1) * P],
                        hfm2[jfp % 2][:, tt * P:(tt + 1) * P], st.ident_bf[:])
                pv2 = b.tick(mm, "pe")
                b.wait(b.ve, "pe", pv2)
                tr_evac[kk] = b.tick(b.ve.tensor_copy(
                    ght3[:, :, jfp * P:(jfp + 1) * P],
                    ptp[kk].ap().rearrange("p (t c) -> p t c", t=4)), "ve")
            pend = None
            prev_se = {}
            for jf2 in range(FT_ // 2):
                k = jf2 % 2
                if k in prev:
                    b.sq.wait_ge(b.sem["pe"], prev[k][1])
                if k in prevo:
                    b.sq.wait_ge(b.sem["pe"], prevo[k])
                vw = b.load(f"wf1{k}",
                            wfw[k].ap().rearrange("p (j e) -> p j e", j=2),
                            st.s_w1t1.ap().rearrange(
                                "f p e -> p f e")[:, 2 * jf2:2 * jf2 + 2, :])
                b.wait(b.pe, f"wf1{k}", vw)
                for j2 in range(2):
                    jf = 2 * jf2 + j2
                    kp = jf % 2
                    if kp in prev_se:
                        b.pe.wait_ge(b.sem["se"], prev_se[kp])
                    for jd in range(DTC):
                        mm = b.pe.matmul(
                            php[kp][:], wfw[k][:, j2 * D + jd * P:j2 * D + (jd + 1) * P],
                            xq_fm[:, jd * S + t0:jd * S + t0 + 512],
                            start=(jd == 0), stop=(jd == DTC - 1))
                    pv = b.tick(mm, "pe")
                    b.wait(b.se, "pe", pv)
                    sv = b.tick(b.se.activation(hfm2[kp][:], php[kp][:], AF.Gelu,
                                                bias=st.b1c1[:, jf:jf + 1]), "se")
                    prev_se[kp] = sv
                    if pend is not None:
                        emit_tr(*pend)
                    pend = (jf, sv)
                prev[k] = (sv, pv)
            emit_tr(*pend)
        def emit_red(tt):
            gt_sl = ght[:, tt * FF:(tt + 1) * FF]
            qst = qst2[tt % 2]
            b.ve.drain()
            b.wait(b.ve, "se")
            b.ve.reduce_max(qst[:, 0:1], gt_sl[:, 0:2048], axis=AX.X,
                            apply_absolute_value=True)
            b.ve.drain()
            b.ve.reduce_max(qst[:, 1:2], gt_sl[:, 2048:], axis=AX.X,
                            apply_absolute_value=True)
            b.ve.drain()
            b.ve.tensor_tensor(qst[:, 0:1], qst[:, 0:1], qst[:, 1:2], ALU.max)
            b.ve.drain()
            b.ve.tensor_single_scalar(qst[:, 0:1], qst[:, 0:1], 1e-5, ALU.max)
            b.ve.drain()
            b.ve.reciprocal(qst[:, 1:2], qst[:, 0:1])
            b.ve.drain()
            b.ve.tensor_scalar_mul(qst[:, 1:2], qst[:, 1:2], 127.0)
            b.ve.drain()
            b.ve.tensor_scalar(qst[:, 2:3], qst[:, 0:1], st.wscol[1][:],
                               1.0 / 127.0, ALU.mult, ALU.mult)
            b.tick(b.ve.drain(), "ve")

        def emit_sr(tt, tr_gate):
            gt_sl = ght[:, tt * FF:(tt + 1) * FF]
            qst = qst2[tt % 2]
            qv = 0
            for hh in range(2):
                hsl = slice(hh * 2048, (hh + 1) * 2048)
                b.wait(b.se, "ve")
                if tr_gate:
                    b.se.wait_ge(b.sem["pe"], tr_gate)
                if "out" in b.cnt:
                    b.wait(b.se, "out")
                b.tick(b.se.activation(sca[:], gt_sl[:, hsl], AF.Identity,
                                       scale=qst[:, 1:2]), "se")
                b.se.drain()
                b.wait(b.ve, "se")
                b.emit_round(scf[:], sca[:], scs, scb, scv, 2048)
                b.wait(b.se, "ve")
                qv = b.tick(b.se.activation(q2s[:, hh * 2048:(hh + 1) * 2048],
                                            scf[:], AF.Identity,
                                            scale=qst[:, 2:3]), "se")
            return qv

        def emit_trq(tt, qv):
            with (nc.psum_tensor(_psname(st, "p1u"), [P, 512], BF16) as pu0,
                  nc.psum_tensor(_psname(st, "p1u"), [P, 512], BF16) as pu1):
                pup = [pu0, pu1]
                b.wait(b.pe, "se", qv)
                ev = {}
                pv = 0
                for jb in range(FT_ // 4):
                    k = jb % 2
                    if k in ev:
                        b.pe.wait_ge(b.sem["ve"], ev[k])
                    for j4 in range(4):
                        jf = jb * 4 + j4
                        mm = b.pe.transpose(pup[k][:, j4 * P:(j4 + 1) * P],
                                            q2s[:, jf * P:(jf + 1) * P],
                                            st.ident_bf[:])
                    pv = b.tick(mm, "pe")
                    b.wait(b.ve, "pe", pv)
                    ev[k] = b.tick(b.ve.tensor_copy(
                        ght[:, tt * FF + jb * 512:tt * FF + (jb + 1) * 512],
                        pup[k][:]), "ve")
            return pv

        qpend = None
        for tt in range(4):
            emit_red(tt)
            if qpend is not None:
                tr_gate = emit_trq(*qpend)
            else:
                tr_gate = prev_trq
            qv = emit_sr(tt, tr_gate)
            qpend = (tt, qv)
        prev_trq = emit_trq(*qpend)
        if "attn1" not in st.skip:
            with (nc.psum_tensor(_psname(st, "p1r"), [P, D], BF16) as pr0,
                  nc.psum_tensor(_psname(st, "p1r"), [P, D], BF16) as pr1):
                prp = [pr0, pr1]
                b.wait(b.pe, "ve")
                b.wait(b.pe, "se")
                trv = {}
                for tt in range(4):
                    k = tt % 2
                    if "atr" in b.cnt:
                        b.wait(b.sq, "pe")
                    b.wait(b.sq, "atw")
                    vf = b.load("atr", o2l[:],
                                st.s_atok.ap()[bi, (ob * 4 + tt) * P:
                                               (ob * 4 + tt + 1) * P, :])
                    b.wait(b.pe, "atr", vf)
                    if k in trv:
                        b.pe.wait_ge(b.sem["ve"], trv[k])
                    for j in range(DTC):
                        mm = b.pe.transpose(prp[k][:, j * P:(j + 1) * P],
                                            o2l[:, j * P:(j + 1) * P],
                                            st.ident_bf[:])
                    pv = b.tick(mm, "pe")
                    b.wait(b.ve, "pe", pv)
                    b.ve.drain()
                    trv[k] = b.tick(b.ve.tensor_copy(
                        o2f4[:, tt * D:(tt + 1) * D], prp[k][:]), "ve")
        with (nc.psum_tensor(_psname(st, "p1o0"), [P, D], F32) as po0,
              nc.psum_tensor(_psname(st, "p1o1"), [P, D], F32) as po1,
              nc.psum_tensor(_psname(st, "p1o2"), [P, D], F32) as po2,
              nc.psum_tensor(_psname(st, "p1o3"), [P, D], F32) as po3):
            pos = [po0, po1, po2, po3]
            b.wait(b.pe, "ve")
            b.wait(b.pe, "se")
            for jf2 in range(FT_ // 2):
                k = jf2 % 2
                if k in prevw:
                    b.sq.wait_ge(b.sem["pe"], prevw[k])
                b.sq.wait_ge(b.sem["pe"], prev[k][1])
                vw = b.load(f"wf2{k}",
                            wfw[k].ap().rearrange("p (j e) -> p j e", j=2),
                            st.s_w2t1.ap().rearrange(
                                "f p e -> p f e")[:, 2 * jf2:2 * jf2 + 2, :])
                b.wait(b.pe, f"wf2{k}", vw)
                for j2 in range(2):
                    jf = 2 * jf2 + j2
                    for tt in range(4):
                        for half in range(2):
                            mm = b.pe.matmul(
                                pos[tt][:, half * 512:(half + 1) * 512],
                                ght[:, tt * FF + jf * P:tt * FF + (jf + 1) * P],
                                wfw[k][:, j2 * D + half * 512:j2 * D + (half + 1) * 512],
                                start=(jf == 0), stop=False)
                prevw[k] = b.tick(mm, "pe")
            if "attn1" not in st.skip:
                for je2 in range(DTC // 2):
                    k = je2 % 2
                    if k in prevo:
                        b.sq.wait_ge(b.sem["pe"], prevo[k])
                    b.sq.wait_ge(b.sem["pe"], prevw[k])
                    vw = b.load(f"wo{k}", wfw[k][:],
                                st.s_wT["L1_attn_Wo"].ap()[2 * je2 * P:
                                                           2 * (je2 + 1) * P, :])
                    b.wait(b.pe, f"wo{k}", vw)
                    for j2 in range(2):
                        je = 2 * je2 + j2
                        for tt in range(4):
                            for half in range(2):
                                mm = b.pe.matmul(
                                    pos[tt][:, half * 512:(half + 1) * 512],
                                    o2f4[:, tt * D + je * P:tt * D + (je + 1) * P],
                                    wfw[k][:, j2 * D + half * 512:j2 * D + (half + 1) * 512],
                                    start=False, stop=False)
                    prevo[k] = b.tick(mm, "pe")
            for tt in range(4):
                gt = bi * NTB + ob * 4 + tt
                for half in range(2):
                    mm = b.pe.matmul(pos[tt][:, half * 512:(half + 1) * 512],
                                     st.ones_row[:],
                                     st.b2r1[:, half * 512:(half + 1) * 512],
                                     start=False, stop=True)
                pv = b.tick(mm, "pe")
                b.wait(b.ve, "pe", pv)
                if "out" in b.cnt:
                    b.wait(b.ve, "out")
                b.ve.drain()
                osl = sca[:, (tt % 2) * D:(tt % 2 + 1) * D]
                vv = b.tick(b.ve.tensor_add(osl, pos[tt][:],
                                            st.xres[:, gt * D:(gt + 1) * D]), "ve")
                b.sq.wait_ge(b.sem["ve"], vv)
                b.store("out", st.x_out.ap()[bi, (ob * 4 + tt) * P:
                                             (ob * 4 + tt + 1) * P, :], osl)


def _attn1_proj(st, bi):
    st.b.phase_barrier()
    b = st.b
    nc = b.nc
    S = st.S
    xn_fm = st.cur["xn_fm"]
    sk = st.sk

    def A(name, shape, dt):
        return sk.enter_context(nc.sbuf_tensor(f"a1{name}{bi}", shape, dt))

    kfm = A("kfm", [P, DTC * S], BF16)
    vtok = A("vtok", [P, (S // P) * H * 65], BF16)
    wall = A("wall", [P, DTC * D], BF16)
    st.cur.update(dict(kfm=kfm, vtok=vtok, wqall=wall))

    NTB = st.NTB
    b.ve.drain()
    b.tick(b.ve.memset(
        vtok.ap().rearrange("p (t h e) -> p t h e", t=NTB, h=H)[:, :, :, 64:65], 1.0), "ve")

    b.wait(b.sq, "pe")
    vlast = b.load("wallh", wall.ap().rearrange("p (jd e) -> p jd e", jd=DTC),
                   st.s_wT["L1_attn_Wk"].ap().rearrange("(jd p) e -> p jd e", p=P))
    b.wait(b.pe, "wallh", vlast)
    with (nc.psum_tensor(_psname(st, "pa"), [P, 512], F32) as p0,
          nc.psum_tensor(_psname(st, "pa"), [P, 512], F32) as p1):
        pp = [p0, p1]
        b.wait(b.pe, "ve")
        b.wait(b.pe, "se")
        ev = {}
        idx = 0
        for je in range(DTC):
            for tc in range(S // 512):
                k = idx % 2
                idx += 1
                if k in ev:
                    b.pe.wait_ge(b.sem["ve"], ev[k])
                for jd in range(DTC):
                    mm = b.pe.matmul(
                        pp[k][:], wall[:, jd * D + je * P:jd * D + (je + 1) * P],
                        xn_fm[:, jd * S + tc * 512:jd * S + (tc + 1) * 512],
                        start=(jd == 0), stop=(jd == DTC - 1))
                pv = b.tick(mm, "pe")
                b.wait(b.ve, "pe", pv)
                ev[k] = b.tick(b.ve.tensor_copy(
                    kfm[:, je * S + tc * 512:je * S + (tc + 1) * 512], pp[k][:]), "ve")
    b.wait(b.sq, "pe")
    vlast = b.load("wallh", wall.ap().rearrange("p (jd e) -> p jd e", jd=DTC),
                   st.s_wT["L1_attn_Wv"].ap().rearrange("(jd p) e -> p jd e", p=P))
    b.wait(b.pe, "wallh", vlast)
    with (nc.psum_tensor(_psname(st, "pa"), [P, D], F32) as p0,
          nc.psum_tensor(_psname(st, "pa"), [P, D], F32) as p1):
        pp = [p0, p1]
        b.wait(b.pe, "ve")
        b.wait(b.pe, "se")
        ev = {}
        for tt in range(NTB):
            k = tt % 2
            if k in ev:
                b.pe.wait_ge(b.sem["ve"], ev[k])
            for jd in range(DTC):
                lhs = xn_fm[:, jd * S + tt * P:jd * S + (tt + 1) * P]
                b.pe.matmul(pp[k][:, 0:512], lhs, wall[:, jd * D:jd * D + 512],
                            start=(jd == 0), stop=(jd == DTC - 1))
                mm = b.pe.matmul(pp[k][:, 512:D], lhs,
                                 wall[:, jd * D + 512:(jd + 1) * D],
                                 start=(jd == 0), stop=(jd == DTC - 1))
            pv = b.tick(mm, "pe")
            b.wait(b.ve, "pe", pv)
            dst = vtok.ap().rearrange("p (t h e) -> p t h e", t=NTB, h=H)[
                :, tt, :, 0:64]
            ev[k] = b.tick(b.ve.tensor_copy(
                dst, pp[k].ap().rearrange("p (h e) -> p h e", h=H)), "ve")
    b.wait(b.sq, "pe")
    vlast = b.load("wallh", wall.ap().rearrange("p (jd e) -> p jd e", jd=DTC),
                   st.s_wT["L1_attn_Wq"].ap().rearrange("(jd p) e -> p jd e", p=P))
    st.cur["wq_v"] = vlast


def _attn1_core(st, bi):
    st.b.phase_barrier()
    b = st.b
    nc = b.nc
    S = st.S
    NTB = st.NTB
    NKT = S // P
    NQC = S // QCH
    sk = st.sk
    kfm, vtok, wall = st.cur["kfm"], st.cur["vtok"], st.cur["wqall"]
    b.wait(b.pe, "wallh", st.cur["wq_v"])

    def A(name, shape, dt):
        return sk.enter_context(nc.sbuf_tensor(f"c1{name}{bi}", shape, dt))

    qch = A("qch", [P, DTC * QCH], BF16)
    At2 = [A(f"At{i}", [P, NKT * QCH], BF16) for i in (0, 1)]
    otsb2 = [A(f"otsb{i}", [P, QCH], BF16) for i in (0, 1)]
    otok = A("otok", [P, (QCH // P) * H * 65], BF16)
    rd = A("rd", [P, H], F32)
    dots = A("dots", [P, H], F32)
    vns = A("vns", [P, H], F32)
    cc2 = A("cc2", [P, H], F32)
    sc1 = A("sc1", [P, D], BF16)

    prev_qgate = None
    for qc in range(NQC):
        q0 = qc * QCH
        with (nc.psum_tensor(_psname(st, "pq"), [P, 2 * QCH], F32) as p0,
              nc.psum_tensor(_psname(st, "pq"), [P, 2 * QCH], F32) as p1):
            pp = [p0, p1]
            if prev_qgate is None:
                b.wait(b.pe, "ve")
            else:
                b.pe.wait_ge(b.sem["ve"], prev_qgate)
            b.wait(b.pe, "se")
            ev = {}
            for je in range(0, DTC, 2):
                k = (je // 2) % 2
                if k in ev:
                    b.pe.wait_ge(b.sem["se"], ev[k])
                for j2 in range(2):
                    for jd in range(DTC):
                        mm = b.pe.matmul(
                            pp[k][:, j2 * QCH:(j2 + 1) * QCH],
                            wall[:, jd * D + (je + j2) * P:jd * D + (je + j2 + 1) * P],
                            st.cur["xn_fm"][:, jd * S + q0:jd * S + q0 + QCH],
                            start=(jd == 0), stop=(jd == DTC - 1))
                pv = b.tick(mm, "pe")
                b.wait(b.se, "pe", pv)
                b.se.drain()
                ev[k] = b.tick(b.se.activation(
                    qch[:, je * QCH:(je + 2) * QCH], pp[k][:], AF.Identity), "se")
        with (nc.psum_tensor(_psname(st, "pss"), [P, 4 * QCH], F32) as ps0,
              nc.psum_tensor(_psname(st, "pss"), [P, 4 * QCH], F32) as ps1,
              nc.psum_tensor(_psname(st, "pot"), [65, QCH], F32) as pt0_,
              nc.psum_tensor(_psname(st, "pot"), [65, QCH], F32) as pt1_,
              nc.psum_tensor(_psname(st, "ptq"), [P, 2 * 66], BF16) as pq0,
              nc.psum_tensor(_psname(st, "ptq"), [P, 2 * 66], BF16) as pq1):
            psp = [ps0, ps1]
            potp = [pt0_, pt1_]
            pqt = [pq0, pq1]
            tr_ev = {}
            se_done = {}
            pot_ev = {}
            exps_of = {}
            b.wait(b.pe, "se")

            def emit_scores_grp(h, kg):
                qf = qch[(h % 2) * 64:(h % 2) * 64 + 64,
                         (h // 2) * QCH:(h // 2) * QCH + QCH]
                Ath = At2[h % 2]
                k = kg % 2
                if k in se_done:
                    b.pe.wait_ge(b.sem["se"], se_done[k])
                for kt4 in range(4):
                    kt = kg * 4 + kt4
                    kf = kfm[(h % 2) * 64:(h % 2) * 64 + 64,
                             (h // 2) * S + kt * P:(h // 2) * S + (kt + 1) * P]
                    mm = b.pe.matmul(psp[k][:, kt4 * QCH:(kt4 + 1) * QCH],
                                     kf, qf, start=True, stop=True)
                pv = b.tick(mm, "pe")
                b.wait(b.se, "pe", pv)
                se_done[k] = b.tick(b.se.activation(
                    Ath[:, kg * 4 * QCH:(kg + 1) * 4 * QCH], psp[k][:],
                    AF.Exp, bias=st.shift_col[:], scale=0.125), "se")
                exps_of[h] = se_done[k]

            def emit_ot(hp, pvp):
                b.wait(b.ve, "pe", pvp)
                b.ve.drain()
                vv = b.tick(b.ve.tensor_copy(otsb2[hp % 2][0:65, :],
                                             potp[hp % 2][:]), "ve")
                pot_ev[hp % 2] = vv
                b.wait(b.pe, "ve", vv)
                k = hp % 2
                if k in tr_ev:
                    b.pe.wait_ge(b.sem["ve"], tr_ev[k])
                for tt in range(QCH // P):
                    mm = b.pe.transpose(
                        pqt[k][:, tt * 66:tt * 66 + 65],
                        otsb2[hp % 2][0:65, tt * P:(tt + 1) * P],
                        st.ident_bf[0:65, 0:65])
                pv = b.tick(mm, "pe")
                b.wait(b.ve, "pe", pv)
                tr_ev[k] = b.tick(b.ve.tensor_copy(
                    otok.ap().rearrange("p (t x) -> p t x", t=QCH // P)[
                        :, :, hp * 65:(hp + 1) * 65],
                    pqt[k].ap().rearrange("p (t c) -> p t c", c=66)[:, :, 0:65]), "ve")
            NG = NKT // 4
            for kg in range(NG):
                emit_scores_grp(0, kg)
            pend = None
            for h in range(H):
                if h + 1 < H:
                    for kg in range(min(2, NG)):
                        emit_scores_grp(h + 1, kg)
                b.pe.wait_ge(b.sem["se"], exps_of[h])
                if h % 2 in pot_ev:
                    b.wait(b.pe, "ve", pot_ev[h % 2])
                Ath = At2[h % 2]
                for kt in range(NKT):
                    va = vtok.ap().rearrange(
                        "p (t h e) -> p t h e", t=NTB, h=H)[:, kt, h, :]
                    mm = b.pe.matmul(potp[h % 2][:], va,
                                     Ath[:, kt * QCH:(kt + 1) * QCH],
                                     start=(kt == 0), stop=(kt == NKT - 1))
                    if kt == 7 and h + 1 < H and NG > 2:
                        emit_scores_grp(h + 1, 2)
                pv = b.tick(mm, "pe")
                if h + 1 < H and NG > 3:
                    emit_scores_grp(h + 1, 3)
                if pend is not None:
                    emit_ot(*pend)
                pend = (h, pv)
            emit_ot(*pend)
            prev_qgate = b.cnt["ve"]
            for tt in range(QCH // P):
                gt = q0 // P + tt
                po3 = otok.ap().rearrange("p (t h e) -> p t h e", t=QCH // P,
                                          h=H)[:, tt]
                b.ve.drain()
                if "atw" in b.cnt:
                    b.wait(b.ve, "atw")
                b.ve.reciprocal(rd[:], po3[:, :, 64:65])
                b.ve.drain()
                v3 = vtok.ap().rearrange("p (t h e) -> p t h e", t=NTB, h=H)[
                    :, gt, :, 0:64]
                b.ve.tensor_tensor(sc1.ap().rearrange("p (h e) -> p h e", h=H),
                                   po3[:, :, 0:64], v3, ALU.mult)
                b.ve.drain()
                b.ve.reduce_sum(dots[:], sc1.ap().rearrange("p (h e) -> p h e", h=H),
                                axis=AX.X)
                b.ve.drain()
                b.ve.tensor_tensor(sc1.ap().rearrange("p (h e) -> p h e", h=H),
                                   v3, v3, ALU.mult)
                b.ve.drain()
                b.ve.reduce_sum(vns[:], sc1.ap().rearrange("p (h e) -> p h e", h=H),
                                axis=AX.X)
                b.ve.drain()
                b.ve.tensor_scalar_add(vns[:], vns[:], XSA_EPS)
                b.ve.drain()
                b.ve.reciprocal(vns[:], vns[:])
                b.ve.drain()
                b.ve.tensor_mul(cc2[:], dots[:], vns[:])
                b.ve.drain()
                b.ve.tensor_scalar(cc2[:], cc2[:], st.xsa_col[:], None, ALU.mult)
                b.ve.drain()
                b.ve.tensor_tensor(sc1.ap().rearrange("p (h e) -> p h e", h=H),
                                   cc2.ap().unsqueeze(2).broadcast_to([P, H, 64]),
                                   v3, ALU.mult)
                b.ve.drain()
                b.ve.tensor_tensor(sc1.ap().rearrange("p (h e) -> p h e", h=H),
                                   po3[:, :, 0:64],
                                   sc1.ap().rearrange("p (h e) -> p h e", h=H), ALU.add)
                b.ve.drain()
                vv = b.tick(b.ve.tensor_tensor(
                    sc1.ap().rearrange("p (h e) -> p h e", h=H),
                    sc1.ap().rearrange("p (h e) -> p h e", h=H),
                    rd.ap().unsqueeze(2).broadcast_to([P, H, 64]), ALU.mult), "ve")
                b.sq.wait_ge(b.sem["ve"], vv)
                b.store("atw", st.s_atok.ap()[bi, gt * P:(gt + 1) * P, :], sc1[:])


